# version 14 (final)
# baseline (speedup 1.0000x reference)
import numpy as np
import ml_dtypes
import concourse.bass as bass
import concourse.mybir as mybir
from concourse.bass_utils import run_bass_kernel_spmd

F32 = mybir.dt.float32
BF16 = mybir.dt.bfloat16
AF = mybir.ActivationFunctionType
ALU = mybir.AluOpType
nbf = ml_dtypes.bfloat16
PI = float(np.pi)



P1_IN = dict(
    xin=([1024, 2304], BF16), gbc=([128, 16], F32),
    wqkvT=([1024, 3072], BF16), maskT=([384, 384 * 3], BF16),
    wqkvb=([2048, 1024], BF16), woutb=([1024, 1024], BF16),
    b1c=([128, 32], F32), b2c=([128, 8], F32),
    w1b0=([4096, 1024], BF16), w2b0=([1024, 4096], BF16),
    inpb=([3072, 1024], BF16), inpbc=([128, 24], F32),
    frcol=([7, 2], F32), fbias=([7, 1], F32),
    mw1p=([7, 128], BF16), mb1c=([64, 2], F32), mw2s=([64, 128], BF16), mb2c=([64, 2], F32),
    mw3s=([64, 1024], BF16), mb3c=([128, 8], F32), decc=([128, 8], F32),
    identd=([128, 128], BF16), titc=([128, 4096], F32),
)


def emit_p1(nc, D, Q, C, SEM, mid_cb=None, h_cb=None):
    stop = 99
    ctx = []
    def ent(t):
        ctx.append(t); return t.__enter__()
    sb = lambda n, s, dt: ent(nc.sbuf_tensor(n, s, dt))
    pst = lambda n, s: ent(nc.psum_tensor(n, s, F32))

    xf = sb("xf", [128, 8 * 2304], BF16)
    xn = sb("xn", [128, 8 * 2304], BF16)
    U1 = sb("U1", [128, 18432], BF16)
    U2 = sb("U2", [128, 16384], BF16)
    scs = sb("scs", [128, 6144], F32)
    masks = sb("masks", [128, 3 * 384], BF16)
    pts = sb("pts", [128, 2 * 384], BF16)
    ptT = sb("ptT", [128, 2 * 384], BF16)
    aos = sb("aos", [128, 16 * 64], BF16)
    qb_ = sb("qb_", [128, 2048], BF16)
    kb_ = sb("kb_", [128, 2304], BF16)
    wwin = sb("wwin", [128, 2048], BF16)
    sqb = sb("sqb", [128, 512], BF16)
    vtmp = sqb
    edgesb = sb("edgesb", [128, 32], BF16)
    rowf = sb("rowf", [1, 4608], F32)
    cols = sb("cols", [128, 48], F32)
    onesc = sb("onesc", [128, 2], BF16)
    onesr = sb("onesr", [1, 128], BF16)
    onesrF = sb("onesrF", [1, 128], F32)
    gbc = sb("gbcS", [128, 16], F32)
    b1c = sb("b1cS", [128, 32], F32)
    b2c = sb("b2cS", [128, 8], F32)
    inpbc = sb("inpbcS", [128, 24], F32)
    identt = sb("identt", [128, 128], BF16)
    frcol = sb("frcolS", [7, 2], F32)
    fbias = sb("fbiasS", [7, 1], F32)
    mw1p = sb("mw1pS", [7, 128], BF16)
    mw2s = sb("mw2sS", [64, 128], BF16)
    mb1c = sb("mb1cS", [64, 2], F32)
    mb2c = sb("mb2cS", [64, 2], F32)
    mb3c = sb("mb3cS", [128, 8], F32)
    decc = sb("deccS", [128, 8], F32)
    wsc = sb("wsc", [128, 2], F32)
    tit = sb("tit", [128, 512], F32)

    vT = U1
    mw3s = U1[0:64, 16384:17408]
    hh = U1
    aoF = U2
    w2win = U2
    enc = U2[0:7, 0:4096]
    fh1 = U2[0:64, 4096:8192]
    fh2 = U2[0:64, 8192:12288]
    hob = U2[0:128, 12288:16384]

    psA = pst("psA", [128, 2048])
    psB = pst("psB", [128, 2048])

    (s_ld, s_w, s_pe, s_ev, s_ac, s_po, s_out) = SEM
    def qq(eng, fn):
        Q[eng].append(fn)

    def load(dst, src):
        C.ld += 16
        qq("sync", lambda e, d=dst, s=src: e.dma_start(out=d, in_=s).then_inc(s_ld, 16))

    def loadw(dst, src, wait_pe=None):
        C.w += 16
        def f(e, d=dst, s=src, wp=wait_pe):
            if wp is not None:
                e.wait_ge(s_pe, wp)
            e.dma_start(out=d, in_=s).then_inc(s_w, 16)
        qq("gpsimd", f)

    def phase_barrier():
        qq("tensor", lambda e, v=C.ev: e.wait_ge(s_ev, v))
        qq("tensor", lambda e, v=C.ac: e.wait_ge(s_ac, v))
        qq("tensor", lambda e, v=C.ld: e.wait_ge(s_ld, v))
        qq("scalar", lambda e, v=C.ev: e.wait_ge(s_ev, v))
        qq("scalar", lambda e, v=C.ld: e.wait_ge(s_ld, v))
        qq("vector", lambda e, v=C.ac: e.wait_ge(s_ac, v))
        qq("vector", lambda e, v=C.ld: e.wait_ge(s_ld, v))

    wqkv_r = D["wqkvT"].rearrange("(dt p) e -> p dt e", dt=8, p=128)
    wqkvb_r = D["wqkvb"].rearrange("(et p) n -> p et n", et=16, p=128)
    woutb_r = D["woutb"].rearrange("(et p) n -> p et n", et=8, p=128)
    w1b0_r = D["w1b0"].rearrange("(ht p) n -> p ht n", ht=32, p=128)
    w2b0_r = D["w2b0"].rearrange("(et p) n -> p et n", et=8, p=128)
    inpb_r = D["inpb"].rearrange("(et p) n -> p et n", et=24, p=128)
    wwin_last = [0, 0]
    w2win_last = [0, 0]

    for dt in range(8):
        load(xf[:, dt * 2304:(dt + 1) * 2304], D["xin"][dt * 128:(dt + 1) * 128, :])
    load(masks[:, :], D["maskT"].rearrange("(m r) w -> r m w", m=3, r=128))
    for nm, t in [("gbc", gbc), ("b1c", b1c), ("b2c", b2c), ("inpbc", inpbc), ("identd", identt),
                  ("frcol", frcol), ("fbias", fbias), ("mw1p", mw1p), ("mw2s", mw2s),
                  ("mb1c", mb1c), ("mb2c", mb2c), ("mb3c", mb3c), ("decc", decc)]:
        load(t[:, :], D[nm][:, :])
    ld0 = C.ld

    C.ev += 1
    def init_ones(e):
        e.memset(onesc[:, :], 1.0)
        e.memset(onesr[:, :], 1.0)
        e.memset(onesrF[:, :], 1.0).then_inc(s_ev, 1)
    qq("vector", init_ones)

    def emit_ln(c0, n, mode):
        nch = (n + 511) // 512
        bank = lambda ch: (psA, ch * 512) if ch < 4 else (psB, 0)
        phase_barrier()
        def mm_sum(e, etgt=C.ev):
            e.wait_ge(s_ev, etgt)
            for ch in range(nch):
                cw = min(512, n - ch * 512)
                ps, col = bank(ch)
                for t in range(8):
                    m = e.matmul(ps[0:1, col:col + cw], onesc[:, 0:1],
                                 xf[:, t * 2304 + c0 + ch * 512:t * 2304 + c0 + ch * 512 + cw],
                                 start=(t == 0), stop=(t == 7))
            m.then_inc(s_pe, 1)
        qq("tensor", mm_sum)
        C.pe += 1
        C.ev += 1
        def ev_sum(e, tgt=C.pe):
            e.wait_ge(s_pe, tgt)
            w4 = min(2048, n)
            e.tensor_scalar_mul(rowf[0:1, 0:w4], psA[0:1, 0:w4], -1.0 / 1024.0)
            if n > 2048:
                e.tensor_scalar_mul(rowf[0:1, 2048:n], psB[0:1, 0:n - 2048], -1.0 / 1024.0)
            e.tensor_scalar_add(rowf[0:1, 0:1], rowf[0:1, 0:1], 0.0).then_inc(s_ev, 1)
        qq("vector", ev_sum)
        sums_ev = C.ev
        for ch in range(nch):
            cw = min(512, n - ch * 512)
            for t in range(8):
                C.ac += 1
                def sq(e, ch=ch, cw=cw, t=t, ptgt=C.pe):
                    if not (ch == 0 and t == 0):
                        e.wait_ge(s_pe, ptgt)
                    e.activation(out=sqb[:, 0:cw],
                                 in_=xf[:, t * 2304 + c0 + ch * 512:t * 2304 + c0 + ch * 512 + cw],
                                 func=AF.Square, bias=0.0, scale=1.0).then_inc(s_ac, 1)
                qq("scalar", sq)
                ps, col = bank(ch)
                def mm_sq(e, ch=ch, cw=cw, t=t, ps=ps, col=col, atgt=C.ac, etgt=sums_ev):
                    e.wait_ge(s_ac, atgt)
                    if ch == 0 and t == 0:
                        e.wait_ge(s_ev, etgt)
                    m = e.matmul(ps[0:1, col:col + cw], onesc[:, 0:1], sqb[:, 0:cw],
                                 start=(t == 0), stop=(t == 7))
                    m.then_inc(s_pe, 1)
                qq("tensor", mm_sq)
                C.pe += 1
        C.ev += 1
        def rowm(e, tgt=C.pe):
            e.wait_ge(s_pe, tgt)
            w4 = min(2048, n)
            e.tensor_tensor(rowf[0:1, 2304:2304 + n], rowf[0:1, 0:n], rowf[0:1, 0:n], op=ALU.mult)
            e.scalar_tensor_tensor(out=rowf[0:1, 2304:2304 + w4], in0=psA[0:1, 0:w4], scalar=1.0 / 1024.0,
                                   in1=rowf[0:1, 2304:2304 + w4], op0=ALU.mult, op1=ALU.subtract)
            if n > 2048:
                e.scalar_tensor_tensor(out=rowf[0:1, 2304 + 2048:2304 + n], in0=psB[0:1, 0:n - 2048],
                                       scalar=1.0 / 1024.0, in1=rowf[0:1, 2304 + 2048:2304 + n],
                                       op0=ALU.mult, op1=ALU.subtract)
            e.tensor_scalar_add(rowf[0:1, 2304:2304 + n], rowf[0:1, 2304:2304 + n], 1e-5).then_inc(s_ev, 1)
        qq("vector", rowm)
        C.ac += 1
        def rows_sqrt(e, tgt=C.ev):
            e.wait_ge(s_ev, tgt)
            e.activation(out=rowf[0:1, 2304:2304 + n], in_=rowf[0:1, 2304:2304 + n], func=AF.Sqrt, bias=0.0, scale=1.0).then_inc(s_ac, 1)
        qq("scalar", rows_sqrt)
        C.ev += 1
        def rows_rec(e, tgt=C.ac):
            e.wait_ge(s_ac, tgt)
            e.reciprocal(rowf[0:1, 2304:2304 + n], rowf[0:1, 2304:2304 + n]).then_inc(s_ev, 1)
        qq("vector", rows_rec)
        rr_ev = C.ev
        for ch in range(nch):
            cw = min(512, n - ch * 512)
            pcol = (ch % 2) * 512
            def mm_bc(e, ch=ch, cw=cw, pcol=pcol, tgt=rr_ev, etgt=C.ev):
                e.wait_ge(s_ev, tgt)
                if ch >= 2:
                    e.wait_ge(s_ev, etgt)
                e.matmul(psA[:, pcol:pcol + cw], onesrF[0:1, :], rowf[0:1, ch * 512:ch * 512 + cw],
                         start=True, stop=True)
                e.matmul(psB[:, pcol:pcol + cw], onesrF[0:1, :], rowf[0:1, 2304 + ch * 512:2304 + ch * 512 + cw],
                         start=True, stop=True).then_inc(s_pe, 1)
            qq("tensor", mm_bc)
            C.pe += 1
            for t in range(8):
                C.ev += 1
                def apply(e, t=t, ch=ch, cw=cw, pcol=pcol, tgt=C.pe, mode=mode):
                    e.wait_ge(s_pe, tgt)
                    sl = slice(t * 2304 + c0 + ch * 512, t * 2304 + c0 + ch * 512 + cw)
                    dst = xn if mode == "ln1" else xf
                    e.tensor_tensor(dst[:, sl], xf[:, sl], psA[:, pcol:pcol + cw], op=ALU.add)
                    e.tensor_tensor(dst[:, sl], dst[:, sl], psB[:, pcol:pcol + cw], op=ALU.mult)
                    if mode == "ln1":
                        e.tensor_scalar(dst[:, sl], dst[:, sl], gbc[:, 2 * t:2 * t + 1], gbc[:, 2 * t + 1:2 * t + 2],
                                        op0=ALU.mult, op1=ALU.add).then_inc(s_ev, 1)
                    else:
                        e.tensor_scalar_add(dst[:, sl], dst[:, sl], 0.0).then_inc(s_ev, 1)
                qq("vector", apply)
        phase_barrier()

    qq("tensor", lambda e, v=ld0: e.wait_ge(s_ld, v))
    emit_ln(0, 2304, "ln1")

    loadw(w2win[:, 0:4096], wqkv_r[:, :, 2048:2560])
    loadw(w2win[:, 4096:8192], wqkv_r[:, :, 2560:3072])
    vw0 = C.w
    evb = C.ev
    for half in range(2):
        for tt in range(18):
            ps = psA if tt % 2 == 0 else psB
            it = half * 18 + tt
            def mm_v(e, tt=tt, ps=ps, half=half, it=it, evb=evb, wtgt=vw0):
                if it == 0:
                    e.wait_ge(s_w, wtgt)
                if it >= 2:
                    e.wait_ge(s_ev, evb + it - 1)
                for dt in range(8):
                    m = e.matmul(ps[:, 0:512], xn[:, dt * 2304 + tt * 128:dt * 2304 + tt * 128 + 128],
                                 w2win[:, half * 4096 + dt * 512:half * 4096 + dt * 512 + 512],
                                 start=(dt == 0), stop=(dt == 7))
                m.then_inc(s_pe, 1)
            qq("tensor", mm_v)
            C.pe += 1
            C.ev += 1
            def ev_v(e, tt=tt, ps=ps, half=half, tgt=C.pe):
                e.wait_ge(s_pe, tgt)
                e.tensor_copy(vT[:, tt * 1024 + half * 512:tt * 1024 + half * 512 + 512], ps[:, 0:512]).then_inc(s_ev, 1)
            qq("vector", ev_v)

    phase_barrier()
    prev_exp_ac = 0
    prev_av_end = 0
    first_av = True
    for g in range(8):
        loadw(wwin[:, 0:1024], wqkvb_r[:, g, :], wait_pe=wwin_last[0])
        loadw(wwin[:, 1024:2048], wqkvb_r[:, 8 + g, :], wait_pe=wwin_last[1])
        wq = C.w
        evb = C.ev
        it = 0
        for (dst, off, tot, wofs, scl) in ((qb_, 256, 2048, 0, 0.125), (kb_, 0, 2304, 1024, 1.0)):
            nch = (tot + 511) // 512
            for ch in range(nch):
                cw = min(512, tot - ch * 512)
                ps = psA if it % 2 == 0 else psB
                def mm_qk(e, ps=ps, ch=ch, cw=cw, off=off, wofs=wofs, it=it, wq=wq, evb=evb):
                    if it == 0:
                        e.wait_ge(s_w, wq)
                    if it >= 2:
                        e.wait_ge(s_ev, evb + it - 1)
                    for dt in range(8):
                        m = e.matmul(ps[:, 0:cw], wwin[:, wofs + dt * 128:wofs + dt * 128 + 128],
                                     xn[:, dt * 2304 + off + ch * 512:dt * 2304 + off + ch * 512 + cw],
                                     start=(dt == 0), stop=(dt == 7))
                    m.then_inc(s_pe, 1)
                qq("tensor", mm_qk)
                C.pe += 1
                C.ev += 1
                def ev_qk(e, ps=ps, ch=ch, cw=cw, dst=dst, scl=scl, tgt=C.pe):
                    e.wait_ge(s_pe, tgt)
                    e.activation(out=dst[:, ch * 512:ch * 512 + cw], in_=ps[:, 0:cw],
                                 func=AF.Copy, bias=0.0, scale=scl).then_inc(s_ev, 1)
                qq("scalar", ev_qk)
                it += 1
            wwin_last[0 if wofs == 0 else 1] = C.pe
        qk_ev = C.ev

        for hh_ in range(2):
            h = 2 * g + hh_
            rb = hh_ * 64
            sc_evb = C.ev
            for qb in range(16):
                slot = qb % 4
                def mm_sc(e, qb=qb, slot=slot, rb=rb, sc_evb=sc_evb, qkev=qk_ev, pexp=prev_exp_ac, pav=prev_av_end):
                    if qb == 0:
                        e.wait_ge(s_ev, qkev)
                        e.wait_ge(s_ac, pexp)
                        e.wait_ge(s_ev, pav)
                    if qb >= 4:
                        e.wait_ge(s_ev, sc_evb + qb - 3)
                    e.matmul(psA[:, slot * 512:slot * 512 + 384],
                             qb_[rb:rb + 64, qb * 128:qb * 128 + 128],
                             kb_[rb:rb + 64, qb * 128:qb * 128 + 384],
                             start=True, stop=True).then_inc(s_pe, 1)
                qq("tensor", mm_sc)
                C.pe += 1
                C.ev += 1
                def ev_sc(e, qb=qb, slot=slot, tgt=C.pe):
                    e.wait_ge(s_pe, tgt)
                    mi = min(qb, 2)
                    e.scalar_tensor_tensor(out=scs[:, qb * 384:qb * 384 + 384],
                                           in0=psA[:, slot * 512:slot * 512 + 384],
                                           scalar=1.0, in1=masks[:, mi * 384:mi * 384 + 384],
                                           op0=ALU.mult, op1=ALU.add).then_inc(s_ev, 1)
                qq("vector", ev_sc)
            C.ev += 1
            def rmax(e, tgt=C.ev - 1):
                e.wait_ge(s_ev, tgt)
                e.tensor_reduce(cols[:, 0:16], scs.rearrange("p (qb w) -> p qb w", qb=16, w=384),
                                axis=mybir.AxisListType.X, op=ALU.max, negate=True).then_inc(s_ev, 1)
            qq("vector", rmax)
            rmax_ev = C.ev
            tp_pe = [None] * 16
            av_pe = [None] * 16
            exp_ac_l = [None] * 16
            tp_ev_l = [None] * 16
            av_ev_l = [None] * 16
            for qb in range(16):
                C.ac += 1
                exp_ac_l[qb] = C.ac
                def expq(e, qb=qb, tgt=rmax_ev, tppe=tp_pe[qb - 2] if qb >= 2 else None):
                    if qb == 0:
                        e.wait_ge(s_ev, tgt)
                    if qb >= 2:
                        e.wait_ge(s_pe, tppe)
                    e.activation(out=pts[:, (qb % 2) * 384:(qb % 2) * 384 + 384], in_=scs[:, qb * 384:qb * 384 + 384],
                                 func=AF.Exp, bias=cols[:, qb:qb + 1], scale=1.0,
                                 accum_out=cols[:, 16 + qb:17 + qb]).then_inc(s_ac, 1)
                qq("scalar", expq)
                def mm_tp(e, qb=qb, atgt=exp_ac_l[qb], evtgt=tp_ev_l[qb - 2] if qb >= 2 else None):
                    e.wait_ge(s_ac, atgt)
                    if qb >= 2:
                        e.wait_ge(s_ev, evtgt)
                    for kk in range(3):
                        m = e.matmul(psB[:, (qb % 2) * 512 + kk * 128:(qb % 2) * 512 + kk * 128 + 128],
                                     pts[:, (qb % 2) * 384 + kk * 128:(qb % 2) * 384 + kk * 128 + 128],
                                     identt[:, :], start=True, stop=True)
                    m.then_inc(s_pe, 1)
                qq("tensor", mm_tp)
                C.pe += 1
                tp_pe[qb] = C.pe
                C.ev += 1
                tp_ev_l[qb] = C.ev
                def ev_tp(e, qb=qb, tgt=C.pe, avpe=av_pe[qb - 2] if qb >= 2 else None):
                    e.wait_ge(s_pe, tgt)
                    if qb >= 2:
                        e.wait_ge(s_pe, avpe)
                    e.reciprocal(cols[:, 32 + qb:33 + qb], cols[:, 16 + qb:17 + qb])
                    e.tensor_copy(ptT[:, (qb % 2) * 384:(qb % 2) * 384 + 384],
                                  psB[:, (qb % 2) * 512:(qb % 2) * 512 + 384]).then_inc(s_ev, 1)
                qq("vector", ev_tp)
                def mm_av(e, qb=qb, h=h, tptgt=tp_ev_l[qb], avtgt=av_ev_l[qb - 2] if qb >= 2 else None,
                          first=(first_av and qb == 0), vtev=None):
                    e.wait_ge(s_ev, tptgt)
                    if qb >= 2:
                        e.wait_ge(s_ev, avtgt)
                    for kk in range(3):
                        ktile = qb + kk
                        m = e.matmul(psA[:, 1024 + (qb % 2) * 512:1024 + (qb % 2) * 512 + 64],
                                     ptT[:, (qb % 2) * 384 + kk * 128:(qb % 2) * 384 + kk * 128 + 128],
                                     vT[:, ktile * 1024 + h * 64:ktile * 1024 + h * 64 + 64],
                                     start=(kk == 0), stop=(kk == 2))
                    m.then_inc(s_pe, 1)
                qq("tensor", mm_av)
                C.pe += 1
                av_pe[qb] = C.pe
                C.ev += 1
                av_ev_l[qb] = C.ev
                def ev_av(e, qb=qb, tgt=C.pe, rtgt=None):
                    e.wait_ge(s_pe, tgt)
                    e.tensor_scalar(aos[:, qb * 64:qb * 64 + 64], psA[:, 1024 + (qb % 2) * 512:1024 + (qb % 2) * 512 + 64],
                                    cols[:, 32 + qb:33 + qb], None, op0=ALU.mult).then_inc(s_ev, 1)
                qq("vector", ev_av)
            first_av = False
            prev_exp_ac = exp_ac_l[15]
            prev_av_end = av_ev_l[15]
            at_acb = C.ac
            for qb in range(16):
                def mm_at(e, qb=qb, avtgt=av_ev_l[qb], atb=at_acb):
                    e.wait_ge(s_ev, avtgt)
                    if qb >= 2:
                        e.wait_ge(s_ac, atb + qb - 1)
                    e.matmul(psB[0:64, 1024 + (qb % 2) * 512:1024 + (qb % 2) * 512 + 128],
                             aos[:, qb * 64:qb * 64 + 64], identt[:, :], start=True, stop=True).then_inc(s_pe, 1)
                qq("tensor", mm_at)
                C.pe += 1
                C.ac += 1
                def ev_at(e, qb=qb, g=g, rb=rb, tgt=C.pe):
                    e.wait_ge(s_pe, tgt)
                    e.activation(out=aoF[rb:rb + 64, g * 2048 + qb * 128:g * 2048 + qb * 128 + 128],
                                 in_=psB[0:64, 1024 + (qb % 2) * 512:1024 + (qb % 2) * 512 + 128],
                                 func=AF.Copy, bias=0.0, scale=1.0).then_inc(s_ac, 1)
                qq("scalar", ev_at)
    aoF_ac = C.ac

    phase_barrier()
    for et in range(8):
        slot = et % 2
        loadw(wwin[:, slot * 1024:slot * 1024 + 1024], woutb_r[:, et, :], wait_pe=wwin_last[slot])
        wq = C.w
        for ch in range(4):
            ps = psA if (4 * et + ch) % 2 == 0 else psB
            def mm_wo(e, et=et, ch=ch, ps=ps, slot=slot, wq=wq, atgt=aoF_ac, etgt=C.ev):
                e.wait_ge(s_w, wq)
                if et == 0 and ch == 0:
                    e.wait_ge(s_ac, atgt)
                if ch >= 2 or et > 0:
                    e.wait_ge(s_ev, etgt - 1)
                for dt in range(8):
                    m = e.matmul(ps[:, 0:512], wwin[:, slot * 1024 + dt * 128:slot * 1024 + dt * 128 + 128],
                                 aoF[:, dt * 2048 + ch * 512:dt * 2048 + ch * 512 + 512],
                                 start=(dt == 0), stop=(dt == 7))
                m.then_inc(s_pe, 1)
            qq("tensor", mm_wo)
            C.pe += 1
            C.ev += 1
            def ev_wo(e, et=et, ch=ch, ps=ps, tgt=C.pe):
                e.wait_ge(s_pe, tgt)
                sl = slice(et * 2304 + 256 + ch * 512, et * 2304 + 256 + ch * 512 + 512)
                e.scalar_tensor_tensor(out=xf[:, sl], in0=xf[:, sl], scalar=2.0, in1=ps[:, 0:512],
                                       op0=ALU.mult, op1=ALU.add).then_inc(s_ev, 1)
            qq("vector", ev_wo)
        wwin_last[slot] = C.pe
    w2win_last[0] = w2win_last[1] = C.pe
    emit_ln(256, 2048, "post")
    C.ld += 16
    def ld_tit(e, etgt=C.ev):
        e.wait_ge(s_ev, etgt)
        e.dma_start(out=scs[:, 0:4096], in_=D["titc"][:, :]).then_inc(s_ld, 16)
    qq("sync", ld_tit)
    load(mw3s[:, :], D["mw3s"][:, :])
    filt_ld = C.ld

    for tc in range(4):
        phase_barrier()
        gel_acb = C.ac
        for ht in range(32):
            loadw(wwin[:, (ht % 2) * 1024:(ht % 2) * 1024 + 1024], w1b0_r[:, ht, :],
                  wait_pe=wwin_last[ht % 2])
            wq = C.w
            ps = psA if ht % 2 == 0 else psB
            def mm_w1(e, tc=tc, ht=ht, ps=ps, wq=wq, acb=gel_acb):
                e.wait_ge(s_w, wq)
                if ht >= 2:
                    e.wait_ge(s_ac, acb + ht - 1)
                for dt in range(8):
                    m = e.matmul(ps[:, 0:512],
                                 wwin[:, (ht % 2) * 1024 + dt * 128:(ht % 2) * 1024 + dt * 128 + 128],
                                 xf[:, dt * 2304 + 256 + tc * 512:dt * 2304 + 256 + tc * 512 + 512],
                                 start=(dt == 0), stop=(dt == 7))
                m.then_inc(s_pe, 1)
            qq("tensor", mm_w1)
            C.pe += 1
            wwin_last[ht % 2] = C.pe
            C.ac += 1
            def ev_w1(e, ht=ht, ps=ps, tgt=C.pe):
                e.wait_ge(s_pe, tgt)
                e.activation(out=hh[:, ht * 512:ht * 512 + 512], in_=ps[:, 0:512],
                             func=AF.Gelu, bias=b1c[:, ht:ht + 1], scale=1.0).then_inc(s_ac, 1)
            qq("scalar", ev_w1)
        w1_ac = C.ac
        for et in range(8):
            loadw(w2win[:, (et % 2) * 4096:(et % 2) * 4096 + 4096], w2b0_r[:, et, :],
                  wait_pe=w2win_last[et % 2])
            wq = C.w
            ps = psA if et % 2 == 0 else psB
            def mm_w2(e, et=et, ps=ps, wq=wq, atgt=w1_ac, etgt=C.ev):
                e.wait_ge(s_w, wq)
                if et == 0:
                    e.wait_ge(s_ac, atgt)
                if et >= 2:
                    e.wait_ge(s_ev, etgt - 1)
                for ht in range(32):
                    m = e.matmul(ps[:, 1024:1536], w2win[:, (et % 2) * 4096 + ht * 128:(et % 2) * 4096 + ht * 128 + 128],
                                 hh[:, ht * 512:ht * 512 + 512], start=(ht == 0), stop=(ht == 31))
                m.then_inc(s_pe, 1)
            qq("tensor", mm_w2)
            C.pe += 1
            w2win_last[et % 2] = C.pe
            C.ev += 1
            def ev_w2(e, et=et, tc=tc, ps=ps, tgt=C.pe):
                e.wait_ge(s_pe, tgt)
                sl = slice(et * 2304 + 256 + tc * 512, et * 2304 + 256 + tc * 512 + 512)
                e.scalar_tensor_tensor(out=xf[:, sl], in0=ps[:, 1024:1536], scalar=b2c[:, et:et + 1], in1=xf[:, sl],
                                       op0=ALU.add, op1=ALU.add).then_inc(s_ev, 1)
            qq("vector", ev_w2)
    emit_ln(256, 2048, "post")
    x2_ev = C.ev

    for dt in range(8):
        C.out += 16
        def dx2(e, dt=dt, etgt=x2_ev):
            e.wait_ge(s_ev, etgt)
            e.dma_start(out=D["x2own"][dt * 128:(dt + 1) * 128, :],
                        in_=xf[:, dt * 2304 + 256:dt * 2304 + 2304]).then_inc(s_out, 16)
        qq("gpsimd", dx2)

    phase_barrier()
    ip_po0 = C.po
    for et in range(24):
        loadw(wwin[:, (et % 2) * 1024:(et % 2) * 1024 + 1024], inpb_r[:, et, :],
              wait_pe=wwin_last[et % 2])
        wq = C.w
        name = "v0own" if et < 8 else ("c0own" if et < 16 else "c1own")
        base = et % 8
        for ch in range(4):
            ps = psA if (4 * et + ch) % 2 == 0 else psB
            half = ch % 2
            def mm_ip(e, et=et, ch=ch, ps=ps, wq=wq, etgt=C.ev, first=(et == 0 and ch == 0)):
                e.wait_ge(s_w, wq)
                if not first and etgt >= 2:
                    e.wait_ge(s_ev, etgt - 1)
                for dt in range(8):
                    m = e.matmul(ps[:, 0:512], wwin[:, (et % 2) * 1024 + dt * 128:(et % 2) * 1024 + dt * 128 + 128],
                                 xf[:, dt * 2304 + 256 + ch * 512:dt * 2304 + 256 + ch * 512 + 512],
                                 start=(dt == 0), stop=(dt == 7))
                m.then_inc(s_pe, 1)
            qq("tensor", mm_ip)
            C.pe += 1
            C.ev += 1
            def ev_ip(e, et=et, ch=ch, ps=ps, tgt=C.pe, pneed=ip_po0 + (2 * et + ch // 2 - 1) * 16, pskip=(et == 0)):
                e.wait_ge(s_pe, tgt)
                if not pskip:
                    e.wait_ge(s_po, pneed)
                vt = qb_[:, ch * 512:ch * 512 + 512]
                e.tensor_scalar(vt[:, 0:512], ps[:, 0:512], inpbc[:, et:et + 1], None,
                                op0=ALU.add)
                if et >= 8 and ch == 0:
                    e.tensor_copy(edgesb[:, (et % 8) * 4 + (et - 8) // 8 * 2:(et % 8) * 4 + (et - 8) // 8 * 2 + 1],
                                  vt[:, 0:1])
                if et >= 8 and ch == 3:
                    e.tensor_copy(edgesb[:, (et % 8) * 4 + (et - 8) // 8 * 2 + 1:(et % 8) * 4 + (et - 8) // 8 * 2 + 2],
                                  vt[:, 511:512])
                e.tensor_scalar_add(edgesb[:, 31:32], edgesb[:, 31:32], 0.0).then_inc(s_ev, 1)
            qq("vector", ev_ip)
            if ch % 2 == 1:
                C.po += 16
                def d_ip(e, name=name, base=base, ch=ch, etgt=C.ev):
                    e.wait_ge(s_ev, etgt)
                    e.dma_start(out=D[name][base * 128:(base + 1) * 128, (ch - 1) * 512:(ch + 1) * 512],
                                in_=qb_[:, (ch - 1) * 512:(ch + 1) * 512]).then_inc(s_po, 16)
                qq("gpsimd", d_ip)
        wwin_last[et % 2] = C.pe

    C.out += 16
    def d_edge(e, etgt=C.ev, otgt=C.out):
        e.wait_ge(s_ev, etgt)
        e.dma_start(out=D["edges"].rearrange("(dt p) k -> p dt k", dt=8, p=128),
                    in_=edgesb.rearrange("p (dt k) -> p dt k", dt=8, k=4)).then_inc(s_out, 16)
    qq("gpsimd", d_edge)
    if mid_cb is not None:
        mid_cb()

    phase_barrier()
    qq("tensor", lambda e, v=filt_ld: e.wait_ge(s_ld, v))
    dh_ac = [0, 0]
    for i in range(2):
        for ch in range(8):
            C.ac += 2
            def ev_enc(e, i=i, ch=ch, ltgt=filt_ld):
                if i == 0 and ch == 0:
                    e.wait_ge(s_ld, ltgt)
                e.activation(out=enc[0:7, ch * 512:ch * 512 + 512], in_=scs[0:7, ch * 512:ch * 512 + 512],
                             func=AF.Sin, bias=fbias[0:7, 0:1], scale=frcol[0:7, i:i + 1]).then_inc(s_ac, 1)
                e.activation(out=enc[0:1, ch * 512:ch * 512 + 512], in_=scs[0:1, ch * 512:ch * 512 + 512],
                             func=AF.Copy, bias=0.0, scale=1.0 / 4095.0).then_inc(s_ac, 1)
            qq("scalar", ev_enc)
        enc_ac = C.ac
        f1_acb = C.ac
        for ch in range(8):
            slot = ch % 2
            def mm_f1(e, i=i, ch=ch, slot=slot, tgt=enc_ac, acb=f1_acb):
                if ch == 0:
                    e.wait_ge(s_ac, tgt)
                e.wait_ge(s_ac, acb + ch - 1 if ch >= 2 else acb)
                e.matmul(psA[0:64, slot * 512:slot * 512 + 512], mw1p[0:7, i * 64:i * 64 + 64],
                         enc[0:7, ch * 512:ch * 512 + 512], start=True, stop=True).then_inc(s_pe, 1)
            qq("tensor", mm_f1)
            C.pe += 1
            C.ac += 1
            def ev_f1(e, i=i, ch=ch, slot=slot, tgt=C.pe, dtgt=dh_ac[1]):
                e.wait_ge(s_pe, tgt)
                if i == 1 and ch == 0 and dtgt:
                    e.wait_ge(s_ac, dtgt)
                e.activation(out=fh1[:, ch * 512:ch * 512 + 512], in_=psA[0:64, slot * 512:slot * 512 + 512],
                             func=AF.Gelu, bias=mb1c[0:64, i:i + 1], scale=1.0).then_inc(s_ac, 1)
            qq("scalar", ev_f1)
        f1_ac = C.ac
        f2_acb = C.ac
        for ch in range(8):
            slot = ch % 2
            def mm_f2(e, i=i, ch=ch, slot=slot, tgt=f1_ac, acb=f2_acb):
                if ch == 0:
                    e.wait_ge(s_ac, tgt)
                e.wait_ge(s_ac, acb + ch - 1 if ch >= 2 else acb)
                e.matmul(psB[0:64, slot * 512:slot * 512 + 512], mw2s[0:64, i * 64:i * 64 + 64],
                         fh1[:, ch * 512:ch * 512 + 512], start=True, stop=True).then_inc(s_pe, 1)
            qq("tensor", mm_f2)
            C.pe += 1
            C.ac += 1
            def ev_f2(e, i=i, ch=ch, slot=slot, tgt=C.pe):
                e.wait_ge(s_pe, tgt)
                e.activation(out=fh2[:, ch * 512:ch * 512 + 512], in_=psB[0:64, slot * 512:slot * 512 + 512],
                             func=AF.Gelu, bias=mb2c[0:64, i:i + 1], scale=1.0).then_inc(s_ac, 1)
            qq("scalar", ev_f2)
        f2_ac = C.ac
        for mt in range(4):
            C.ac += 1
            def mkabs(e, i=i, mt=mt):
                e.activation(out=wsc[:, 0:1], in_=decc[:, i * 4 + mt:i * 4 + mt + 1], func=AF.Abs,
                             bias=0.0, scale=1.0).then_inc(s_ac, 1)
            qq("scalar", mkabs)
            C.ev += 1
            def mkneg(e, tgt=C.ac):
                e.wait_ge(s_ac, tgt)
                e.tensor_scalar_mul(wsc[:, 1:2], wsc[:, 0:1], -4096.0 / 4095.0).then_inc(s_ev, 1)
            qq("vector", mkneg)
            wsc_ev = C.ev
            hob_evb = C.ev
            k = i * 4 + mt
            hbuf = hob if k % 2 == 0 else U2[0:128, 4096:8192]
            for ch in range(8):
                slot = ch % 2
                def mm_f3(e, i=i, mt=mt, ch=ch, slot=slot, tgt=f2_ac, evb=hob_evb):
                    if ch == 0 and mt == 0:
                        e.wait_ge(s_ac, tgt)
                    e.wait_ge(s_ev, evb + ch - 1 if ch >= 2 else evb)
                    e.matmul(psA[:, slot * 512:slot * 512 + 512], mw3s[0:64, i * 512 + mt * 128:i * 512 + mt * 128 + 128],
                             fh2[:, ch * 512:ch * 512 + 512], start=True, stop=True).then_inc(s_pe, 1)
                qq("tensor", mm_f3)
                C.pe += 1
                C.ac += 1
                def ev_f3a(e, ch=ch, hbuf=hbuf, dtgt=dh_ac[k % 2], wtgt=wsc_ev, petgt=C.pe):
                    if ch == 0:
                        e.wait_ge(s_ev, wtgt)
                        if dtgt:
                            e.wait_ge(s_ac, dtgt)
                    e.activation(out=hbuf[:, ch * 512:ch * 512 + 512], in_=scs[:, ch * 512:ch * 512 + 512],
                                 func=AF.Exp, bias=0.0, scale=wsc[:, 1:2]).then_inc(s_ac, 1)
                qq("scalar", ev_f3a)
                C.ev += 1
                def ev_f3b(e, i=i, mt=mt, ch=ch, slot=slot, hbuf=hbuf, tgt=C.ac, petgt=C.pe):
                    e.wait_ge(s_ac, tgt)
                    e.wait_ge(s_pe, petgt)
                    e.scalar_tensor_tensor(out=hbuf[:, ch * 512:ch * 512 + 512], in0=psA[:, slot * 512:slot * 512 + 512],
                                           scalar=mb3c[:, i * 4 + mt:i * 4 + mt + 1], in1=hbuf[:, ch * 512:ch * 512 + 512],
                                           op0=ALU.add, op1=ALU.mult).then_inc(s_ev, 1)
                qq("vector", ev_f3b)
            C.out += 16
            C.ac += 1
            def d_h(e, i=i, mt=mt, hbuf=hbuf, etgt=C.ev, otgt=C.out):
                e.wait_ge(s_ev, etgt)
                e.dma_start(out=D["h0own" if i == 0 else "h1own"][mt * 128:(mt + 1) * 128, :],
                            in_=hbuf[:, :]).then_inc(s_out, 16)
                e.wait_ge(s_out, otgt)
                e.sem_inc(s_ac, 1)
            qq("sync", d_h)
            dh_ac[k % 2] = C.ac
        if i == 0 and h_cb is not None:
            h_cb()

    for t in reversed(ctx):
        t.__exit__(None, None, None)
    return None




CONV_TAB_SHAPES = {"W1s": [128, 128], "A2s": [128, 4096], "B2s": [128, 4096], "B2n": [128, 4096],
                   "Ps": [128, 128], "Qs": [128, 128], "Qn": [128, 128], "Ss": [128, 4096],
                   "sgt2": [128, 1], "sgtau": [1, 128], "ident": [128, 128]}


def conv_tables():
    t1g = np.arange(64); k1g = np.arange(64); t2g = np.arange(128)
    k2g = np.arange(64); taug = np.arange(128); t1pg = np.arange(32)
    W1 = np.zeros((128, 128), np.float32)
    ang = 2 * np.pi * np.outer(t1g, k1g) / 64.0
    W1[:64, 0::2] = np.cos(ang)
    W1[:64, 1::2] = -np.sin(ang)
    A2 = np.zeros((64, 128, 64), np.float32)
    B2 = np.zeros((64, 128, 64), np.float32)
    for k1 in range(64):
        th = 2 * np.pi * np.outer(t2g, k1 + 64 * k2g) / 8192.0
        A2[k1] = np.cos(th)
        B2[k1] = np.sin(th)
    phi = 2 * np.pi * np.outer(k2g, taug) / 128.0
    Pd = np.concatenate([np.cos(phi), np.cos(phi)], axis=0)
    Qd = np.concatenate([np.sin(phi), np.sin(phi)], axis=0)
    S = np.zeros((128, 128, 32), np.float32)
    for tau in range(128):
        th = 2 * np.pi * (tau * k1g[:, None] / 8192.0 + np.outer(k1g, t1pg) / 64.0)
        S[tau][:64] = np.cos(th)
        S[tau][64:] = -np.sin(th)
    S *= 2.0 / 8192.0
    sgt2 = ((-1.0) ** t2g).astype(np.float32).reshape(128, 1)
    sgtau = (0.5 * (-1.0) ** taug).astype(np.float32).reshape(1, 128)
    ident = np.eye(128, dtype=np.float32)
    A2s = A2.transpose(1, 0, 2).reshape(128, 64 * 64)
    B2s = B2.transpose(1, 0, 2).reshape(128, 64 * 64)
    Ss = S.transpose(1, 0, 2).reshape(128, 128 * 32)
    c = lambda a: np.ascontiguousarray(a).astype(nbf)
    return {"W1s": c(W1), "A2s": c(A2s), "B2s": c(B2s), "B2n": c(-B2s),
            "Ps": c(Pd), "Qs": c(Qd), "Qn": c(-Qd), "Ss": c(Ss),
            "sgt2": c(sgt2), "sgtau": c(sgtau), "ident": c(ident)}


def emit_conv(nc, Q, C, SEM, tabs, vg, hg, yown, order, ccsem=None, vcc=0, hcc=0, first_tag="v"):
    ctx = []
    def ent(t):
        ctx.append(t)
        return t.__enter__()
    sb = lambda name, shape, dt: ent(nc.sbuf_tensor(name + "_o%d" % order, shape, dt))
    ps_ = lambda name, shape: ent(nc.psum_tensor(name + "_o%d" % order, shape, F32))

    xv = sb("xv", [128, 16384], BF16)
    vg2 = vg.rearrange("(rh c) w -> rh c w", rh=2)
    C2 = sb("C2", [128, 16384], BF16)
    Gt = sb("Gt", [128, 16384], BF16)
    Zvr = sb("Zvr", [128, 4096], BF16)
    Zvi = sb("Zvi", [128, 4096], BF16)
    Zhr = sb("Zhr", [128, 4096], BF16)
    Zhi = sb("Zhi", [128, 4096], BF16)
    tscr = sb("tscr", [128, 4096], BF16)
    ny = sb("ny", [1, 512], F32)
    nyz = sb("nyz", [1, 128], BF16)
    psA = ps_("psA", [128, 2048])
    psB = ps_("psB", [128, 2048])

    (s_ld, s_w, s_pe, s_ev, s_ac, s_po, s_out) = SEM

    def copy_op(e, o, i):
        if hasattr(e, "tensor_copy"):
            e.tensor_copy(o, i)
            return None
        e.copy(o, i)
        return None

    def copy_inc(e, o, i, semh, n=1):
        if hasattr(e, "tensor_copy"):
            e.tensor_copy(o, i).then_inc(semh, n)
        else:
            e.copy(o, i).then_inc(semh, n)

    def emit_conv_inner(Q, C):
        def z0(e):
            e.memset(xv[:, :], 0.0).then_inc(s_po, 1)
        Q["gpsimd"].append(z0)
        C.po += 1
        Q["sync"].append(lambda e, v=C.po: e.wait_ge(s_po, v))

        passes = (("v", None, Zvr, Zvi), ("h", None, Zhr, Zhi))
        if first_tag == "h":
            passes = (passes[1], passes[0])
        prev_s1_pe = None
        for db in range(8):
            for (tag, src, Zr, Zi) in passes:
                if prev_s1_pe is not None:
                    Q["sync"].append(lambda e, v=prev_s1_pe: e.wait_ge(s_pe, v))
                if ccsem is not None and db == 0:
                    if tag == "v" and vcc:
                        Q["sync"].append(lambda e, v=vcc: e.wait_ge(ccsem, v))
                    if tag == "h" and hcc:
                        Q["sync"].append(lambda e, v=hcc: e.wait_ge(ccsem, v))
                if tag == "v":
                    for rh in range(2):
                        ap = vg2[rh, :, :].rearrange("c (t1p t2) -> t1p c t2", t1p=16, t2=128)[:, db * 128:(db + 1) * 128, :]
                        C.ld += 16
                        Q["sync"].append(lambda e, a=ap, rh=rh: e.dma_start(out=xv[rh * 16:(rh + 1) * 16, :], in_=a).then_inc(s_ld, 16))
                else:
                    row0 = (db // 4) * 512 + (db % 4) * 128
                    ap = hg[row0:row0 + 128, :].rearrange("d (t1 t2) -> t1 d t2", t1=32, t2=128)
                    C.ld += 16
                    Q["sync"].append(lambda e, a=ap: e.dma_start(out=xv[0:32, :], in_=a).then_inc(s_ld, 16))

                Q["tensor"].append(lambda e, v=C.ld: e.wait_ge(s_ld, v))
                ev_base = C.ev
                for g in range(8):
                    ps = psA if (g % 2 == 0) else psB
                    if g >= 2:
                        Q["tensor"].append(lambda e, v=ev_base + g - 1: e.wait_ge(s_ev, v))
                    else:
                        Q["tensor"].append(lambda e, v=ev_base: e.wait_ge(s_ev, v))
                    def mm_s1(e, ps=ps, g=g):
                        for dd in range(16):
                            d = g * 16 + dd
                            m = e.matmul(ps[:, dd * 128:(dd + 1) * 128], xv[:, d * 128:(d + 1) * 128],
                                         tabs["W1s"][:, :], start=True, stop=True)
                        m.then_inc(s_pe, 1)
                    Q["tensor"].append(mm_s1)
                    C.pe += 1
                    eng = "vector" if g % 2 == 0 else "scalar"
                    def evac_s1(e, ps=ps, g=g, tgt=C.pe):
                        e.wait_ge(s_pe, tgt)
                        o = C2.rearrange("p (j d) -> p j d", j=128, d=128)[:, :, g * 16:(g + 1) * 16]
                        i = ps.rearrange("p (d j) -> p j d", d=16, j=128)
                        copy_inc(e, o, i, s_ev)
                    C.ev += 1
                    Q[eng].append(evac_s1)
                prev_s1_pe = C.pe
                Q["tensor"].append(lambda e, v=C.ev: e.wait_ge(s_ev, v))

                ev_base = C.ev
                for p in range(32):
                    ps = psA if ((p // 4) % 2 == 0) else psB
                    slot = p % 4
                    recol = slot * 512
                    imcol = slot * 512 + 128
                    if p >= 8:
                        Q["tensor"].append(lambda e, v=ev_base + p - 7: e.wait_ge(s_ev, v))
                    k1lo, k1hi = p, p + 32
                    def mm_s2(e, ps=ps, recol=recol, imcol=imcol, k1lo=k1lo, k1hi=k1hi):
                        CrLO = C2[:, (2 * k1lo) * 128:(2 * k1lo + 1) * 128]
                        CiLO = C2[:, (2 * k1lo + 1) * 128:(2 * k1lo + 2) * 128]
                        CrHI = C2[:, (2 * k1hi) * 128:(2 * k1hi + 1) * 128]
                        CiHI = C2[:, (2 * k1hi + 1) * 128:(2 * k1hi + 2) * 128]
                        A_lo = tabs["A2s"][:, k1lo * 64:(k1lo + 1) * 64]
                        B_lo = tabs["B2s"][:, k1lo * 64:(k1lo + 1) * 64]
                        Bn_lo = tabs["B2n"][:, k1lo * 64:(k1lo + 1) * 64]
                        A_hi = tabs["A2s"][:, k1hi * 64:(k1hi + 1) * 64]
                        B_hi = tabs["B2s"][:, k1hi * 64:(k1hi + 1) * 64]
                        Bn_hi = tabs["B2n"][:, k1hi * 64:(k1hi + 1) * 64]
                        e.matmul(ps[0:64, recol:recol + 128], A_lo, CrLO, start=True, stop=False)
                        e.matmul(ps[0:64, recol:recol + 128], B_lo, CiLO, start=False, stop=True)
                        e.matmul(ps[64:128, recol:recol + 128], A_hi, CrHI, start=True, stop=False, tile_position=(0, 64))
                        e.matmul(ps[64:128, recol:recol + 128], B_hi, CiHI, start=False, stop=True, tile_position=(0, 64))
                        e.matmul(ps[0:64, imcol:imcol + 128], A_lo, CiLO, start=True, stop=False)
                        e.matmul(ps[0:64, imcol:imcol + 128], Bn_lo, CrLO, start=False, stop=True)
                        e.matmul(ps[64:128, imcol:imcol + 128], A_hi, CiHI, start=True, stop=False, tile_position=(0, 64))
                        m = e.matmul(ps[64:128, imcol:imcol + 128], Bn_hi, CrHI, start=False, stop=True, tile_position=(0, 64))
                        m.then_inc(s_pe, 1)
                    Q["tensor"].append(mm_s2)
                    C.pe += 1
                    eng = "vector" if p % 2 == 0 else "scalar"
                    def evac_s2(e, ps=ps, recol=recol, imcol=imcol, p=p, tgt=C.pe, Zr=Zr, Zi=Zi):
                        e.wait_ge(s_pe, tgt)
                        copy_op(e, Zr[:, p * 128:(p + 1) * 128], ps[:, recol:recol + 128])
                        copy_inc(e, Zi[:, p * 128:(p + 1) * 128], ps[:, imcol:imcol + 128], s_ev)
                    C.ev += 1
                    Q[eng].append(evac_s2)
                Q["tensor"].append(lambda e, v=C.ev: e.wait_ge(s_ev, v))

                def mm_ny(e):
                    e.matmul(psB[0:1, 0:128], tabs["sgt2"][:, :], C2[:, 0:128], start=True, stop=True).then_inc(s_pe, 1)
                Q["tensor"].append(mm_ny)
                C.pe += 1
                off = 0 if tag == "v" else 128
                def evac_ny(e, off=off, tgt=C.pe):
                    e.wait_ge(s_pe, tgt)
                    copy_inc(e, ny[0:1, off:off + 128], psB[0:1, 0:128], s_ev)
                C.ev += 1
                Q["vector"].append(evac_ny)
                Q["tensor"].append(lambda e, v=C.ev: e.wait_ge(s_ev, v))

            pw_b = C.w
            pa_b = C.ac
            def prod_chunk(e, lo, hi):
                sl = slice(lo, hi)
                e.tensor_tensor(tscr[:, sl], Zvr[:, sl], Zhr[:, sl], op=ALU.mult)
                e.tensor_tensor(Zhr[:, sl], Zvi[:, sl], Zhr[:, sl], op=ALU.mult)
                e.tensor_tensor(Zvi[:, sl], Zvi[:, sl], Zhi[:, sl], op=ALU.mult)
                e.tensor_tensor(Zhi[:, sl], Zvr[:, sl], Zhi[:, sl], op=ALU.mult)
                e.tensor_tensor(Zvr[:, sl], tscr[:, sl], Zvi[:, sl], op=ALU.subtract)
                return e.tensor_tensor(Zvi[:, sl], Zhi[:, sl], Zhr[:, sl], op=ALU.add)
            def prod_v(e, vtgt=C.ev, ptgt=C.po):
                e.wait_ge(s_ev, vtgt)
                e.wait_ge(s_po, ptgt)
                prod_chunk(e, 0, 3072)
                e.tensor_scalar_mul(Zvr[0:1, 0:128], Zvr[0:1, 0:128], 0.5)
                e.tensor_scalar_mul(Zvi[0:1, 0:128], Zvi[0:1, 0:128], 0.5)
                e.tensor_tensor(ny[0:1, 256:384], ny[0:1, 0:128], ny[0:1, 128:256], op=ALU.mult)
                copy_inc(e, nyz[0:1, :], ny[0:1, 256:384], s_ev)
            C.ev += 1
            def prod_g(e, vtgt=C.ev - 1, ptgt=C.po):
                e.wait_ge(s_ev, vtgt)
                e.wait_ge(s_po, ptgt)
                prod_chunk(e, 3072, 4096).then_inc(s_ev, 1)
            C.ev += 1
            Q["vector"].append(prod_v)
            Q["gpsimd"].append(prod_g)
            Q["tensor"].append(lambda e, v=C.ev: e.wait_ge(s_ev, v))

            for g in range(2):
                pr = slice(0, 64) if g == 0 else slice(64, 128)
                for comp in range(2):
                    if not (g == 0 and comp == 0):
                        Q["tensor"].append(lambda e, v=C.ev: e.wait_ge(s_ev, v))
                    for c in range(8):
                        ps = psA if c < 4 else psB
                        col = (c % 4) * 512
                        mv = slice(c * 512, (c + 1) * 512)
                        def mm_invA(e, ps=ps, col=col, g=g, comp=comp, c=c, pr=pr, mv=mv):
                            Zr_m = Zvr[pr, mv]
                            Zi_m = Zvi[pr, mv]
                            Pt = tabs["Ps"][pr, :]
                            Qt = tabs["Qs"][pr, :]
                            Qnt = tabs["Qn"][pr, :]
                            if comp == 0:
                                e.matmul(ps[:, col:col + 512], Pt, Zr_m, start=True, stop=False)
                                nyq_here = (g == 0 and c == 0)
                                m = e.matmul(ps[:, col:col + 512], Qnt, Zi_m, start=False, stop=not nyq_here)
                                if nyq_here:
                                    m = e.matmul(ps[:, col:col + 128], tabs["sgtau"][0:1, :], nyz[0:1, :],
                                                 start=False, stop=True)
                            else:
                                e.matmul(ps[:, col:col + 512], Pt, Zi_m, start=True, stop=False)
                                m = e.matmul(ps[:, col:col + 512], Qt, Zr_m, start=False, stop=True)
                            m.then_inc(s_pe, 1)
                        Q["tensor"].append(mm_invA)
                        C.pe += 1
                        eng = "vector" if c % 2 == 0 else "scalar"
                        kbase = comp * 64 + g * 32 + 4 * c
                        def evac_invA(e, ps=ps, col=col, kbase=kbase, tgt=C.pe):
                            e.wait_ge(s_pe, tgt)
                            o = Gt.rearrange("p (d k) -> p d k", d=128, k=128)[:, :, kbase:kbase + 4]
                            i = ps[:, col:col + 512].rearrange("p (a b) -> p b a", a=4, b=128)
                            copy_inc(e, o, i, s_ev)
                        C.ev += 1
                        Q[eng].append(evac_invA)
            Q["tensor"].append(lambda e, v=C.ev: e.wait_ge(s_ev, v))

            ev_base = C.ev
            for batch in range(16):
                ps = psA if (batch % 2 == 0) else psB
                if batch >= 2:
                    Q["tensor"].append(lambda e, v=ev_base + batch - 1: e.wait_ge(s_ev, v))
                def mm_tr(e, ps=ps, batch=batch):
                    for dd in range(8):
                        d = batch * 8 + dd
                        m = e.matmul(ps[:, dd * 128:(dd + 1) * 128], Gt[:, d * 128:(d + 1) * 128],
                                     tabs["ident"][:, :], start=True, stop=True)
                    m.then_inc(s_pe, 1)
                Q["tensor"].append(mm_tr)
                C.pe += 1
                eng = "vector" if batch % 2 == 0 else "scalar"
                def evac_tr(e, ps=ps, batch=batch, tgt=C.pe):
                    e.wait_ge(s_pe, tgt)
                    o = C2.rearrange("p (t d) -> p t d", t=128, d=128)[:, :, batch * 8:(batch + 1) * 8]
                    i = ps[:, 0:1024].rearrange("p (d t) -> p t d", d=8, t=128)
                    copy_inc(e, o, i, s_ev)
                C.ev += 1
                Q[eng].append(evac_tr)
            Q["tensor"].append(lambda e, v=C.ev: e.wait_ge(s_ev, v))

            ev_base = C.ev
            for batch in range(16):
                ps = psA if (batch % 2 == 0) else psB
                if batch >= 2:
                    Q["tensor"].append(lambda e, v=ev_base + batch - 1: e.wait_ge(s_ev, v))
                def mm_sb(e, ps=ps, batch=batch):
                    for tt in range(8):
                        tau = batch * 8 + tt
                        m = e.matmul(ps[:, tt * 16:(tt + 1) * 16], C2[:, tau * 128:(tau + 1) * 128],
                                     tabs["SsT"][:, tau * 16:(tau + 1) * 16], start=True, stop=True)
                    m.then_inc(s_pe, 1)
                Q["tensor"].append(mm_sb)
                C.pe += 1
                eng = "vector" if batch % 2 == 0 else "scalar"
                def evac_sb(e, ps=ps, batch=batch, db=db, tgt=C.pe):
                    e.wait_ge(s_pe, tgt)
                    o = yown[:, db * 2048:(db + 1) * 2048].rearrange("p (a b) -> p a b", a=16, b=128)[:, :, batch * 8:(batch + 1) * 8]
                    i = ps[:, 0:128].rearrange("p (t a) -> p a t", t=8, a=16)
                    copy_inc(e, o, i, s_ev)
                C.ev += 1
                Q[eng].append(evac_sb)

            def blk_done(e, etgt=C.ev):
                e.wait_ge(s_ev, etgt)
                e.sem_inc(s_po, 1)
            Q["gpsimd"].append(blk_done)
            C.po += 1

    emit_conv_inner(Q, C)
    for t in reversed(ctx):
        t.__exit__(None, None, None)
    return None


def emit_gating(qq, C, sems, bufs, first_ld):
    s_ld, s_w, s_pe, s_ev, s_ac, s_po, s_out = sems
    ctrlh, yslc, xcb, vno, gwwin, gsb, cwc, cbc, gbc2, gw_r, psA, psB = bufs
    for dt in range(8):
        C.ev += 1
        def mkxc(e, dt=dt, ltgt=first_ld):
            e.wait_ge(s_ld, ltgt)
            c0 = dt * 2050
            o = slice(dt * 2048, (dt + 1) * 2048)
            e.tensor_scalar(xcb[:, o], ctrlh[:, c0:c0 + 2048], cwc[:, dt:dt + 1], None, op0=ALU.mult)
            e.scalar_tensor_tensor(out=xcb[:, o], in0=ctrlh[:, c0 + 1:c0 + 2049], scalar=cwc[:, 8 + dt:9 + dt],
                                   in1=xcb[:, o], op0=ALU.mult, op1=ALU.add)
            e.scalar_tensor_tensor(out=xcb[:, o], in0=ctrlh[:, c0 + 2:c0 + 2050], scalar=cwc[:, 16 + dt:17 + dt],
                                   in1=xcb[:, o], op0=ALU.mult, op1=ALU.add)
            e.tensor_scalar(xcb[:, o], xcb[:, o], cbc[:, dt:dt + 1], None, op0=ALU.add).then_inc(s_ev, 1)
        qq("vector", mkxc)
    xc_ev = C.ev
    gw_last = [0, 0]
    acb = C.ac
    evb = C.ev
    for et in range(8):
        C.w += 16
        def lw(e, et=et, wp=gw_last[et % 2]):
            e.wait_ge(s_pe, wp)
            e.dma_start(out=gwwin[:, (et % 2) * 1024:(et % 2) * 1024 + 1024],
                        in_=gw_r[:, et, :]).then_inc(s_w, 16)
        qq("gpsimd", lw)
        wq = C.w
        for ch in range(4):
            i = 4 * et + ch
            ps = psA if i % 2 == 0 else psB
            def mm_g(e, et=et, ch=ch, i=i, ps=ps, wq=wq, xtgt=xc_ev, acb=acb, first=(i == 0)):
                e.wait_ge(s_w, wq)
                if first:
                    e.wait_ge(s_ev, xtgt)
                if i >= 2:
                    e.wait_ge(s_ac, acb + i - 1)
                for dt in range(8):
                    m = e.matmul(ps[:, 0:512], gwwin[:, (et % 2) * 1024 + dt * 128:(et % 2) * 1024 + dt * 128 + 128],
                                 xcb[:, dt * 2048 + ch * 512:dt * 2048 + ch * 512 + 512],
                                 start=(dt == 0), stop=(dt == 7))
                m.then_inc(s_pe, 1)
            qq("tensor", mm_g)
            C.pe += 1
            C.ac += 1
            def ev_g(e, et=et, ch=ch, i=i, ps=ps, tgt=C.pe, evb=evb):
                e.wait_ge(s_pe, tgt)
                if i >= 2:
                    e.wait_ge(s_ev, evb + i - 1)
                e.activation(out=gsb[:, (ch % 2) * 512:(ch % 2) * 512 + 512], in_=ps[:, 0:512],
                             func=AF.Sigmoid, bias=gbc2[:, et:et + 1], scale=1.0).then_inc(s_ac, 1)
            qq("scalar", ev_g)
            C.ev += 1
            def ev_v(e, et=et, ch=ch, tgt=C.ac):
                e.wait_ge(s_ac, tgt)
                o = slice(et * 2048 + ch * 512, et * 2048 + ch * 512 + 512)
                e.tensor_tensor(vno[:, o], yslc[:, o], xcb[:, o], op=ALU.mult)
                e.tensor_tensor(vno[:, o], vno[:, o], gsb[:, (ch % 2) * 512:(ch % 2) * 512 + 512],
                                op=ALU.mult).then_inc(s_ev, 1)
            qq("vector", ev_v)
        gw_last[et % 2] = C.pe
    return C.ev








def emit_gate(nc, D, Q, C, SEM, yown, vno_ext, order, gidx):
    (s_ld, s_w, s_pe, s_ev, s_ac, s_po, s_out) = SEM
    ctx = []
    def ent(t):
        ctx.append(t); return t.__enter__()
    sb = lambda n, s, dt: ent(nc.sbuf_tensor(n + "_g%d" % gidx, s, dt))
    pst = lambda n, s: ent(nc.psum_tensor(n + "_g%d" % gidx, s, F32))
    def qq(eng, fn):
        Q[eng].append(fn)

    ctrlh = sb("ctrlh", [128, 8 * 2050], BF16)
    xcb = sb("xcb", [128, 8 * 2048], BF16)
    vno = vno_ext if vno_ext is not None else sb("vnoL", [128, 8 * 2048], BF16)
    gwwin = sb("gwwin", [128, 2048], BF16)
    gsb = sb("gsb", [128, 1024], BF16)
    cwc = sb("cwc", [128, 24], F32)
    cbc = sb("cbc", [128, 8], F32)
    gbc2 = sb("gbc2", [128, 8], F32)
    egb = sb("egb", [128, 64], BF16)
    hm = sb("hm", [128, 4], F32)
    psA = pst("psA", [128, 2048])
    psB = pst("psB", [128, 2048])

    cd = D["c%dd" % order]
    gw_r = D["gw%d" % order].rearrange("(et p) n -> p et n", et=8, p=128)

    def load(dst, src):
        C.ld += 16
        qq("sync", lambda e, d=dst, s=src: e.dma_start(out=d, in_=s).then_inc(s_ld, 16))

    for dt in range(8):
        load(ctrlh[:, dt * 2050 + 1:dt * 2050 + 2049], cd[dt * 128:(dt + 1) * 128, :])
    load(egb[:, :], D["edgeg"].rearrange("(r dt p) k -> p (r dt) k", r=2, dt=8, p=128))
    for nm, t in [("cwc%d" % order, cwc), ("cbc%d" % order, cbc), ("gbc2%d" % order, gbc2), ("hm", hm)]:
        load(t[:, :], D[nm][:, :])
    ld0 = C.ld

    for dt in range(8):
        C.ev += 1
        def mkhalo(e, dt=dt, order=order, tgt=ld0):
            e.wait_ge(s_ld, tgt)
            lastc = 2 * order + 1
            firstc = 2 * order
            e.tensor_scalar(ctrlh[:, dt * 2050:dt * 2050 + 1], egb[:, dt * 4 + lastc:dt * 4 + lastc + 1],
                            hm[:, 0:1], None, op0=ALU.mult)
            e.scalar_tensor_tensor(out=ctrlh[:, dt * 2050:dt * 2050 + 1],
                                   in0=egb[:, 32 + dt * 4 + lastc:32 + dt * 4 + lastc + 1], scalar=hm[:, 1:2],
                                   in1=ctrlh[:, dt * 2050:dt * 2050 + 1], op0=ALU.mult, op1=ALU.add)
            e.tensor_scalar(ctrlh[:, dt * 2050 + 2049:dt * 2050 + 2050],
                            egb[:, dt * 4 + firstc:dt * 4 + firstc + 1], hm[:, 2:3], None, op0=ALU.mult)
            e.scalar_tensor_tensor(out=ctrlh[:, dt * 2050 + 2049:dt * 2050 + 2050],
                                   in0=egb[:, 32 + dt * 4 + firstc:32 + dt * 4 + firstc + 1], scalar=hm[:, 3:4],
                                   in1=ctrlh[:, dt * 2050 + 2049:dt * 2050 + 2050],
                                   op0=ALU.mult, op1=ALU.add).then_inc(s_ev, 1)
        qq("vector", mkhalo)

    bufs = (ctrlh, yown, xcb, vno, gwwin, gsb, cwc, cbc, gbc2, gw_r, psA, psB)
    emit_gating(qq, C, SEM, bufs, C.ld)
    if order == 0:
        for et in range(8):
            C.out += 16
            def d_v(e, et=et, etgt=C.ev, otgt=C.out):
                e.wait_ge(s_ev, etgt)
                e.dma_start(out=D["v1d"][et * 128:(et + 1) * 128, :],
                            in_=vno[:, et * 2048:(et + 1) * 2048]).then_inc(s_out, 16)
                e.wait_ge(s_out, otgt)
            qq("gpsimd", d_v)
    for t in reversed(ctx):
        t.__exit__(None, None, None)
    return vno


def emit_p4rest(nc, D, Q, C, SEM, vno, acc):
    (s_ld, s_w, s_pe, s_ev, s_ac, s_po, s_out) = SEM
    ctx = []
    def ent(t):
        ctx.append(t); return t.__enter__()
    sb = lambda n, s, dt: ent(nc.sbuf_tensor(n + "_p4", s, dt))
    pst = lambda n, s: ent(nc.psum_tensor(n + "_p4", s, F32))
    def qq(eng, fn):
        Q[eng].append(fn)

    x3 = sb("x3", [128, 16384], BF16)
    hst = sb("hst", [128, 16384], BF16)
    x3T = sb("x3T", [128, 16384], BF16)
    gwwin = sb("gwwin", [128, 4096], BF16)
    w2win = sb("w2win", [128, 4096], BF16)
    opbc = sb("opbc", [128, 8], F32)
    wrs = sb("wrs", [128, 64], BF16)
    lt = sb("lt", [128, 128], F32)
    mew = sb("mew", [128, 128], F32)
    cols = sb("cols", [128, 64], F32)
    rowf = sb("rowf", [1, 4096], F32)
    onesc = sb("onesc", [128, 2], BF16)
    onesrB = sb("onesrB", [1, 128], F32)
    identt = sb("identt", [128, 128], BF16)
    sqb = sb("sqb", [128, 512], BF16)
    outst = sb("outst", [128, 1024], F32)
    psA = pst("psA", [128, 2048])
    psB = pst("psB", [128, 2048])

    def qq_(eng):
        return lambda fn: Q[eng].append(fn)

    def phase_barrier():
        for eng in ("tensor", "scalar", "vector"):
            Q[eng].append(lambda e, a=C.ev, b=C.ac, c=C.ld, d=C.po: (e.wait_ge(s_ev, a), e.wait_ge(s_ac, b),
                                                                     e.wait_ge(s_ld, c), e.wait_ge(s_po, d)))

    opw_r = D["opw"].rearrange("(et p) n -> p et n", et=8, p=128)
    w1_r = D["w1e"].rearrange("(ex ht p) n -> p ex ht n", ex=8, ht=32, p=128)
    w2_r = D["w2e"].rearrange("(ex ht p) e -> p ex ht e", ex=8, ht=32, p=128)
    gw_last = [0, 0]
    w2_last = [0, 0]

    def load(dst, src):
        C.ld += 16
        qq("sync", lambda e, d=dst, s=src: e.dma_start(out=d, in_=s).then_inc(s_ld, 16))

    for dt in range(8):
        load(x3[:, dt * 2048:(dt + 1) * 2048], D["x2d"][dt * 128:(dt + 1) * 128, :])
    for nm, t in [("opbc", opbc), ("wrs", wrs), ("identd", identt)]:
        load(t[:, :], D[nm][:, :])
    ld0 = C.ld
    C.ev += 1
    def init_ones(e, tgt=ld0):
        e.wait_ge(s_ld, tgt)
        e.memset(onesc[:, :], 1.0)
        e.memset(onesrB[:, :], 1.0)
        e.memset(cols[:, :], 0.0)
        e.memset(cols[:, 60:61], 1e-5).then_inc(s_ev, 1)
    qq("vector", init_ones)
    qq("tensor", lambda e, v=C.ev: e.wait_ge(s_ev, v))
    qq("tensor", lambda e, v=C.ld: e.wait_ge(s_ld, v))

    for et in range(8):
        C.w += 16
        def lw(e, et=et, wp=gw_last[et % 2]):
            e.wait_ge(s_pe, wp)
            e.dma_start(out=gwwin[:, (et % 2) * 1024:(et % 2) * 1024 + 1024],
                        in_=opw_r[:, et, :]).then_inc(s_w, 16)
        qq("gpsimd", lw)
        wq = C.w
        for ch in range(4):
            ps = psA if (4 * et + ch) % 2 == 0 else psB
            def mm_op(e, et=et, ch=ch, ps=ps, wq=wq, etgt=C.ev, first=(et == 0 and ch == 0)):
                e.wait_ge(s_w, wq)
                if not first:
                    e.wait_ge(s_ev, etgt - 1)
                for dt in range(8):
                    m = e.matmul(ps[:, 0:512], gwwin[:, (et % 2) * 1024 + dt * 128:(et % 2) * 1024 + dt * 128 + 128],
                                 vno[:, dt * 2048 + ch * 512:dt * 2048 + ch * 512 + 512],
                                 start=(dt == 0), stop=(dt == 7))
                m.then_inc(s_pe, 1)
            qq("tensor", mm_op)
            C.pe += 1
            C.ev += 1
            def ev_op(e, et=et, ch=ch, ps=ps, tgt=C.pe):
                e.wait_ge(s_pe, tgt)
                sl = slice(et * 2048 + ch * 512, et * 2048 + ch * 512 + 512)
                e.scalar_tensor_tensor(out=x3[:, sl], in0=ps[:, 0:512], scalar=opbc[:, et:et + 1], in1=x3[:, sl],
                                       op0=ALU.add, op1=ALU.add).then_inc(s_ev, 1)
            qq("vector", ev_op)
        gw_last[et % 2] = C.pe

    phase_barrier()
    def mm_sum(e, etgt=C.ev):
        e.wait_ge(s_ev, etgt)
        for ch in range(4):
            for t in range(8):
                m = e.matmul(psA[0:1, ch * 512:ch * 512 + 512], onesc[:, 0:1],
                             x3[:, t * 2048 + ch * 512:t * 2048 + ch * 512 + 512], start=(t == 0), stop=(t == 7))
        m.then_inc(s_pe, 1)
    qq("tensor", mm_sum)
    C.pe += 1
    C.ev += 1
    def ev_sum(e, tgt=C.pe):
        e.wait_ge(s_pe, tgt)
        e.tensor_scalar_mul(rowf[0:1, 0:2048], psA[0:1, 0:2048], -1.0 / 1024.0)
        e.tensor_scalar_add(cols[:, 62:63], cols[:, 62:63], 0.0).then_inc(s_ev, 1)
    qq("vector", ev_sum)
    sums_ev = C.ev
    for ch in range(4):
        for t in range(8):
            C.ac += 1
            def sq(e, ch=ch, t=t, ptgt=C.pe):
                if not (ch == 0 and t == 0):
                    e.wait_ge(s_pe, ptgt)
                e.activation(out=sqb[:, 0:512], in_=x3[:, t * 2048 + ch * 512:t * 2048 + ch * 512 + 512],
                             func=AF.Square, bias=0.0, scale=1.0).then_inc(s_ac, 1)
            qq("scalar", sq)
            def mm_sq(e, ch=ch, t=t, atgt=C.ac, etgt=sums_ev):
                e.wait_ge(s_ac, atgt)
                if ch == 0 and t == 0:
                    e.wait_ge(s_ev, etgt)
                m = e.matmul(psA[0:1, ch * 512:ch * 512 + 512], onesc[:, 0:1], sqb[:, 0:512],
                             start=(t == 0), stop=(t == 7))
                m.then_inc(s_pe, 1)
            qq("tensor", mm_sq)
            C.pe += 1
    C.ev += 1
    def rowm(e, tgt=C.pe):
        e.wait_ge(s_pe, tgt)
        e.tensor_tensor(rowf[0:1, 2048:4096], rowf[0:1, 0:2048], rowf[0:1, 0:2048], op=ALU.mult)
        e.scalar_tensor_tensor(out=rowf[0:1, 2048:4096], in0=psA[0:1, 0:2048], scalar=1.0 / 1024.0,
                               in1=rowf[0:1, 2048:4096], op0=ALU.mult, op1=ALU.subtract)
        e.tensor_scalar_add(rowf[0:1, 2048:4096], rowf[0:1, 2048:4096], 1e-5).then_inc(s_ev, 1)
    qq("vector", rowm)
    C.ac += 1
    def rows_sqrt(e, tgt=C.ev):
        e.wait_ge(s_ev, tgt)
        e.activation(out=rowf[0:1, 2048:4096], in_=rowf[0:1, 2048:4096], func=AF.Sqrt, bias=0.0, scale=1.0).then_inc(s_ac, 1)
    qq("scalar", rows_sqrt)
    C.ev += 1
    def rows_rec(e, tgt=C.ac):
        e.wait_ge(s_ac, tgt)
        e.reciprocal(rowf[0:1, 2048:4096], rowf[0:1, 2048:4096]).then_inc(s_ev, 1)
    qq("vector", rows_rec)
    rr_ev = C.ev
    for ch in range(4):
        pcol = (ch % 2) * 512
        def mm_bc(e, ch=ch, pcol=pcol, tgt=rr_ev, etgt=C.ev):
            e.wait_ge(s_ev, tgt)
            if ch >= 2:
                e.wait_ge(s_ev, etgt)
            e.matmul(psA[:, pcol:pcol + 512], onesrB[0:1, :], rowf[0:1, ch * 512:ch * 512 + 512], start=True, stop=True)
            e.matmul(psB[:, pcol:pcol + 512], onesrB[0:1, :], rowf[0:1, 2048 + ch * 512:2048 + ch * 512 + 512],
                     start=True, stop=True).then_inc(s_pe, 1)
        qq("tensor", mm_bc)
        C.pe += 1
        for t in range(8):
            C.ev += 1
            def apply(e, t=t, ch=ch, pcol=pcol, tgt=C.pe):
                e.wait_ge(s_pe, tgt)
                sl = slice(t * 2048 + ch * 512, t * 2048 + ch * 512 + 512)
                e.tensor_tensor(x3[:, sl], x3[:, sl], psA[:, pcol:pcol + 512], op=ALU.add)
                e.tensor_tensor(x3[:, sl], x3[:, sl], psB[:, pcol:pcol + 512], op=ALU.mult).then_inc(s_ev, 1)
            qq("vector", apply)
    phase_barrier()

    lg_evb = C.ev
    for tb in range(16):
        def mm_lg(e, tb=tb, etgt=C.ev, evb=lg_evb):
            if tb == 0:
                e.wait_ge(s_ev, etgt)
            if tb >= 2:
                e.wait_ge(s_ev, evb + tb - 1)
            for dt in range(8):
                m = e.matmul(psB[:, 1024 + (tb % 2) * 512:1024 + (tb % 2) * 512 + 8],
                             x3[:, dt * 2048 + tb * 128:dt * 2048 + tb * 128 + 128],
                             wrs[:, dt * 8:dt * 8 + 8], start=(dt == 0), stop=(dt == 7))
            m.then_inc(s_pe, 1)
        qq("tensor", mm_lg)
        C.pe += 1
        C.ev += 1
        def ev_lg(e, tb=tb, tgt=C.pe):
            e.wait_ge(s_pe, tgt)
            e.tensor_copy(lt[:, tb * 8:tb * 8 + 8],
                          psB[:, 1024 + (tb % 2) * 512:1024 + (tb % 2) * 512 + 8]).then_inc(s_ev, 1)
        qq("vector", ev_lg)
    lg_ev = C.ev
    C.ev += 1
    def mkmaskA(e, tgt=lg_ev):
        e.wait_ge(s_ev, tgt)
        ltv = lt.rearrange("p (tb e) -> p tb e", tb=16, e=8)
        e.tensor_reduce(cols[:, 0:16].rearrange("p (a b) -> p a b", a=16, b=1), ltv, axis=mybir.AxisListType.X, op=ALU.max)
        e.tensor_scalar_add(cols[:, 62:63], cols[:, 62:63], 0.0).then_inc(s_ev, 1)
    qq("vector", mkmaskA)
    C.ac += 1
    def echoA(e, tgt=C.ev):
        e.wait_ge(s_ev, tgt)
        e.activation(out=cols[:, 61:62], in_=cols[:, 61:62], func=AF.Copy, bias=0.0, scale=0.0).then_inc(s_ac, 1)
    qq("scalar", echoA)
    C.ev += 1
    def mkmaskB(e, tgt=C.ac):
        e.wait_ge(s_ac, tgt)
        ltv = lt.rearrange("p (tb e) -> p tb e", tb=16, e=8)
        for tb in range(16):
            e.tensor_tensor(mew[:, tb * 8:tb * 8 + 8], lt[:, tb * 8:tb * 8 + 8],
                            cols[:, tb:tb + 1].to_broadcast([128, 8]), op=ALU.is_equal)
        for tb in range(16):
            e.scalar_tensor_tensor(out=lt[:, tb * 8:tb * 8 + 8], in0=mew[:, tb * 8:tb * 8 + 8], scalar=-1e30,
                                   in1=lt[:, tb * 8:tb * 8 + 8], op0=ALU.mult, op1=ALU.add)
        e.tensor_reduce(cols[:, 16:32].rearrange("p (a b) -> p a b", a=16, b=1), ltv, axis=mybir.AxisListType.X,
                        op=ALU.max, negate=True)
        e.tensor_scalar_add(cols[:, 62:63], cols[:, 62:63], 0.0).then_inc(s_ev, 1)
    qq("vector", mkmaskB)
    C.ac += 1
    def echoB(e, tgt=C.ev):
        e.wait_ge(s_ev, tgt)
        e.activation(out=cols[:, 61:62], in_=cols[:, 61:62], func=AF.Copy, bias=0.0, scale=0.0).then_inc(s_ac, 1)
    qq("scalar", echoB)
    C.ev += 1
    def mkmaskC(e, tgt=C.ac):
        e.wait_ge(s_ac, tgt)
        e.tensor_tensor(cols[:, 16:32], cols[:, 0:16], cols[:, 16:32], op=ALU.add)
        e.tensor_scalar_add(cols[:, 62:63], cols[:, 62:63], 0.0).then_inc(s_ev, 1)
    qq("vector", mkmaskC)
    C.ac += 1
    def mkw3(e, tgt=C.ev):
        e.wait_ge(s_ev, tgt)
        e.activation(out=cols[:, 32:48], in_=cols[:, 16:32], func=AF.Sigmoid, bias=0.0, scale=1.0)
        e.activation(out=cols[:, 61:62], in_=cols[:, 61:62], func=AF.Copy, bias=0.0, scale=0.0).then_inc(s_ac, 1)
    qq("scalar", mkw3)
    C.ev += 1
    def mkmew(e, tgt=C.ac):
        e.wait_ge(s_ac, tgt)
        for tb in range(16):
            e.tensor_scalar(mew[:, tb * 8:tb * 8 + 8], mew[:, tb * 8:tb * 8 + 8], cols[:, 32 + tb:33 + tb], None,
                            op0=ALU.mult)
        e.tensor_scalar_add(cols[:, 62:63], cols[:, 62:63], 0.0).then_inc(s_ev, 1)
    qq("vector", mkmew)
    phase_barrier()

    w1_r2 = D["w1e"].rearrange("(ex h2 a p) n -> p ex h2 a n", ex=8, h2=16, a=2, p=128)
    w2_r2 = D["w2e"].rearrange("(ex h2 a p) e -> p ex h2 a e", ex=8, h2=16, a=2, p=128)
    gwp_last = [max(gw_last), 0]
    w2p_last = [0, 0]
    prev_evb = None
    for ex in range(8):
        for tc in range(4):
            gel_acb = C.ac
            for ht in range(32):
                if ht % 2 == 0:
                    j = ht // 2
                    C.ld += 16
                    def lw1(e, ex=ex, j=j, wp=gwp_last[j % 2]):
                        e.wait_ge(s_pe, wp)
                        e.dma_start(out=gwwin[:, (j % 2) * 2048:(j % 2) * 2048 + 2048].rearrange(
                                        "p (a n) -> p a n", a=2, n=1024),
                                    in_=w1_r2[:, ex, j, :, :]).then_inc(s_ld, 16)
                    qq("sync", lw1)
                w1w = C.ld
                ps = psA if ht % 2 == 0 else psB
                def mm_m1(e, ex=ex, tc=tc, ht=ht, ps=ps, wq=w1w, acb=gel_acb, pevb=prev_evb):
                    e.wait_ge(s_ld, wq)
                    if ht >= 2:
                        e.wait_ge(s_ac, acb + ht - 1)
                    if pevb is not None and ht == 0:
                        e.wait_ge(s_ev, pevb + 1)
                    if pevb is not None and ht == 1:
                        e.wait_ge(s_ev, pevb + 3)
                    for dt in range(8):
                        m = e.matmul(ps[:, 0:512], gwwin[:, (ht % 4) * 1024 + dt * 128:(ht % 4) * 1024 + dt * 128 + 128],
                                     x3[:, dt * 2048 + tc * 512:dt * 2048 + tc * 512 + 512],
                                     start=(dt == 0), stop=(dt == 7))
                    m.then_inc(s_pe, 1)
                qq("tensor", mm_m1)
                C.pe += 1
                if ht % 2 == 1:
                    gwp_last[(ht // 2) % 2] = C.pe
                C.ac += 1
                def ev_m1(e, ht=ht, ps=ps, tgt=C.pe):
                    e.wait_ge(s_pe, tgt)
                    e.activation(out=hst[:, ht * 512:ht * 512 + 512], in_=ps[:, 0:512],
                                 func=AF.Gelu, bias=0.0, scale=1.0).then_inc(s_ac, 1)
                qq("scalar", ev_m1)
            m1_ac = C.ac
            for ht in range(32):
                if ht % 2 == 0:
                    j = ht // 2
                    C.w += 16
                    def lw2(e, ex=ex, j=j, wp=w2p_last[j % 2]):
                        e.wait_ge(s_pe, wp)
                        e.dma_start(out=w2win[:, (j % 2) * 2048:(j % 2) * 2048 + 2048].rearrange(
                                        "p (a n) -> p a n", a=2, n=1024),
                                    in_=w2_r2[:, ex, j, :, :]).then_inc(s_w, 16)
                    qq("gpsimd", lw2)
                wq = C.w
                def mm_m2(e, ex=ex, tc=tc, ht=ht, wq=wq, atgt=m1_ac, pevb=prev_evb):
                    e.wait_ge(s_w, wq)
                    if ht == 0:
                        e.wait_ge(s_ac, atgt)
                        if pevb is not None:
                            e.wait_ge(s_ev, pevb + 4)
                    for tb in range(4):
                        psx = psA if tb < 2 else psB
                        col = (tb % 2) * 1024
                        m = e.matmul(psx[:, col:col + 512], hst[:, ht * 512 + tb * 128:ht * 512 + tb * 128 + 128],
                                     w2win[:, (ht % 4) * 1024:(ht % 4) * 1024 + 512], start=(ht == 0), stop=(ht == 31))
                        m = e.matmul(psx[:, col + 512:col + 1024], hst[:, ht * 512 + tb * 128:ht * 512 + tb * 128 + 128],
                                     w2win[:, (ht % 4) * 1024 + 512:(ht % 4) * 1024 + 1024],
                                     start=(ht == 0), stop=(ht == 31))
                    m.then_inc(s_pe, 1)
                qq("tensor", mm_m2)
                C.pe += 1
                if ht % 2 == 1:
                    w2p_last[(ht // 2) % 2] = C.pe
            prev_evb = C.ev
            for tb in range(4):
                tbg = tc * 4 + tb
                psx = psA if tb < 2 else psB
                col = (tb % 2) * 1024
                C.ev += 1
                def ev_m2(e, ex=ex, tbg=tbg, psx=psx, col=col, tgt=C.pe):
                    e.wait_ge(s_pe, tgt)
                    o = slice(tbg * 1024, (tbg + 1) * 1024)
                    if ex == 0:
                        e.tensor_scalar(acc[:, o], psx[:, col:col + 1024], mew[:, tbg * 8 + ex:tbg * 8 + ex + 1], None,
                                        op0=ALU.mult).then_inc(s_ev, 1)
                    else:
                        e.scalar_tensor_tensor(out=acc[:, o], in0=psx[:, col:col + 1024],
                                               scalar=mew[:, tbg * 8 + ex:tbg * 8 + ex + 1], in1=acc[:, o],
                                               op0=ALU.mult, op1=ALU.add).then_inc(s_ev, 1)
                qq("vector", ev_m2)
    qq("tensor", lambda e, v=C.ev: e.wait_ge(s_ev, v))
    phase_barrier()

    tr_evb = C.ev
    for tb in range(16):
        for dt in range(8):
            it = tb * 8 + dt
            slot = it % 4
            def mm_t(e, tb=tb, dt=dt, slot=slot, etgt=C.ev, it=it):
                if it >= 4:
                    e.wait_ge(s_ev, etgt - 3)
                e.matmul(psB[:, slot * 512:slot * 512 + 128], x3[:, dt * 2048 + tb * 128:dt * 2048 + tb * 128 + 128],
                         identt[:, :], start=True, stop=True).then_inc(s_pe, 1)
            qq("tensor", mm_t)
            C.pe += 1
            C.ev += 1
            def ev_t(e, tb=tb, dt=dt, slot=slot, tgt=C.pe):
                e.wait_ge(s_pe, tgt)
                e.tensor_copy(x3T[:, tb * 1024 + dt * 128:tb * 1024 + dt * 128 + 128],
                              psB[:, slot * 512:slot * 512 + 128]).then_inc(s_ev, 1)
            qq("vector", ev_t)
    phase_barrier()

    for tb in range(16):
        C.ev += 1
        def fz(e, tb=tb):
            o = slice(tb * 1024, (tb + 1) * 1024)
            e.tensor_tensor(x3T[:, o], x3T[:, o], acc[:, o], op=ALU.add)
            e.bn_stats(cols[:, 16:22], x3T[:, o.start:o.start + 512])
            e.bn_stats(cols[:, 22:28], x3T[:, o.start + 512:o.start + 1024])
            e.tensor_scalar_add(cols[:, 62:63], cols[:, 62:63], 0.0).then_inc(s_ev, 1)
        qq("vector", fz)
        C.ac += 1
        def fe(e, tgt=C.ev):
            e.wait_ge(s_ev, tgt)
            e.activation(out=cols[:, 61:62], in_=cols[:, 61:62], func=AF.Copy, bias=0.0, scale=0.0).then_inc(s_ac, 1)
        qq("scalar", fe)
        C.ev += 1
        def fg(e, tgt=C.ac):
            e.wait_ge(s_ac, tgt)
            e.bn_aggr(cols[:, 8:10], cols[:, 16:28])
            e.tensor_scalar_add(cols[:, 62:63], cols[:, 62:63], 0.0).then_inc(s_ev, 1)
        qq("vector", fg)
        C.ac += 1
        def fh(e, tgt=C.ev):
            e.wait_ge(s_ev, tgt)
            e.activation(out=cols[:, 4:5], in_=cols[:, 9:10], func=AF.Sqrt, bias=cols[:, 60:61], scale=1.0)
            e.activation(out=cols[:, 3:4], in_=cols[:, 8:9], func=AF.Copy, bias=0.0, scale=-1.0)
            e.activation(out=cols[:, 61:62], in_=cols[:, 61:62], func=AF.Copy, bias=0.0, scale=0.0).then_inc(s_ac, 1)
        qq("scalar", fh)
        C.ev += 1
        def fapply(e, tb=tb, tgt=C.ac, ptgt=C.po):
            e.wait_ge(s_ac, tgt)
            e.wait_ge(s_po, ptgt)
            e.reciprocal(cols[:, 5:6], cols[:, 4:5])
            o = slice(tb * 1024, (tb + 1) * 1024)
            e.tensor_scalar(outst[:, 0:1024], x3T[:, o], cols[:, 3:4], None, op0=ALU.add)
            e.tensor_scalar(outst[:, 0:1024], outst[:, 0:1024], cols[:, 5:6], None, op0=ALU.mult).then_inc(s_ev, 1)
        qq("vector", fapply)
        C.po += 1
        C.out += 16
        def dout(e, tb=tb, etgt=C.ev, otgt=C.out):
            e.wait_ge(s_ev, etgt)
            e.dma_start(out=D["outT"][tb * 128:(tb + 1) * 128, :], in_=outst[:, 0:1024]).then_inc(s_out, 16)
            e.wait_ge(s_out, otgt)
            e.sem_inc(s_po, 1)
        qq("gpsimd", dout)
    for t in reversed(ctx):
        t.__exit__(None, None, None)


def build_fused():
    nc = bass.Bass("TRN2", target_bir_lowering=False)
    D = {}
    shapes = dict(P1_IN)
    shapes["maskT"] = ([384, 384], BF16)
    for name, (shape, dt) in shapes.items():
        D[name] = nc.dram_tensor(name, shape, dt, kind="ExternalInput").ap()
    for name, shape, dt in [("cwc0", [128, 24], F32), ("cbc0", [128, 8], F32), ("gbc20", [128, 8], F32),
                            ("gw0", [1024, 1024], BF16),
                            ("cwc1", [128, 24], F32), ("cbc1", [128, 8], F32), ("gbc21", [128, 8], F32),
                            ("gw1", [1024, 1024], BF16),
                            ("hm", [128, 4], F32), ("SsT", [128, 2048], BF16),
                            ("opw", [1024, 1024], BF16), ("opbc", [128, 8], F32),
                            ("wrs", [128, 64], BF16), ("w1e", [32768, 1024], BF16), ("w2e", [32768, 1024], BF16)]:
        D[name] = nc.dram_tensor(name, shape, dt, kind="ExternalInput").ap()
    for k, shp in CONV_TAB_SHAPES.items():
        if k != "Ss":
            D["tab_" + k] = nc.dram_tensor("tab_" + k, shp, BF16, kind="ExternalInput").ap()
    D["outT"] = nc.dram_tensor("outT", [2048, 1024], F32, kind="ExternalOutput").ap()
    intern = {}
    def dint(name, shape, dt=BF16):
        t = nc.dram_tensor(name, shape, dt)
        intern[name] = t
        D[name] = t.ap()
        return D[name]
    dint("x2d", [1024, 2048])
    dint("v0d", [1024, 2048])
    dint("c0d", [1024, 2048])
    dint("c1d", [1024, 2048])
    dint("h0d", [512, 4096])
    dint("h1d", [512, 4096])
    dint("edges", [1024, 4])
    dint("v0g", [2048, 2048])
    dint("v1d", [1024, 2048])
    dint("v1g", [2048, 2048])
    dint("hg0", [1024, 4096])
    dint("hg1", [1024, 4096])
    dint("edgeg", [2048, 4])
    D["x2own"] = D["x2d"]
    D["v0own"] = D["v0d"]
    D["c0own"] = D["c0d"]
    D["c1own"] = D["c1d"]
    D["h0own"] = D["h0d"]
    D["h1own"] = D["h1d"]

    SEMS = []
    sem_ctx = []
    for nm in ["s_ld", "s_w", "s_pe", "s_ev", "s_ac", "s_po", "s_out", "s_cc"]:
        cm = nc.semaphore(nm)
        sem_ctx.append(cm)
        SEMS.append(cm.__enter__())
    SEM = tuple(SEMS[:7])
    s_cc = SEMS[7]
    (s_ld, s_w, s_pe, s_ev, s_ac, s_po, s_out) = SEM

    class C:
        ld = 0; w = 0; pe = 0; ev = 0; ac = 0; po = 0; out = 0; cc = 0
    Q = {"sync": [], "tensor": [], "vector": [], "scalar": [], "gpsimd": []}

    def full_barrier():
        for eng in ("sync", "tensor", "vector", "scalar", "gpsimd"):
            def bar(e, a=C.ld, b=C.w, c=C.pe, d=C.ev, f=C.ac, g=C.po, h=C.out, i=C.cc):
                e.wait_ge(s_ld, a); e.wait_ge(s_w, b); e.wait_ge(s_pe, c); e.wait_ge(s_ev, d)
                e.wait_ge(s_ac, f); e.wait_ge(s_po, g); e.wait_ge(s_out, h); e.wait_ge(s_cc, i)
            Q[eng].append(bar)

    def collective(in_ap, out_ap):
        C.cc += 1
        def cc(e, i=in_ap, o=out_ap, w_out=C.out, w_po=C.po, prev=C.cc - 1):
            e.wait_ge(s_out, w_out)
            e.wait_ge(s_po, w_po)
            e.wait_ge(s_cc, prev)
            e.collective_compute("AllGather", ALU.bypass,
                                 replica_groups=[[0, 1], [2, 3], [4, 5], [6, 7]],
                                 ins=[i.opt()], outs=[o.opt()]).then_inc(s_cc)
        Q["gpsimd"].append(cc)

    def mid_cb():
        collective(intern["v0d"].ap(), intern["v0g"].ap())
        collective(intern["edges"].ap(), intern["edgeg"].ap())
    def h_cb():
        collective(intern["h0d"].ap(), intern["hg0"].ap())
    emit_p1(nc, D, Q, C, SEM, mid_cb, h_cb)
    full_barrier()

    pctx = []
    def pent(t):
        pctx.append(t); return t.__enter__()
    yown = pent(nc.sbuf_tensor("yown", [128, 16384], BF16))
    tabs_ctx = []
    tabs = {}
    for k, shp in CONV_TAB_SHAPES.items():
        if k == "Ss":
            continue
        cm = nc.sbuf_tensor("tb_" + k, shp, BF16)
        tabs_ctx.append(cm)
        tabs[k] = cm.__enter__()
    cmS = nc.sbuf_tensor("tb_SsT", [128, 2048], BF16)
    tabs_ctx.append(cmS)
    tabs["SsT"] = cmS.__enter__()
    for k in tabs:
        src = D["SsT"] if k == "SsT" else D["tab_" + k]
        C.ld += 16
        Q["sync"].append(lambda e, d=tabs[k], s=src: e.dma_start(out=d[:, :], in_=s[:, :]).then_inc(s_ld, 16))
    collective(intern["h1d"].ap(), intern["hg1"].ap())

    emit_conv(nc, Q, C, SEM, tabs, D["v0g"], D["hg0"], yown, 0, ccsem=s_cc, vcc=1, hcc=3)
    full_barrier()
    emit_gate(nc, D, Q, C, SEM, yown, None, 0, 0)
    full_barrier()
    collective(intern["v1d"].ap(), intern["v1g"].ap())

    emit_conv(nc, Q, C, SEM, tabs, D["v1g"], D["hg1"], yown, 1, ccsem=s_cc, vcc=5, hcc=4, first_tag="h")
    full_barrier()
    for cm in reversed(tabs_ctx):
        cm.__exit__(None, None, None)

    vno = pent(nc.sbuf_tensor("vno_f", [128, 16384], BF16))
    emit_gate(nc, D, Q, C, SEM, yown, vno, 1, 1)
    full_barrier()
    emit_p4rest(nc, D, Q, C, SEM, vno, yown)
    Q["gpsimd"].append(lambda e, v=C.out: e.wait_ge(s_out, v))

    with nc.allow_low_precision(reason="bf16 row stats within validated tolerance"):
        with nc.Block() as block:
            @block.sync
            def _(e):
                for f in Q["sync"]:
                    f(e)
            @block.tensor
            def _(e):
                for f in Q["tensor"]:
                    f(e)
            @block.vector
            def _(e):
                for f in Q["vector"]:
                    f(e)
            @block.scalar
            def _(e):
                for f in Q["scalar"]:
                    f(e)
            @block.gpsimd
            def _(e):
                for f in Q["gpsimd"]:
                    f(e)
    for cm in reversed(pctx):
        cm.__exit__(None, None, None)
    for cm in reversed(sem_ctx):
        cm.__exit__(None, None, None)
    return nc, D



def _blk(w, nb):
    K, N = w.shape
    d = K // 128
    return np.ascontiguousarray(w.reshape(d, 128, nb, 128).transpose(2, 1, 0, 3).reshape(nb * 128, K))


def _p1_host_inputs(inp):
    f32 = np.float32
    x0 = np.ascontiguousarray(inp['L0_attn_x'], f32)
    gamma = inp['L0_attn_gamma'].astype(f32); beta = inp['L0_attn_beta'].astype(f32)
    gbc = np.zeros((128, 16), f32)
    gbc[:, 0::2] = gamma.reshape(8, 128).T
    gbc[:, 1::2] = beta.reshape(8, 128).T
    wqkvT = np.ascontiguousarray(inp['L0_attn_W_qkv'].astype(f32).T)
    woutT = np.ascontiguousarray(inp['L0_attn_W_out'].astype(f32).T)
    shared = dict(
        gbc=gbc,
        wqkvT=wqkvT.astype(nbf),
        wqkvb=_blk(wqkvT[:, 0:2048], 16).astype(nbf),
        woutb=_blk(woutT, 8).astype(nbf),
        w1b0=_blk(inp['L0_ffn_w1'].astype(f32), 32).astype(nbf),
        b1c=np.ascontiguousarray(inp['L0_ffn_b1'].astype(f32).reshape(32, 128).T),
        w2b0=_blk(inp['L0_ffn_w2'].astype(f32), 8).astype(nbf),
        b2c=np.ascontiguousarray(inp['L0_ffn_b2'].astype(f32).reshape(8, 128).T),
        inpb=_blk(inp['L1_attn_in_proj_w'].astype(f32), 24).astype(nbf),
        inpbc=np.ascontiguousarray(inp['L1_attn_in_proj_b'].astype(f32).reshape(24, 128).T),
        identd=np.eye(128, dtype=f32).astype(nbf),
        fbias=np.array([0, 0, 0, 0, np.pi / 2, np.pi / 2, np.pi / 2], f32).reshape(7, 1),
        titc=np.ascontiguousarray(np.broadcast_to(np.arange(4096, dtype=f32), (128, 4096))),
    )
    freqs = inp['L1_attn_freqs'].astype(f32)
    frcol = np.zeros((7, 2), f32)
    for i in range(2):
        frcol[1:4, i] = 2 * np.pi * freqs[i] / 4095.0
        frcol[4:7, i] = 2 * np.pi * freqs[i] / 4095.0
    shared['frcol'] = frcol
    perm = [6, 0, 1, 2, 3, 4, 5]
    mw1p = np.zeros((7, 128), f32); mw2s = np.zeros((64, 128), f32)
    mb1c = np.zeros((64, 2), f32); mb2c = np.zeros((64, 2), f32)
    for i in range(2):
        mw1p[:, i * 64:(i + 1) * 64] = inp['L1_attn_mlp_w1'].astype(f32)[i][perm, :]
        mw2s[:, i * 64:(i + 1) * 64] = inp['L1_attn_mlp_w2'].astype(f32)[i]
        mb1c[:, i] = inp['L1_attn_mlp_b1'].astype(f32)[i]
        mb2c[:, i] = inp['L1_attn_mlp_b2'].astype(f32)[i]
    shared.update(mw1p=mw1p.astype(nbf), mw2s=mw2s.astype(nbf), mb1c=mb1c, mb2c=mb2c)
    masks = []
    for half in range(2):
        m = np.full((384, 384), -1e30, f32)
        own_start = half * 2048
        for mi in range(3):
            for r in range(128):
                for j0 in range(r, min(384, r + 257)):
                    if mi == 2 or (own_start + mi * 128 + j0 - 256) >= 0:
                        m[mi * 128 + r, j0] = 0.0
        masks.append(m.astype(nbf))
    w3 = inp['L1_attn_mlp_w3'].astype(f32)
    b3 = inp['L1_attn_mlp_b3'].astype(f32)
    dec = inp['L1_attn_decay'].astype(f32)
    per_core = []
    for core in range(8):
        b = core // 2; half = core % 2; os_ = half * 2048
        xin = np.zeros((1024, 2304), f32)
        lo = max(0, os_ - 256)
        xin[:, 256 - (os_ - lo):2304] = x0[b].T[:, lo:os_ + 2048]
        ch0 = half * 512
        mw3s = np.zeros((64, 1024), f32)
        mb3c = np.zeros((128, 8), f32)
        decc = np.zeros((128, 8), f32)
        for i in range(2):
            mw3s[:, i * 512:(i + 1) * 512] = w3[i][:, ch0:ch0 + 512]
            for mt in range(4):
                mb3c[:, i * 4 + mt] = b3[i][ch0 + mt * 128:ch0 + (mt + 1) * 128]
                decc[:, i * 4 + mt] = dec[i][ch0 + mt * 128:ch0 + (mt + 1) * 128]
        m = dict(shared)
        m.update(xin=np.ascontiguousarray(xin).astype(nbf), maskT=masks[half],
                 mw3s=mw3s.astype(nbf), mb3c=mb3c, decc=decc)
        per_core.append(m)
    return per_core


def _fused_inputs(inp):
    f32 = np.float32
    per_core = _p1_host_inputs(inp)
    tabs = conv_tables()
    Ss_full = tabs.pop("Ss")
    base = {("tab_" + k): v for k, v in tabs.items()}
    for order in range(2):
        cw = inp['L1_attn_conv_w'].astype(f32)[order][:, 0, :]
        cb = inp['L1_attn_conv_b'].astype(f32)[order]
        gb = inp['L1_attn_gate_b'].astype(f32)[order]
        cwc = np.zeros((128, 24), f32)
        for k in range(3):
            cwc[:, k * 8:(k + 1) * 8] = cw[k].reshape(8, 128).T
        base["cwc%d" % order] = cwc
        base["cbc%d" % order] = np.ascontiguousarray(cb.reshape(8, 128).T)
        base["gbc2%d" % order] = np.ascontiguousarray(gb.reshape(8, 128).T)
        base["gw%d" % order] = _blk(inp['L1_attn_gate_w'].astype(f32)[order], 8).astype(nbf)
    base["opw"] = _blk(inp['L1_attn_out_proj_w'].astype(f32), 8).astype(nbf)
    base["opbc"] = np.ascontiguousarray(inp['L1_attn_out_proj_b'].astype(f32).reshape(8, 128).T)
    Wr = inp['L1_ffn_Wr'].astype(f32)
    base["wrs"] = np.ascontiguousarray(Wr.reshape(8, 128, 8).transpose(1, 0, 2).reshape(128, 64)).astype(nbf)
    base["w1e"] = np.ascontiguousarray(
        inp['L1_ffn_W1'].astype(f32).reshape(8, 8, 128, 32, 128).transpose(0, 3, 2, 1, 4).reshape(32768, 1024)
    ).astype(nbf)
    base["w2e"] = np.ascontiguousarray(inp['L1_ffn_W2'].astype(f32).reshape(32768, 1024)).astype(nbf)
    Ssv = np.asarray(Ss_full.astype(f32)).reshape(128, 128, 32)
    for core in range(8):
        half = core % 2
        SsT = np.ascontiguousarray(Ssv[:, :, half * 16:(half + 1) * 16].reshape(128, 2048)).astype(nbf)
        hm = np.zeros((128, 4), f32)
        if half == 1:
            hm[:, 0] = 1.0
        else:
            hm[:, 3] = 1.0
        per_core[core].update(base)
        per_core[core]["SsT"] = SsT
        per_core[core]["hm"] = hm
    return per_core


_CACHE = {}


def kernel(**inputs):
    if "nc" not in _CACHE:
        _CACHE["nc"] = build_fused()
    nc, D = _CACHE["nc"]
    per_core = _fused_inputs(inputs)
    res = run_bass_kernel_spmd(nc, per_core, core_ids=list(range(8))).results
    out = np.zeros((4, 4096, 1024), np.float32)
    for core in range(8):
        b = core // 2; half = core % 2; os_ = half * 2048
        out[b, os_:os_ + 2048, :] = res[core]["outT"]
    return out


# version 15
# speedup vs baseline: 1.0194x; 1.0194x over previous
import numpy as np
import ml_dtypes
import concourse.bass as bass
import concourse.mybir as mybir
from concourse.bass_utils import run_bass_kernel_spmd

F32 = mybir.dt.float32
BF16 = mybir.dt.bfloat16
AF = mybir.ActivationFunctionType
ALU = mybir.AluOpType
nbf = ml_dtypes.bfloat16
PI = float(np.pi)



P1_IN = dict(
    xin=([1024, 2304], BF16), gbc=([128, 16], F32),
    wqkvT=([1024, 3072], BF16), maskT=([384, 384 * 3], BF16),
    wqkvb=([2048, 1024], BF16), woutb=([1024, 1024], BF16),
    b1c=([128, 32], F32), b2c=([128, 8], F32),
    w1b0=([4096, 1024], BF16), w2b0=([1024, 4096], BF16),
    inpb=([3072, 1024], BF16), inpbc=([128, 24], F32),
    frcol=([7, 2], F32), fbias=([7, 1], F32),
    mw1p=([7, 128], BF16), mb1c=([64, 2], F32), mw2s=([64, 128], BF16), mb2c=([64, 2], F32),
    mw3s=([64, 1024], BF16), mb3c=([128, 8], F32), decc=([128, 8], F32),
    identd=([128, 128], BF16), titc=([128, 4096], F32),
)


def emit_p1(nc, D, Q, C, SEM, mid_cb=None, h_cb=None):
    stop = 99
    ctx = []
    def ent(t):
        ctx.append(t); return t.__enter__()
    sb = lambda n, s, dt: ent(nc.sbuf_tensor(n, s, dt))
    pst = lambda n, s: ent(nc.psum_tensor(n, s, F32))

    xf = sb("xf", [128, 8 * 2304], BF16)
    xn = sb("xn", [128, 8 * 2304], BF16)
    U1 = sb("U1", [128, 18432], BF16)
    U2 = sb("U2", [128, 16384], BF16)
    scs = sb("scs", [128, 6144], F32)
    masks = sb("masks", [128, 3 * 384], BF16)
    pts = sb("pts", [128, 2 * 384], BF16)
    ptT = sb("ptT", [128, 2 * 384], BF16)
    aos = sb("aos", [128, 16 * 64], BF16)
    qb_ = sb("qb_", [128, 2048], BF16)
    kb_ = sb("kb_", [128, 2304], BF16)
    wwin = sb("wwin", [128, 2048], BF16)
    sqb = sb("sqb", [128, 512], BF16)
    vtmp = sqb
    edgesb = sb("edgesb", [128, 32], BF16)
    rowf = sb("rowf", [1, 4608], F32)
    cols = sb("cols", [128, 48], F32)
    onesc = sb("onesc", [128, 2], BF16)
    onesr = sb("onesr", [1, 128], BF16)
    onesrF = sb("onesrF", [1, 128], F32)
    gbc = sb("gbcS", [128, 16], F32)
    b1c = sb("b1cS", [128, 32], F32)
    b2c = sb("b2cS", [128, 8], F32)
    inpbc = sb("inpbcS", [128, 24], F32)
    identt = sb("identt", [128, 128], BF16)
    frcol = sb("frcolS", [7, 2], F32)
    fbias = sb("fbiasS", [7, 1], F32)
    mw1p = sb("mw1pS", [7, 128], BF16)
    mw2s = sb("mw2sS", [64, 128], BF16)
    mb1c = sb("mb1cS", [64, 2], F32)
    mb2c = sb("mb2cS", [64, 2], F32)
    mb3c = sb("mb3cS", [128, 8], F32)
    decc = sb("deccS", [128, 8], F32)
    wsc = sb("wsc", [128, 2], F32)
    tit = sb("tit", [128, 512], F32)

    vT = U1
    mw3s = U1[0:64, 16384:17408]
    hh = U1
    aoF = U2
    w2win = U2
    enc = U2[0:7, 0:4096]
    fh1 = U2[0:64, 4096:8192]
    fh2 = U2[0:64, 8192:12288]
    hob = U2[0:128, 12288:16384]

    psA = pst("psA", [128, 2048])
    psB = pst("psB", [128, 2048])

    (s_ld, s_w, s_pe, s_ev, s_ac, s_po, s_out) = SEM
    def qq(eng, fn):
        Q[eng].append(fn)

    def load(dst, src):
        C.ld += 16
        qq("sync", lambda e, d=dst, s=src: e.dma_start(out=d, in_=s).then_inc(s_ld, 16))

    def loadw(dst, src, wait_pe=None):
        C.w += 16
        def f(e, d=dst, s=src, wp=wait_pe):
            if wp is not None:
                e.wait_ge(s_pe, wp)
            e.dma_start(out=d, in_=s).then_inc(s_w, 16)
        qq("gpsimd", f)

    def phase_barrier():
        qq("tensor", lambda e, v=C.ev: e.wait_ge(s_ev, v))
        qq("tensor", lambda e, v=C.ac: e.wait_ge(s_ac, v))
        qq("tensor", lambda e, v=C.ld: e.wait_ge(s_ld, v))
        qq("scalar", lambda e, v=C.ev: e.wait_ge(s_ev, v))
        qq("scalar", lambda e, v=C.ld: e.wait_ge(s_ld, v))
        qq("vector", lambda e, v=C.ac: e.wait_ge(s_ac, v))
        qq("vector", lambda e, v=C.ld: e.wait_ge(s_ld, v))

    wqkv_r = D["wqkvT"].rearrange("(dt p) e -> p dt e", dt=8, p=128)
    wqkvb_r = D["wqkvb"].rearrange("(et p) n -> p et n", et=16, p=128)
    woutb_r = D["woutb"].rearrange("(et p) n -> p et n", et=8, p=128)
    w1b0_r = D["w1b0"].rearrange("(ht p) n -> p ht n", ht=32, p=128)
    w2b0_r = D["w2b0"].rearrange("(et p) n -> p et n", et=8, p=128)
    inpb_r = D["inpb"].rearrange("(et p) n -> p et n", et=24, p=128)
    wwin_last = [0, 0]
    w2win_last = [0, 0]

    for dt in range(8):
        load(xf[:, dt * 2304:(dt + 1) * 2304], D["xin"][dt * 128:(dt + 1) * 128, :])
    load(masks[:, :], D["maskT"].rearrange("(m r) w -> r m w", m=3, r=128))
    for nm, t in [("gbc", gbc), ("b1c", b1c), ("b2c", b2c), ("inpbc", inpbc), ("identd", identt),
                  ("frcol", frcol), ("fbias", fbias), ("mw1p", mw1p), ("mw2s", mw2s),
                  ("mb1c", mb1c), ("mb2c", mb2c), ("mb3c", mb3c), ("decc", decc)]:
        load(t[:, :], D[nm][:, :])
    ld0 = C.ld

    C.ev += 1
    def init_ones(e):
        e.memset(onesc[:, :], 1.0)
        e.memset(onesr[:, :], 1.0)
        e.memset(onesrF[:, :], 1.0).then_inc(s_ev, 1)
    qq("vector", init_ones)

    def emit_ln(c0, n, mode):
        nch = (n + 511) // 512
        bank = lambda ch: (psA, ch * 512) if ch < 4 else (psB, 0)
        phase_barrier()
        def mm_sum(e, etgt=C.ev):
            e.wait_ge(s_ev, etgt)
            for ch in range(nch):
                cw = min(512, n - ch * 512)
                ps, col = bank(ch)
                for t in range(8):
                    m = e.matmul(ps[0:1, col:col + cw], onesc[:, 0:1],
                                 xf[:, t * 2304 + c0 + ch * 512:t * 2304 + c0 + ch * 512 + cw],
                                 start=(t == 0), stop=(t == 7))
            m.then_inc(s_pe, 1)
        qq("tensor", mm_sum)
        C.pe += 1
        C.ev += 1
        def ev_sum(e, tgt=C.pe):
            e.wait_ge(s_pe, tgt)
            w4 = min(2048, n)
            e.tensor_scalar_mul(rowf[0:1, 0:w4], psA[0:1, 0:w4], -1.0 / 1024.0)
            if n > 2048:
                e.tensor_scalar_mul(rowf[0:1, 2048:n], psB[0:1, 0:n - 2048], -1.0 / 1024.0)
            e.tensor_scalar_add(rowf[0:1, 0:1], rowf[0:1, 0:1], 0.0).then_inc(s_ev, 1)
        qq("vector", ev_sum)
        sums_ev = C.ev
        for ch in range(nch):
            cw = min(512, n - ch * 512)
            for t in range(8):
                C.ac += 1
                def sq(e, ch=ch, cw=cw, t=t, ptgt=C.pe):
                    if not (ch == 0 and t == 0):
                        e.wait_ge(s_pe, ptgt)
                    e.activation(out=sqb[:, 0:cw],
                                 in_=xf[:, t * 2304 + c0 + ch * 512:t * 2304 + c0 + ch * 512 + cw],
                                 func=AF.Square, bias=0.0, scale=1.0).then_inc(s_ac, 1)
                qq("scalar", sq)
                ps, col = bank(ch)
                def mm_sq(e, ch=ch, cw=cw, t=t, ps=ps, col=col, atgt=C.ac, etgt=sums_ev):
                    e.wait_ge(s_ac, atgt)
                    if ch == 0 and t == 0:
                        e.wait_ge(s_ev, etgt)
                    m = e.matmul(ps[0:1, col:col + cw], onesc[:, 0:1], sqb[:, 0:cw],
                                 start=(t == 0), stop=(t == 7))
                    m.then_inc(s_pe, 1)
                qq("tensor", mm_sq)
                C.pe += 1
        C.ev += 1
        def rowm(e, tgt=C.pe):
            e.wait_ge(s_pe, tgt)
            w4 = min(2048, n)
            e.tensor_tensor(rowf[0:1, 2304:2304 + n], rowf[0:1, 0:n], rowf[0:1, 0:n], op=ALU.mult)
            e.scalar_tensor_tensor(out=rowf[0:1, 2304:2304 + w4], in0=psA[0:1, 0:w4], scalar=1.0 / 1024.0,
                                   in1=rowf[0:1, 2304:2304 + w4], op0=ALU.mult, op1=ALU.subtract)
            if n > 2048:
                e.scalar_tensor_tensor(out=rowf[0:1, 2304 + 2048:2304 + n], in0=psB[0:1, 0:n - 2048],
                                       scalar=1.0 / 1024.0, in1=rowf[0:1, 2304 + 2048:2304 + n],
                                       op0=ALU.mult, op1=ALU.subtract)
            e.tensor_scalar_add(rowf[0:1, 2304:2304 + n], rowf[0:1, 2304:2304 + n], 1e-5).then_inc(s_ev, 1)
        qq("vector", rowm)
        C.ac += 1
        def rows_sqrt(e, tgt=C.ev):
            e.wait_ge(s_ev, tgt)
            e.activation(out=rowf[0:1, 2304:2304 + n], in_=rowf[0:1, 2304:2304 + n], func=AF.Sqrt, bias=0.0, scale=1.0).then_inc(s_ac, 1)
        qq("scalar", rows_sqrt)
        C.ev += 1
        def rows_rec(e, tgt=C.ac):
            e.wait_ge(s_ac, tgt)
            e.reciprocal(rowf[0:1, 2304:2304 + n], rowf[0:1, 2304:2304 + n]).then_inc(s_ev, 1)
        qq("vector", rows_rec)
        rr_ev = C.ev
        for ch in range(nch):
            cw = min(512, n - ch * 512)
            pcol = (ch % 2) * 512
            def mm_bc(e, ch=ch, cw=cw, pcol=pcol, tgt=rr_ev, etgt=C.ev):
                e.wait_ge(s_ev, tgt)
                if ch >= 2:
                    e.wait_ge(s_ev, etgt)
                e.matmul(psA[:, pcol:pcol + cw], onesrF[0:1, :], rowf[0:1, ch * 512:ch * 512 + cw],
                         start=True, stop=True)
                e.matmul(psB[:, pcol:pcol + cw], onesrF[0:1, :], rowf[0:1, 2304 + ch * 512:2304 + ch * 512 + cw],
                         start=True, stop=True).then_inc(s_pe, 1)
            qq("tensor", mm_bc)
            C.pe += 1
            for t in range(8):
                C.ev += 1
                def apply(e, t=t, ch=ch, cw=cw, pcol=pcol, tgt=C.pe, mode=mode):
                    e.wait_ge(s_pe, tgt)
                    sl = slice(t * 2304 + c0 + ch * 512, t * 2304 + c0 + ch * 512 + cw)
                    dst = xn if mode == "ln1" else xf
                    e.tensor_tensor(dst[:, sl], xf[:, sl], psA[:, pcol:pcol + cw], op=ALU.add)
                    e.tensor_tensor(dst[:, sl], dst[:, sl], psB[:, pcol:pcol + cw], op=ALU.mult)
                    if mode == "ln1":
                        e.tensor_scalar(dst[:, sl], dst[:, sl], gbc[:, 2 * t:2 * t + 1], gbc[:, 2 * t + 1:2 * t + 2],
                                        op0=ALU.mult, op1=ALU.add).then_inc(s_ev, 1)
                    else:
                        e.tensor_scalar_add(dst[:, sl], dst[:, sl], 0.0).then_inc(s_ev, 1)
                qq("vector", apply)
        phase_barrier()

    qq("tensor", lambda e, v=ld0: e.wait_ge(s_ld, v))
    emit_ln(0, 2304, "ln1")

    loadw(w2win[:, 0:4096], wqkv_r[:, :, 2048:2560])
    loadw(w2win[:, 4096:8192], wqkv_r[:, :, 2560:3072])
    vw0 = C.w
    evb = C.ev
    for half in range(2):
        for tt in range(18):
            ps = psA if tt % 2 == 0 else psB
            it = half * 18 + tt
            def mm_v(e, tt=tt, ps=ps, half=half, it=it, evb=evb, wtgt=vw0):
                if it == 0:
                    e.wait_ge(s_w, wtgt)
                if it >= 2:
                    e.wait_ge(s_ev, evb + it - 1)
                for dt in range(8):
                    m = e.matmul(ps[:, 0:512], xn[:, dt * 2304 + tt * 128:dt * 2304 + tt * 128 + 128],
                                 w2win[:, half * 4096 + dt * 512:half * 4096 + dt * 512 + 512],
                                 start=(dt == 0), stop=(dt == 7))
                m.then_inc(s_pe, 1)
            qq("tensor", mm_v)
            C.pe += 1
            C.ev += 1
            def ev_v(e, tt=tt, ps=ps, half=half, tgt=C.pe):
                e.wait_ge(s_pe, tgt)
                e.tensor_copy(vT[:, tt * 1024 + half * 512:tt * 1024 + half * 512 + 512], ps[:, 0:512]).then_inc(s_ev, 1)
            qq("vector", ev_v)

    phase_barrier()
    prev_exp_ac = 0
    prev_av_end = 0
    first_av = True
    for g in range(8):
        loadw(wwin[:, 0:1024], wqkvb_r[:, g, :], wait_pe=wwin_last[0])
        loadw(wwin[:, 1024:2048], wqkvb_r[:, 8 + g, :], wait_pe=wwin_last[1])
        wq = C.w
        evb = C.ev
        it = 0
        for (dst, off, tot, wofs, scl) in ((qb_, 256, 2048, 0, 0.125), (kb_, 0, 2304, 1024, 1.0)):
            nch = (tot + 511) // 512
            for ch in range(nch):
                cw = min(512, tot - ch * 512)
                ps = psA if it % 2 == 0 else psB
                def mm_qk(e, ps=ps, ch=ch, cw=cw, off=off, wofs=wofs, it=it, wq=wq, evb=evb):
                    if it == 0:
                        e.wait_ge(s_w, wq)
                    if it >= 2:
                        e.wait_ge(s_ev, evb + it - 1)
                    for dt in range(8):
                        m = e.matmul(ps[:, 0:cw], wwin[:, wofs + dt * 128:wofs + dt * 128 + 128],
                                     xn[:, dt * 2304 + off + ch * 512:dt * 2304 + off + ch * 512 + cw],
                                     start=(dt == 0), stop=(dt == 7))
                    m.then_inc(s_pe, 1)
                qq("tensor", mm_qk)
                C.pe += 1
                C.ev += 1
                def ev_qk(e, ps=ps, ch=ch, cw=cw, dst=dst, scl=scl, tgt=C.pe):
                    e.wait_ge(s_pe, tgt)
                    e.activation(out=dst[:, ch * 512:ch * 512 + cw], in_=ps[:, 0:cw],
                                 func=AF.Copy, bias=0.0, scale=scl).then_inc(s_ev, 1)
                qq("scalar", ev_qk)
                it += 1
            wwin_last[0 if wofs == 0 else 1] = C.pe
        qk_ev = C.ev

        for hh_ in range(2):
            h = 2 * g + hh_
            rb = hh_ * 64
            sc_evb = C.ev
            for qb in range(16):
                slot = qb % 4
                def mm_sc(e, qb=qb, slot=slot, rb=rb, sc_evb=sc_evb, qkev=qk_ev, pexp=prev_exp_ac, pav=prev_av_end):
                    if qb == 0:
                        e.wait_ge(s_ev, qkev)
                        e.wait_ge(s_ac, pexp)
                        e.wait_ge(s_ev, pav)
                    if qb >= 4:
                        e.wait_ge(s_ev, sc_evb + qb - 3)
                    e.matmul(psA[:, slot * 512:slot * 512 + 384],
                             qb_[rb:rb + 64, qb * 128:qb * 128 + 128],
                             kb_[rb:rb + 64, qb * 128:qb * 128 + 384],
                             start=True, stop=True).then_inc(s_pe, 1)
                qq("tensor", mm_sc)
                C.pe += 1
                C.ev += 1
                def ev_sc(e, qb=qb, slot=slot, tgt=C.pe):
                    e.wait_ge(s_pe, tgt)
                    mi = min(qb, 2)
                    e.scalar_tensor_tensor(out=scs[:, qb * 384:qb * 384 + 384],
                                           in0=psA[:, slot * 512:slot * 512 + 384],
                                           scalar=1.0, in1=masks[:, mi * 384:mi * 384 + 384],
                                           op0=ALU.mult, op1=ALU.add).then_inc(s_ev, 1)
                qq("vector", ev_sc)
            C.ev += 1
            def rmax(e, tgt=C.ev - 1):
                e.wait_ge(s_ev, tgt)
                e.tensor_reduce(cols[:, 0:16], scs.rearrange("p (qb w) -> p qb w", qb=16, w=384),
                                axis=mybir.AxisListType.X, op=ALU.max, negate=True).then_inc(s_ev, 1)
            qq("vector", rmax)
            rmax_ev = C.ev
            tp_pe = [None] * 16
            av_pe = [None] * 16
            exp_ac_l = [None] * 16
            tp_ev_l = [None] * 16
            av_ev_l = [None] * 16
            for qb in range(16):
                C.ac += 1
                exp_ac_l[qb] = C.ac
                def expq(e, qb=qb, tgt=rmax_ev, tppe=tp_pe[qb - 2] if qb >= 2 else None):
                    if qb == 0:
                        e.wait_ge(s_ev, tgt)
                    if qb >= 2:
                        e.wait_ge(s_pe, tppe)
                    e.activation(out=pts[:, (qb % 2) * 384:(qb % 2) * 384 + 384], in_=scs[:, qb * 384:qb * 384 + 384],
                                 func=AF.Exp, bias=cols[:, qb:qb + 1], scale=1.0,
                                 accum_out=cols[:, 16 + qb:17 + qb]).then_inc(s_ac, 1)
                qq("scalar", expq)
                def mm_tp(e, qb=qb, atgt=exp_ac_l[qb], evtgt=tp_ev_l[qb - 2] if qb >= 2 else None):
                    e.wait_ge(s_ac, atgt)
                    if qb >= 2:
                        e.wait_ge(s_ev, evtgt)
                    for kk in range(3):
                        m = e.matmul(psB[:, (qb % 2) * 512 + kk * 128:(qb % 2) * 512 + kk * 128 + 128],
                                     pts[:, (qb % 2) * 384 + kk * 128:(qb % 2) * 384 + kk * 128 + 128],
                                     identt[:, :], start=True, stop=True)
                    m.then_inc(s_pe, 1)
                qq("tensor", mm_tp)
                C.pe += 1
                tp_pe[qb] = C.pe
                C.ev += 1
                tp_ev_l[qb] = C.ev
                def ev_tp(e, qb=qb, tgt=C.pe, avpe=av_pe[qb - 2] if qb >= 2 else None):
                    e.wait_ge(s_pe, tgt)
                    if qb >= 2:
                        e.wait_ge(s_pe, avpe)
                    e.reciprocal(cols[:, 32 + qb:33 + qb], cols[:, 16 + qb:17 + qb])
                    e.tensor_copy(ptT[:, (qb % 2) * 384:(qb % 2) * 384 + 384],
                                  psB[:, (qb % 2) * 512:(qb % 2) * 512 + 384]).then_inc(s_ev, 1)
                qq("vector", ev_tp)
                def mm_av(e, qb=qb, h=h, tptgt=tp_ev_l[qb], avtgt=av_ev_l[qb - 2] if qb >= 2 else None,
                          first=(first_av and qb == 0), vtev=None):
                    e.wait_ge(s_ev, tptgt)
                    if qb >= 2:
                        e.wait_ge(s_ev, avtgt)
                    for kk in range(3):
                        ktile = qb + kk
                        m = e.matmul(psA[:, 1024 + (qb % 2) * 512:1024 + (qb % 2) * 512 + 64],
                                     ptT[:, (qb % 2) * 384 + kk * 128:(qb % 2) * 384 + kk * 128 + 128],
                                     vT[:, ktile * 1024 + h * 64:ktile * 1024 + h * 64 + 64],
                                     start=(kk == 0), stop=(kk == 2))
                    m.then_inc(s_pe, 1)
                qq("tensor", mm_av)
                C.pe += 1
                av_pe[qb] = C.pe
                C.ev += 1
                av_ev_l[qb] = C.ev
                def ev_av(e, qb=qb, tgt=C.pe, rtgt=None):
                    e.wait_ge(s_pe, tgt)
                    e.tensor_scalar(aos[:, qb * 64:qb * 64 + 64], psA[:, 1024 + (qb % 2) * 512:1024 + (qb % 2) * 512 + 64],
                                    cols[:, 32 + qb:33 + qb], None, op0=ALU.mult).then_inc(s_ev, 1)
                qq("vector", ev_av)
            first_av = False
            prev_exp_ac = exp_ac_l[15]
            prev_av_end = av_ev_l[15]
            at_acb = C.ac
            for qb in range(16):
                def mm_at(e, qb=qb, avtgt=av_ev_l[qb], atb=at_acb):
                    e.wait_ge(s_ev, avtgt)
                    if qb >= 2:
                        e.wait_ge(s_ac, atb + qb - 1)
                    e.matmul(psB[0:64, 1024 + (qb % 2) * 512:1024 + (qb % 2) * 512 + 128],
                             aos[:, qb * 64:qb * 64 + 64], identt[:, :], start=True, stop=True).then_inc(s_pe, 1)
                qq("tensor", mm_at)
                C.pe += 1
                C.ac += 1
                def ev_at(e, qb=qb, g=g, rb=rb, tgt=C.pe):
                    e.wait_ge(s_pe, tgt)
                    e.activation(out=aoF[rb:rb + 64, g * 2048 + qb * 128:g * 2048 + qb * 128 + 128],
                                 in_=psB[0:64, 1024 + (qb % 2) * 512:1024 + (qb % 2) * 512 + 128],
                                 func=AF.Copy, bias=0.0, scale=1.0).then_inc(s_ac, 1)
                qq("scalar", ev_at)
    aoF_ac = C.ac

    phase_barrier()
    for et in range(8):
        slot = et % 2
        loadw(wwin[:, slot * 1024:slot * 1024 + 1024], woutb_r[:, et, :], wait_pe=wwin_last[slot])
        wq = C.w
        for ch in range(4):
            ps = psA if (4 * et + ch) % 2 == 0 else psB
            def mm_wo(e, et=et, ch=ch, ps=ps, slot=slot, wq=wq, atgt=aoF_ac, etgt=C.ev):
                e.wait_ge(s_w, wq)
                if et == 0 and ch == 0:
                    e.wait_ge(s_ac, atgt)
                if ch >= 2 or et > 0:
                    e.wait_ge(s_ev, etgt - 1)
                for dt in range(8):
                    m = e.matmul(ps[:, 0:512], wwin[:, slot * 1024 + dt * 128:slot * 1024 + dt * 128 + 128],
                                 aoF[:, dt * 2048 + ch * 512:dt * 2048 + ch * 512 + 512],
                                 start=(dt == 0), stop=(dt == 7))
                m.then_inc(s_pe, 1)
            qq("tensor", mm_wo)
            C.pe += 1
            C.ev += 1
            def ev_wo(e, et=et, ch=ch, ps=ps, tgt=C.pe):
                e.wait_ge(s_pe, tgt)
                sl = slice(et * 2304 + 256 + ch * 512, et * 2304 + 256 + ch * 512 + 512)
                e.scalar_tensor_tensor(out=xf[:, sl], in0=xf[:, sl], scalar=2.0, in1=ps[:, 0:512],
                                       op0=ALU.mult, op1=ALU.add).then_inc(s_ev, 1)
            qq("vector", ev_wo)
        wwin_last[slot] = C.pe
    w2win_last[0] = w2win_last[1] = C.pe
    emit_ln(256, 2048, "post")
    C.ld += 16
    def ld_tit(e, etgt=C.ev):
        e.wait_ge(s_ev, etgt)
        e.dma_start(out=scs[:, 0:4096], in_=D["titc"][:, :]).then_inc(s_ld, 16)
    qq("sync", ld_tit)
    load(mw3s[:, :], D["mw3s"][:, :])
    filt_ld = C.ld

    for tc in range(4):
        phase_barrier()
        gel_acb = C.ac
        for ht in range(32):
            loadw(wwin[:, (ht % 2) * 1024:(ht % 2) * 1024 + 1024], w1b0_r[:, ht, :],
                  wait_pe=wwin_last[ht % 2])
            wq = C.w
            ps = psA if ht % 2 == 0 else psB
            def mm_w1(e, tc=tc, ht=ht, ps=ps, wq=wq, acb=gel_acb):
                e.wait_ge(s_w, wq)
                if ht >= 2:
                    e.wait_ge(s_ac, acb + ht - 1)
                for dt in range(8):
                    m = e.matmul(ps[:, 0:512],
                                 wwin[:, (ht % 2) * 1024 + dt * 128:(ht % 2) * 1024 + dt * 128 + 128],
                                 xf[:, dt * 2304 + 256 + tc * 512:dt * 2304 + 256 + tc * 512 + 512],
                                 start=(dt == 0), stop=(dt == 7))
                m.then_inc(s_pe, 1)
            qq("tensor", mm_w1)
            C.pe += 1
            wwin_last[ht % 2] = C.pe
            C.ac += 1
            def ev_w1(e, ht=ht, ps=ps, tgt=C.pe):
                e.wait_ge(s_pe, tgt)
                e.activation(out=hh[:, ht * 512:ht * 512 + 512], in_=ps[:, 0:512],
                             func=AF.Gelu, bias=b1c[:, ht:ht + 1], scale=1.0).then_inc(s_ac, 1)
            qq("scalar", ev_w1)
        w1_ac = C.ac
        for et in range(8):
            loadw(w2win[:, (et % 2) * 4096:(et % 2) * 4096 + 4096], w2b0_r[:, et, :],
                  wait_pe=w2win_last[et % 2])
            wq = C.w
            ps = psA if et % 2 == 0 else psB
            def mm_w2(e, et=et, ps=ps, wq=wq, atgt=w1_ac, etgt=C.ev):
                e.wait_ge(s_w, wq)
                if et == 0:
                    e.wait_ge(s_ac, atgt)
                if et >= 2:
                    e.wait_ge(s_ev, etgt - 1)
                for ht in range(32):
                    m = e.matmul(ps[:, 1024:1536], w2win[:, (et % 2) * 4096 + ht * 128:(et % 2) * 4096 + ht * 128 + 128],
                                 hh[:, ht * 512:ht * 512 + 512], start=(ht == 0), stop=(ht == 31))
                m.then_inc(s_pe, 1)
            qq("tensor", mm_w2)
            C.pe += 1
            w2win_last[et % 2] = C.pe
            C.ev += 1
            def ev_w2(e, et=et, tc=tc, ps=ps, tgt=C.pe):
                e.wait_ge(s_pe, tgt)
                sl = slice(et * 2304 + 256 + tc * 512, et * 2304 + 256 + tc * 512 + 512)
                e.scalar_tensor_tensor(out=xf[:, sl], in0=ps[:, 1024:1536], scalar=b2c[:, et:et + 1], in1=xf[:, sl],
                                       op0=ALU.add, op1=ALU.add).then_inc(s_ev, 1)
            qq("vector", ev_w2)
    emit_ln(256, 2048, "post")
    x2_ev = C.ev

    for dt in range(8):
        C.out += 16
        def dx2(e, dt=dt, etgt=x2_ev):
            e.wait_ge(s_ev, etgt)
            e.dma_start(out=D["x2own"][dt * 128:(dt + 1) * 128, :],
                        in_=xf[:, dt * 2304 + 256:dt * 2304 + 2304]).then_inc(s_out, 16)
        qq("gpsimd", dx2)

    phase_barrier()
    ip_po0 = C.po
    for et in range(24):
        loadw(wwin[:, (et % 2) * 1024:(et % 2) * 1024 + 1024], inpb_r[:, et, :],
              wait_pe=wwin_last[et % 2])
        wq = C.w
        name = "v0own" if et < 8 else ("c0own" if et < 16 else "c1own")
        base = et % 8
        for ch in range(4):
            ps = psA if (4 * et + ch) % 2 == 0 else psB
            half = ch % 2
            def mm_ip(e, et=et, ch=ch, ps=ps, wq=wq, etgt=C.ev, first=(et == 0 and ch == 0)):
                e.wait_ge(s_w, wq)
                if not first and etgt >= 2:
                    e.wait_ge(s_ev, etgt - 1)
                for dt in range(8):
                    m = e.matmul(ps[:, 0:512], wwin[:, (et % 2) * 1024 + dt * 128:(et % 2) * 1024 + dt * 128 + 128],
                                 xf[:, dt * 2304 + 256 + ch * 512:dt * 2304 + 256 + ch * 512 + 512],
                                 start=(dt == 0), stop=(dt == 7))
                m.then_inc(s_pe, 1)
            qq("tensor", mm_ip)
            C.pe += 1
            C.ev += 1
            i8 = (4 * et + ch) % 8
            stbuf = qb_ if i8 < 4 else kb_
            pair_idx = (4 * et + ch) // 2
            def ev_ip(e, et=et, ch=ch, ps=ps, vt=stbuf[:, (i8 % 4) * 512:(i8 % 4) * 512 + 512],
                      tgt=C.pe, pneed=ip_po0 + (pair_idx - 3) * 16, pskip=(pair_idx < 4)):
                e.wait_ge(s_pe, tgt)
                if not pskip:
                    e.wait_ge(s_po, pneed)
                e.tensor_scalar(vt[:, 0:512], ps[:, 0:512], inpbc[:, et:et + 1], None,
                                op0=ALU.add)
                if et >= 8 and ch == 0:
                    e.tensor_copy(edgesb[:, (et % 8) * 4 + (et - 8) // 8 * 2:(et % 8) * 4 + (et - 8) // 8 * 2 + 1],
                                  vt[:, 0:1])
                if et >= 8 and ch == 3:
                    e.tensor_copy(edgesb[:, (et % 8) * 4 + (et - 8) // 8 * 2 + 1:(et % 8) * 4 + (et - 8) // 8 * 2 + 2],
                                  vt[:, 511:512])
                e.tensor_scalar_add(edgesb[:, 31:32], edgesb[:, 31:32], 0.0).then_inc(s_ev, 1)
            qq("vector", ev_ip)
            if ch % 2 == 1:
                C.po += 16
                def d_ip(e, name=name, base=base, ch=ch, stbuf=stbuf, c0=((i8 - 1) % 4) * 512, etgt=C.ev):
                    e.wait_ge(s_ev, etgt)
                    e.dma_start(out=D[name][base * 128:(base + 1) * 128, (ch - 1) * 512:(ch + 1) * 512],
                                in_=stbuf[:, c0:c0 + 1024]).then_inc(s_po, 16)
                qq("gpsimd", d_ip)
        wwin_last[et % 2] = C.pe

    C.out += 16
    def d_edge(e, etgt=C.ev, otgt=C.out):
        e.wait_ge(s_ev, etgt)
        e.dma_start(out=D["edges"].rearrange("(dt p) k -> p dt k", dt=8, p=128),
                    in_=edgesb.rearrange("p (dt k) -> p dt k", dt=8, k=4)).then_inc(s_out, 16)
    qq("gpsimd", d_edge)
    if mid_cb is not None:
        mid_cb()

    phase_barrier()
    qq("tensor", lambda e, v=filt_ld: e.wait_ge(s_ld, v))
    dh_ac = [0, 0]
    for i in range(2):
        for ch in range(8):
            C.ac += 2
            def ev_enc(e, i=i, ch=ch, ltgt=filt_ld):
                if i == 0 and ch == 0:
                    e.wait_ge(s_ld, ltgt)
                e.activation(out=enc[0:7, ch * 512:ch * 512 + 512], in_=scs[0:7, ch * 512:ch * 512 + 512],
                             func=AF.Sin, bias=fbias[0:7, 0:1], scale=frcol[0:7, i:i + 1]).then_inc(s_ac, 1)
                e.activation(out=enc[0:1, ch * 512:ch * 512 + 512], in_=scs[0:1, ch * 512:ch * 512 + 512],
                             func=AF.Copy, bias=0.0, scale=1.0 / 4095.0).then_inc(s_ac, 1)
            qq("scalar", ev_enc)
        enc_ac = C.ac
        f1_acb = C.ac
        for ch in range(8):
            slot = ch % 2
            def mm_f1(e, i=i, ch=ch, slot=slot, tgt=enc_ac, acb=f1_acb):
                if ch == 0:
                    e.wait_ge(s_ac, tgt)
                e.wait_ge(s_ac, acb + ch - 1 if ch >= 2 else acb)
                e.matmul(psA[0:64, slot * 512:slot * 512 + 512], mw1p[0:7, i * 64:i * 64 + 64],
                         enc[0:7, ch * 512:ch * 512 + 512], start=True, stop=True).then_inc(s_pe, 1)
            qq("tensor", mm_f1)
            C.pe += 1
            C.ac += 1
            def ev_f1(e, i=i, ch=ch, slot=slot, tgt=C.pe, dtgt=dh_ac[1]):
                e.wait_ge(s_pe, tgt)
                if i == 1 and ch == 0 and dtgt:
                    e.wait_ge(s_ac, dtgt)
                e.activation(out=fh1[:, ch * 512:ch * 512 + 512], in_=psA[0:64, slot * 512:slot * 512 + 512],
                             func=AF.Gelu, bias=mb1c[0:64, i:i + 1], scale=1.0).then_inc(s_ac, 1)
            qq("scalar", ev_f1)
        f1_ac = C.ac
        f2_acb = C.ac
        for ch in range(8):
            slot = ch % 2
            def mm_f2(e, i=i, ch=ch, slot=slot, tgt=f1_ac, acb=f2_acb):
                if ch == 0:
                    e.wait_ge(s_ac, tgt)
                e.wait_ge(s_ac, acb + ch - 1 if ch >= 2 else acb)
                e.matmul(psB[0:64, slot * 512:slot * 512 + 512], mw2s[0:64, i * 64:i * 64 + 64],
                         fh1[:, ch * 512:ch * 512 + 512], start=True, stop=True).then_inc(s_pe, 1)
            qq("tensor", mm_f2)
            C.pe += 1
            C.ac += 1
            def ev_f2(e, i=i, ch=ch, slot=slot, tgt=C.pe):
                e.wait_ge(s_pe, tgt)
                e.activation(out=fh2[:, ch * 512:ch * 512 + 512], in_=psB[0:64, slot * 512:slot * 512 + 512],
                             func=AF.Gelu, bias=mb2c[0:64, i:i + 1], scale=1.0).then_inc(s_ac, 1)
            qq("scalar", ev_f2)
        f2_ac = C.ac
        for mt in range(4):
            C.ac += 1
            def mkabs(e, i=i, mt=mt):
                e.activation(out=wsc[:, 0:1], in_=decc[:, i * 4 + mt:i * 4 + mt + 1], func=AF.Abs,
                             bias=0.0, scale=1.0).then_inc(s_ac, 1)
            qq("scalar", mkabs)
            C.ev += 1
            def mkneg(e, tgt=C.ac):
                e.wait_ge(s_ac, tgt)
                e.tensor_scalar_mul(wsc[:, 1:2], wsc[:, 0:1], -4096.0 / 4095.0).then_inc(s_ev, 1)
            qq("vector", mkneg)
            wsc_ev = C.ev
            hob_evb = C.ev
            k = i * 4 + mt
            hbuf = hob if k % 2 == 0 else U2[0:128, 4096:8192]
            for ch in range(8):
                slot = ch % 2
                def mm_f3(e, i=i, mt=mt, ch=ch, slot=slot, tgt=f2_ac, evb=hob_evb):
                    if ch == 0 and mt == 0:
                        e.wait_ge(s_ac, tgt)
                    e.wait_ge(s_ev, evb + ch - 1 if ch >= 2 else evb)
                    e.matmul(psA[:, slot * 512:slot * 512 + 512], mw3s[0:64, i * 512 + mt * 128:i * 512 + mt * 128 + 128],
                             fh2[:, ch * 512:ch * 512 + 512], start=True, stop=True).then_inc(s_pe, 1)
                qq("tensor", mm_f3)
                C.pe += 1
                C.ac += 1
                def ev_f3a(e, ch=ch, hbuf=hbuf, dtgt=dh_ac[k % 2], wtgt=wsc_ev, petgt=C.pe):
                    if ch == 0:
                        e.wait_ge(s_ev, wtgt)
                        if dtgt:
                            e.wait_ge(s_ac, dtgt)
                    e.activation(out=hbuf[:, ch * 512:ch * 512 + 512], in_=scs[:, ch * 512:ch * 512 + 512],
                                 func=AF.Exp, bias=0.0, scale=wsc[:, 1:2]).then_inc(s_ac, 1)
                qq("scalar", ev_f3a)
                C.ev += 1
                def ev_f3b(e, i=i, mt=mt, ch=ch, slot=slot, hbuf=hbuf, tgt=C.ac, petgt=C.pe):
                    e.wait_ge(s_ac, tgt)
                    e.wait_ge(s_pe, petgt)
                    e.scalar_tensor_tensor(out=hbuf[:, ch * 512:ch * 512 + 512], in0=psA[:, slot * 512:slot * 512 + 512],
                                           scalar=mb3c[:, i * 4 + mt:i * 4 + mt + 1], in1=hbuf[:, ch * 512:ch * 512 + 512],
                                           op0=ALU.add, op1=ALU.mult).then_inc(s_ev, 1)
                qq("vector", ev_f3b)
            C.out += 16
            C.ac += 1
            def d_h(e, i=i, mt=mt, hbuf=hbuf, etgt=C.ev, otgt=C.out):
                e.wait_ge(s_ev, etgt)
                e.dma_start(out=D["h0own" if i == 0 else "h1own"][mt * 128:(mt + 1) * 128, :],
                            in_=hbuf[:, :]).then_inc(s_out, 16)
                e.wait_ge(s_out, otgt)
                e.sem_inc(s_ac, 1)
            qq("sync", d_h)
            dh_ac[k % 2] = C.ac
        if i == 0 and h_cb is not None:
            h_cb()

    for t in reversed(ctx):
        t.__exit__(None, None, None)
    return None




CONV_TAB_SHAPES = {"W1s": [128, 128], "A2s": [128, 4096], "B2s": [128, 4096], "B2n": [128, 4096],
                   "Ps": [128, 128], "Qs": [128, 128], "Qn": [128, 128], "Ss": [128, 4096],
                   "sgt2": [128, 1], "sgtau": [1, 128], "ident": [128, 128]}


def conv_tables():
    t1g = np.arange(64); k1g = np.arange(64); t2g = np.arange(128)
    k2g = np.arange(64); taug = np.arange(128); t1pg = np.arange(32)
    W1 = np.zeros((128, 128), np.float32)
    ang = 2 * np.pi * np.outer(t1g, k1g) / 64.0
    W1[:64, 0::2] = np.cos(ang)
    W1[:64, 1::2] = -np.sin(ang)
    A2 = np.zeros((64, 128, 64), np.float32)
    B2 = np.zeros((64, 128, 64), np.float32)
    for k1 in range(64):
        th = 2 * np.pi * np.outer(t2g, k1 + 64 * k2g) / 8192.0
        A2[k1] = np.cos(th)
        B2[k1] = np.sin(th)
    phi = 2 * np.pi * np.outer(k2g, taug) / 128.0
    Pd = np.concatenate([np.cos(phi), np.cos(phi)], axis=0)
    Qd = np.concatenate([np.sin(phi), np.sin(phi)], axis=0)
    S = np.zeros((128, 128, 32), np.float32)
    for tau in range(128):
        th = 2 * np.pi * (tau * k1g[:, None] / 8192.0 + np.outer(k1g, t1pg) / 64.0)
        S[tau][:64] = np.cos(th)
        S[tau][64:] = -np.sin(th)
    S *= 2.0 / 8192.0
    sgt2 = ((-1.0) ** t2g).astype(np.float32).reshape(128, 1)
    sgtau = (0.5 * (-1.0) ** taug).astype(np.float32).reshape(1, 128)
    ident = np.eye(128, dtype=np.float32)
    A2s = A2.transpose(1, 0, 2).reshape(128, 64 * 64)
    B2s = B2.transpose(1, 0, 2).reshape(128, 64 * 64)
    Ss = S.transpose(1, 0, 2).reshape(128, 128 * 32)
    c = lambda a: np.ascontiguousarray(a).astype(nbf)
    return {"W1s": c(W1), "A2s": c(A2s), "B2s": c(B2s), "B2n": c(-B2s),
            "Ps": c(Pd), "Qs": c(Qd), "Qn": c(-Qd), "Ss": c(Ss),
            "sgt2": c(sgt2), "sgtau": c(sgtau), "ident": c(ident)}


def emit_conv(nc, Q, C, SEM, tabs, vg, hg, yown, order, ccsem=None, vcc=0, hcc=0, first_tag="v"):
    ctx = []
    def ent(t):
        ctx.append(t)
        return t.__enter__()
    sb = lambda name, shape, dt: ent(nc.sbuf_tensor(name + "_o%d" % order, shape, dt))
    ps_ = lambda name, shape: ent(nc.psum_tensor(name + "_o%d" % order, shape, F32))

    xv = sb("xv", [128, 16384], BF16)
    vg2 = vg.rearrange("(rh c) w -> rh c w", rh=2)
    C2 = sb("C2", [128, 16384], BF16)
    Gt = sb("Gt", [128, 16384], BF16)
    Zvr = sb("Zvr", [128, 4096], BF16)
    Zvi = sb("Zvi", [128, 4096], BF16)
    Zhr = sb("Zhr", [128, 4096], BF16)
    Zhi = sb("Zhi", [128, 4096], BF16)
    tscr = sb("tscr", [128, 4096], BF16)
    ny = sb("ny", [1, 512], F32)
    nyz = sb("nyz", [1, 128], BF16)
    psA = ps_("psA", [128, 2048])
    psB = ps_("psB", [128, 2048])

    (s_ld, s_w, s_pe, s_ev, s_ac, s_po, s_out) = SEM

    def copy_op(e, o, i):
        if hasattr(e, "tensor_copy"):
            e.tensor_copy(o, i)
            return None
        e.copy(o, i)
        return None

    def copy_inc(e, o, i, semh, n=1):
        if hasattr(e, "tensor_copy"):
            e.tensor_copy(o, i).then_inc(semh, n)
        else:
            e.copy(o, i).then_inc(semh, n)

    def emit_conv_inner(Q, C):
        def z0(e):
            e.memset(xv[:, :], 0.0).then_inc(s_po, 1)
        Q["gpsimd"].append(z0)
        C.po += 1
        Q["sync"].append(lambda e, v=C.po: e.wait_ge(s_po, v))

        passes = (("v", None, Zvr, Zvi), ("h", None, Zhr, Zhi))
        if first_tag == "h":
            passes = (passes[1], passes[0])
        prev_s1_pe = None
        for db in range(8):
            for (tag, src, Zr, Zi) in passes:
                if prev_s1_pe is not None:
                    Q["sync"].append(lambda e, v=prev_s1_pe: e.wait_ge(s_pe, v))
                if ccsem is not None and db == 0:
                    if tag == "v" and vcc:
                        Q["sync"].append(lambda e, v=vcc: e.wait_ge(ccsem, v))
                    if tag == "h" and hcc:
                        Q["sync"].append(lambda e, v=hcc: e.wait_ge(ccsem, v))
                if tag == "v":
                    for rh in range(2):
                        ap = vg2[rh, :, :].rearrange("c (t1p t2) -> t1p c t2", t1p=16, t2=128)[:, db * 128:(db + 1) * 128, :]
                        C.ld += 16
                        Q["sync"].append(lambda e, a=ap, rh=rh: e.dma_start(out=xv[rh * 16:(rh + 1) * 16, :], in_=a).then_inc(s_ld, 16))
                else:
                    row0 = (db // 4) * 512 + (db % 4) * 128
                    ap = hg[row0:row0 + 128, :].rearrange("d (t1 t2) -> t1 d t2", t1=32, t2=128)
                    C.ld += 16
                    Q["sync"].append(lambda e, a=ap: e.dma_start(out=xv[0:32, :], in_=a).then_inc(s_ld, 16))

                Q["tensor"].append(lambda e, v=C.ld: e.wait_ge(s_ld, v))
                ev_base = C.ev
                for g in range(8):
                    ps = psA if (g % 2 == 0) else psB
                    if g >= 2:
                        Q["tensor"].append(lambda e, v=ev_base + g - 1: e.wait_ge(s_ev, v))
                    else:
                        Q["tensor"].append(lambda e, v=ev_base: e.wait_ge(s_ev, v))
                    def mm_s1(e, ps=ps, g=g):
                        for dd in range(16):
                            d = g * 16 + dd
                            m = e.matmul(ps[:, dd * 128:(dd + 1) * 128], xv[:, d * 128:(d + 1) * 128],
                                         tabs["W1s"][:, :], start=True, stop=True)
                        m.then_inc(s_pe, 1)
                    Q["tensor"].append(mm_s1)
                    C.pe += 1
                    eng = "vector" if g % 2 == 0 else "scalar"
                    def evac_s1(e, ps=ps, g=g, tgt=C.pe):
                        e.wait_ge(s_pe, tgt)
                        o = C2.rearrange("p (j d) -> p j d", j=128, d=128)[:, :, g * 16:(g + 1) * 16]
                        i = ps.rearrange("p (d j) -> p j d", d=16, j=128)
                        copy_inc(e, o, i, s_ev)
                    C.ev += 1
                    Q[eng].append(evac_s1)
                prev_s1_pe = C.pe
                Q["tensor"].append(lambda e, v=C.ev: e.wait_ge(s_ev, v))

                ev_base = C.ev
                for p in range(32):
                    ps = psA if ((p // 4) % 2 == 0) else psB
                    slot = p % 4
                    recol = slot * 512
                    imcol = slot * 512 + 128
                    if p >= 8:
                        Q["tensor"].append(lambda e, v=ev_base + p - 7: e.wait_ge(s_ev, v))
                    k1lo, k1hi = p, p + 32
                    def mm_s2(e, ps=ps, recol=recol, imcol=imcol, k1lo=k1lo, k1hi=k1hi):
                        CrLO = C2[:, (2 * k1lo) * 128:(2 * k1lo + 1) * 128]
                        CiLO = C2[:, (2 * k1lo + 1) * 128:(2 * k1lo + 2) * 128]
                        CrHI = C2[:, (2 * k1hi) * 128:(2 * k1hi + 1) * 128]
                        CiHI = C2[:, (2 * k1hi + 1) * 128:(2 * k1hi + 2) * 128]
                        A_lo = tabs["A2s"][:, k1lo * 64:(k1lo + 1) * 64]
                        B_lo = tabs["B2s"][:, k1lo * 64:(k1lo + 1) * 64]
                        Bn_lo = tabs["B2n"][:, k1lo * 64:(k1lo + 1) * 64]
                        A_hi = tabs["A2s"][:, k1hi * 64:(k1hi + 1) * 64]
                        B_hi = tabs["B2s"][:, k1hi * 64:(k1hi + 1) * 64]
                        Bn_hi = tabs["B2n"][:, k1hi * 64:(k1hi + 1) * 64]
                        e.matmul(ps[0:64, recol:recol + 128], A_lo, CrLO, start=True, stop=False)
                        e.matmul(ps[0:64, recol:recol + 128], B_lo, CiLO, start=False, stop=True)
                        e.matmul(ps[64:128, recol:recol + 128], A_hi, CrHI, start=True, stop=False, tile_position=(0, 64))
                        e.matmul(ps[64:128, recol:recol + 128], B_hi, CiHI, start=False, stop=True, tile_position=(0, 64))
                        e.matmul(ps[0:64, imcol:imcol + 128], A_lo, CiLO, start=True, stop=False)
                        e.matmul(ps[0:64, imcol:imcol + 128], Bn_lo, CrLO, start=False, stop=True)
                        e.matmul(ps[64:128, imcol:imcol + 128], A_hi, CiHI, start=True, stop=False, tile_position=(0, 64))
                        m = e.matmul(ps[64:128, imcol:imcol + 128], Bn_hi, CrHI, start=False, stop=True, tile_position=(0, 64))
                        m.then_inc(s_pe, 1)
                    Q["tensor"].append(mm_s2)
                    C.pe += 1
                    eng = "vector" if p % 2 == 0 else "scalar"
                    def evac_s2(e, ps=ps, recol=recol, imcol=imcol, p=p, tgt=C.pe, Zr=Zr, Zi=Zi):
                        e.wait_ge(s_pe, tgt)
                        copy_op(e, Zr[:, p * 128:(p + 1) * 128], ps[:, recol:recol + 128])
                        copy_inc(e, Zi[:, p * 128:(p + 1) * 128], ps[:, imcol:imcol + 128], s_ev)
                    C.ev += 1
                    Q[eng].append(evac_s2)
                Q["tensor"].append(lambda e, v=C.ev: e.wait_ge(s_ev, v))

                def mm_ny(e):
                    e.matmul(psB[0:1, 0:128], tabs["sgt2"][:, :], C2[:, 0:128], start=True, stop=True).then_inc(s_pe, 1)
                Q["tensor"].append(mm_ny)
                C.pe += 1
                off = 0 if tag == "v" else 128
                def evac_ny(e, off=off, tgt=C.pe):
                    e.wait_ge(s_pe, tgt)
                    copy_inc(e, ny[0:1, off:off + 128], psB[0:1, 0:128], s_ev)
                C.ev += 1
                Q["vector"].append(evac_ny)
                Q["tensor"].append(lambda e, v=C.ev: e.wait_ge(s_ev, v))

            pw_b = C.w
            pa_b = C.ac
            def prod_chunk(e, lo, hi):
                sl = slice(lo, hi)
                e.tensor_tensor(tscr[:, sl], Zvr[:, sl], Zhr[:, sl], op=ALU.mult)
                e.tensor_tensor(Zhr[:, sl], Zvi[:, sl], Zhr[:, sl], op=ALU.mult)
                e.tensor_tensor(Zvi[:, sl], Zvi[:, sl], Zhi[:, sl], op=ALU.mult)
                e.tensor_tensor(Zhi[:, sl], Zvr[:, sl], Zhi[:, sl], op=ALU.mult)
                e.tensor_tensor(Zvr[:, sl], tscr[:, sl], Zvi[:, sl], op=ALU.subtract)
                return e.tensor_tensor(Zvi[:, sl], Zhi[:, sl], Zhr[:, sl], op=ALU.add)
            def prod_v(e, vtgt=C.ev, ptgt=C.po):
                e.wait_ge(s_ev, vtgt)
                e.wait_ge(s_po, ptgt)
                prod_chunk(e, 0, 3072)
                e.tensor_scalar_mul(Zvr[0:1, 0:128], Zvr[0:1, 0:128], 0.5)
                e.tensor_scalar_mul(Zvi[0:1, 0:128], Zvi[0:1, 0:128], 0.5)
                e.tensor_tensor(ny[0:1, 256:384], ny[0:1, 0:128], ny[0:1, 128:256], op=ALU.mult)
                copy_inc(e, nyz[0:1, :], ny[0:1, 256:384], s_ev)
            C.ev += 1
            def prod_g(e, vtgt=C.ev - 1, ptgt=C.po):
                e.wait_ge(s_ev, vtgt)
                e.wait_ge(s_po, ptgt)
                prod_chunk(e, 3072, 4096).then_inc(s_ev, 1)
            C.ev += 1
            Q["vector"].append(prod_v)
            Q["gpsimd"].append(prod_g)
            Q["tensor"].append(lambda e, v=C.ev: e.wait_ge(s_ev, v))

            for g in range(2):
                pr = slice(0, 64) if g == 0 else slice(64, 128)
                for comp in range(2):
                    if not (g == 0 and comp == 0):
                        Q["tensor"].append(lambda e, v=C.ev: e.wait_ge(s_ev, v))
                    for c in range(8):
                        ps = psA if c < 4 else psB
                        col = (c % 4) * 512
                        mv = slice(c * 512, (c + 1) * 512)
                        def mm_invA(e, ps=ps, col=col, g=g, comp=comp, c=c, pr=pr, mv=mv):
                            Zr_m = Zvr[pr, mv]
                            Zi_m = Zvi[pr, mv]
                            Pt = tabs["Ps"][pr, :]
                            Qt = tabs["Qs"][pr, :]
                            Qnt = tabs["Qn"][pr, :]
                            if comp == 0:
                                e.matmul(ps[:, col:col + 512], Pt, Zr_m, start=True, stop=False)
                                nyq_here = (g == 0 and c == 0)
                                m = e.matmul(ps[:, col:col + 512], Qnt, Zi_m, start=False, stop=not nyq_here)
                                if nyq_here:
                                    m = e.matmul(ps[:, col:col + 128], tabs["sgtau"][0:1, :], nyz[0:1, :],
                                                 start=False, stop=True)
                            else:
                                e.matmul(ps[:, col:col + 512], Pt, Zi_m, start=True, stop=False)
                                m = e.matmul(ps[:, col:col + 512], Qt, Zr_m, start=False, stop=True)
                            m.then_inc(s_pe, 1)
                        Q["tensor"].append(mm_invA)
                        C.pe += 1
                        eng = "vector" if c % 2 == 0 else "scalar"
                        kbase = comp * 64 + g * 32 + 4 * c
                        def evac_invA(e, ps=ps, col=col, kbase=kbase, tgt=C.pe):
                            e.wait_ge(s_pe, tgt)
                            o = Gt.rearrange("p (d k) -> p d k", d=128, k=128)[:, :, kbase:kbase + 4]
                            i = ps[:, col:col + 512].rearrange("p (a b) -> p b a", a=4, b=128)
                            copy_inc(e, o, i, s_ev)
                        C.ev += 1
                        Q[eng].append(evac_invA)
            Q["tensor"].append(lambda e, v=C.ev: e.wait_ge(s_ev, v))

            ev_base = C.ev
            for batch in range(16):
                ps = psA if (batch % 2 == 0) else psB
                if batch >= 2:
                    Q["tensor"].append(lambda e, v=ev_base + batch - 1: e.wait_ge(s_ev, v))
                def mm_tr(e, ps=ps, batch=batch):
                    for dd in range(8):
                        d = batch * 8 + dd
                        m = e.matmul(ps[:, dd * 128:(dd + 1) * 128], Gt[:, d * 128:(d + 1) * 128],
                                     tabs["ident"][:, :], start=True, stop=True)
                    m.then_inc(s_pe, 1)
                Q["tensor"].append(mm_tr)
                C.pe += 1
                eng = "vector" if batch % 2 == 0 else "scalar"
                def evac_tr(e, ps=ps, batch=batch, tgt=C.pe):
                    e.wait_ge(s_pe, tgt)
                    o = C2.rearrange("p (t d) -> p t d", t=128, d=128)[:, :, batch * 8:(batch + 1) * 8]
                    i = ps[:, 0:1024].rearrange("p (d t) -> p t d", d=8, t=128)
                    copy_inc(e, o, i, s_ev)
                C.ev += 1
                Q[eng].append(evac_tr)
            Q["tensor"].append(lambda e, v=C.ev: e.wait_ge(s_ev, v))

            ev_base = C.ev
            for batch in range(16):
                ps = psA if (batch % 2 == 0) else psB
                if batch >= 2:
                    Q["tensor"].append(lambda e, v=ev_base + batch - 1: e.wait_ge(s_ev, v))
                def mm_sb(e, ps=ps, batch=batch):
                    for tt in range(8):
                        tau = batch * 8 + tt
                        m = e.matmul(ps[:, tt * 16:(tt + 1) * 16], C2[:, tau * 128:(tau + 1) * 128],
                                     tabs["SsT"][:, tau * 16:(tau + 1) * 16], start=True, stop=True)
                    m.then_inc(s_pe, 1)
                Q["tensor"].append(mm_sb)
                C.pe += 1
                eng = "vector" if batch % 2 == 0 else "scalar"
                def evac_sb(e, ps=ps, batch=batch, db=db, tgt=C.pe):
                    e.wait_ge(s_pe, tgt)
                    o = yown[:, db * 2048:(db + 1) * 2048].rearrange("p (a b) -> p a b", a=16, b=128)[:, :, batch * 8:(batch + 1) * 8]
                    i = ps[:, 0:128].rearrange("p (t a) -> p a t", t=8, a=16)
                    copy_inc(e, o, i, s_ev)
                C.ev += 1
                Q[eng].append(evac_sb)

            def blk_done(e, etgt=C.ev):
                e.wait_ge(s_ev, etgt)
                e.sem_inc(s_po, 1)
            Q["gpsimd"].append(blk_done)
            C.po += 1

    emit_conv_inner(Q, C)
    for t in reversed(ctx):
        t.__exit__(None, None, None)
    return None


def emit_gating(qq, C, sems, bufs, first_ld):
    s_ld, s_w, s_pe, s_ev, s_ac, s_po, s_out = sems
    ctrlh, yslc, xcb, vno, gwwin, gsb, cwc, cbc, gbc2, gw_r, psA, psB = bufs
    for dt in range(8):
        C.ev += 1
        def mkxc(e, dt=dt, ltgt=first_ld):
            e.wait_ge(s_ld, ltgt)
            c0 = dt * 2050
            o = slice(dt * 2048, (dt + 1) * 2048)
            e.tensor_scalar(xcb[:, o], ctrlh[:, c0:c0 + 2048], cwc[:, dt:dt + 1], None, op0=ALU.mult)
            e.scalar_tensor_tensor(out=xcb[:, o], in0=ctrlh[:, c0 + 1:c0 + 2049], scalar=cwc[:, 8 + dt:9 + dt],
                                   in1=xcb[:, o], op0=ALU.mult, op1=ALU.add)
            e.scalar_tensor_tensor(out=xcb[:, o], in0=ctrlh[:, c0 + 2:c0 + 2050], scalar=cwc[:, 16 + dt:17 + dt],
                                   in1=xcb[:, o], op0=ALU.mult, op1=ALU.add)
            e.tensor_scalar(xcb[:, o], xcb[:, o], cbc[:, dt:dt + 1], None, op0=ALU.add).then_inc(s_ev, 1)
        qq("vector", mkxc)
    xc_ev = C.ev
    gw_last = [0, 0]
    acb = C.ac
    evb = C.ev
    for et in range(8):
        C.w += 16
        def lw(e, et=et, wp=gw_last[et % 2]):
            e.wait_ge(s_pe, wp)
            e.dma_start(out=gwwin[:, (et % 2) * 1024:(et % 2) * 1024 + 1024],
                        in_=gw_r[:, et, :]).then_inc(s_w, 16)
        qq("gpsimd", lw)
        wq = C.w
        for ch in range(4):
            i = 4 * et + ch
            ps = psA if i % 2 == 0 else psB
            def mm_g(e, et=et, ch=ch, i=i, ps=ps, wq=wq, xtgt=xc_ev, acb=acb, first=(i == 0)):
                e.wait_ge(s_w, wq)
                if first:
                    e.wait_ge(s_ev, xtgt)
                if i >= 2:
                    e.wait_ge(s_ac, acb + i - 1)
                for dt in range(8):
                    m = e.matmul(ps[:, 0:512], gwwin[:, (et % 2) * 1024 + dt * 128:(et % 2) * 1024 + dt * 128 + 128],
                                 xcb[:, dt * 2048 + ch * 512:dt * 2048 + ch * 512 + 512],
                                 start=(dt == 0), stop=(dt == 7))
                m.then_inc(s_pe, 1)
            qq("tensor", mm_g)
            C.pe += 1
            C.ac += 1
            def ev_g(e, et=et, ch=ch, i=i, ps=ps, tgt=C.pe, evb=evb):
                e.wait_ge(s_pe, tgt)
                if i >= 2:
                    e.wait_ge(s_ev, evb + i - 1)
                e.activation(out=gsb[:, (ch % 2) * 512:(ch % 2) * 512 + 512], in_=ps[:, 0:512],
                             func=AF.Sigmoid, bias=gbc2[:, et:et + 1], scale=1.0).then_inc(s_ac, 1)
            qq("scalar", ev_g)
            C.ev += 1
            def ev_v(e, et=et, ch=ch, tgt=C.ac):
                e.wait_ge(s_ac, tgt)
                o = slice(et * 2048 + ch * 512, et * 2048 + ch * 512 + 512)
                e.tensor_tensor(vno[:, o], yslc[:, o], xcb[:, o], op=ALU.mult)
                e.tensor_tensor(vno[:, o], vno[:, o], gsb[:, (ch % 2) * 512:(ch % 2) * 512 + 512],
                                op=ALU.mult).then_inc(s_ev, 1)
            qq("vector", ev_v)
        gw_last[et % 2] = C.pe
    return C.ev








def emit_gate(nc, D, Q, C, SEM, yown, vno_ext, order, gidx):
    (s_ld, s_w, s_pe, s_ev, s_ac, s_po, s_out) = SEM
    ctx = []
    def ent(t):
        ctx.append(t); return t.__enter__()
    sb = lambda n, s, dt: ent(nc.sbuf_tensor(n + "_g%d" % gidx, s, dt))
    pst = lambda n, s: ent(nc.psum_tensor(n + "_g%d" % gidx, s, F32))
    def qq(eng, fn):
        Q[eng].append(fn)

    ctrlh = sb("ctrlh", [128, 8 * 2050], BF16)
    xcb = sb("xcb", [128, 8 * 2048], BF16)
    vno = vno_ext if vno_ext is not None else sb("vnoL", [128, 8 * 2048], BF16)
    gwwin = sb("gwwin", [128, 2048], BF16)
    gsb = sb("gsb", [128, 1024], BF16)
    cwc = sb("cwc", [128, 24], F32)
    cbc = sb("cbc", [128, 8], F32)
    gbc2 = sb("gbc2", [128, 8], F32)
    egb = sb("egb", [128, 64], BF16)
    hm = sb("hm", [128, 4], F32)
    psA = pst("psA", [128, 2048])
    psB = pst("psB", [128, 2048])

    cd = D["c%dd" % order]
    gw_r = D["gw%d" % order].rearrange("(et p) n -> p et n", et=8, p=128)

    def load(dst, src):
        C.ld += 16
        qq("sync", lambda e, d=dst, s=src: e.dma_start(out=d, in_=s).then_inc(s_ld, 16))

    for dt in range(8):
        load(ctrlh[:, dt * 2050 + 1:dt * 2050 + 2049], cd[dt * 128:(dt + 1) * 128, :])
    load(egb[:, :], D["edgeg"].rearrange("(r dt p) k -> p (r dt) k", r=2, dt=8, p=128))
    for nm, t in [("cwc%d" % order, cwc), ("cbc%d" % order, cbc), ("gbc2%d" % order, gbc2), ("hm", hm)]:
        load(t[:, :], D[nm][:, :])
    ld0 = C.ld

    for dt in range(8):
        C.ev += 1
        def mkhalo(e, dt=dt, order=order, tgt=ld0):
            e.wait_ge(s_ld, tgt)
            lastc = 2 * order + 1
            firstc = 2 * order
            e.tensor_scalar(ctrlh[:, dt * 2050:dt * 2050 + 1], egb[:, dt * 4 + lastc:dt * 4 + lastc + 1],
                            hm[:, 0:1], None, op0=ALU.mult)
            e.scalar_tensor_tensor(out=ctrlh[:, dt * 2050:dt * 2050 + 1],
                                   in0=egb[:, 32 + dt * 4 + lastc:32 + dt * 4 + lastc + 1], scalar=hm[:, 1:2],
                                   in1=ctrlh[:, dt * 2050:dt * 2050 + 1], op0=ALU.mult, op1=ALU.add)
            e.tensor_scalar(ctrlh[:, dt * 2050 + 2049:dt * 2050 + 2050],
                            egb[:, dt * 4 + firstc:dt * 4 + firstc + 1], hm[:, 2:3], None, op0=ALU.mult)
            e.scalar_tensor_tensor(out=ctrlh[:, dt * 2050 + 2049:dt * 2050 + 2050],
                                   in0=egb[:, 32 + dt * 4 + firstc:32 + dt * 4 + firstc + 1], scalar=hm[:, 3:4],
                                   in1=ctrlh[:, dt * 2050 + 2049:dt * 2050 + 2050],
                                   op0=ALU.mult, op1=ALU.add).then_inc(s_ev, 1)
        qq("vector", mkhalo)

    bufs = (ctrlh, yown, xcb, vno, gwwin, gsb, cwc, cbc, gbc2, gw_r, psA, psB)
    emit_gating(qq, C, SEM, bufs, C.ld)
    if order == 0:
        for et in range(8):
            C.out += 16
            def d_v(e, et=et, etgt=C.ev, otgt=C.out):
                e.wait_ge(s_ev, etgt)
                e.dma_start(out=D["v1d"][et * 128:(et + 1) * 128, :],
                            in_=vno[:, et * 2048:(et + 1) * 2048]).then_inc(s_out, 16)
                e.wait_ge(s_out, otgt)
            qq("gpsimd", d_v)
    for t in reversed(ctx):
        t.__exit__(None, None, None)
    return vno


def emit_p4rest(nc, D, Q, C, SEM, vno, acc):
    (s_ld, s_w, s_pe, s_ev, s_ac, s_po, s_out) = SEM
    ctx = []
    def ent(t):
        ctx.append(t); return t.__enter__()
    sb = lambda n, s, dt: ent(nc.sbuf_tensor(n + "_p4", s, dt))
    pst = lambda n, s: ent(nc.psum_tensor(n + "_p4", s, F32))
    def qq(eng, fn):
        Q[eng].append(fn)

    x3 = sb("x3", [128, 16384], BF16)
    hst = sb("hst", [128, 16384], BF16)
    x3T = sb("x3T", [128, 16384], BF16)
    gwwin = sb("gwwin", [128, 4096], BF16)
    w2win = sb("w2win", [128, 4096], BF16)
    opbc = sb("opbc", [128, 8], F32)
    wrs = sb("wrs", [128, 64], BF16)
    lt = sb("lt", [128, 128], F32)
    mew = sb("mew", [128, 128], F32)
    cols = sb("cols", [128, 64], F32)
    rowf = sb("rowf", [1, 4096], F32)
    onesc = sb("onesc", [128, 2], BF16)
    onesrB = sb("onesrB", [1, 128], F32)
    identt = sb("identt", [128, 128], BF16)
    sqb = sb("sqb", [128, 512], BF16)
    outst = sb("outst", [128, 1024], F32)
    psA = pst("psA", [128, 2048])
    psB = pst("psB", [128, 2048])

    def qq_(eng):
        return lambda fn: Q[eng].append(fn)

    def phase_barrier():
        for eng in ("tensor", "scalar", "vector"):
            Q[eng].append(lambda e, a=C.ev, b=C.ac, c=C.ld, d=C.po: (e.wait_ge(s_ev, a), e.wait_ge(s_ac, b),
                                                                     e.wait_ge(s_ld, c), e.wait_ge(s_po, d)))

    opw_r = D["opw"].rearrange("(et p) n -> p et n", et=8, p=128)
    w1_r = D["w1e"].rearrange("(ex ht p) n -> p ex ht n", ex=8, ht=32, p=128)
    w2_r = D["w2e"].rearrange("(ex ht p) e -> p ex ht e", ex=8, ht=32, p=128)
    gw_last = [0, 0]
    w2_last = [0, 0]

    def load(dst, src):
        C.ld += 16
        qq("sync", lambda e, d=dst, s=src: e.dma_start(out=d, in_=s).then_inc(s_ld, 16))

    for dt in range(8):
        load(x3[:, dt * 2048:(dt + 1) * 2048], D["x2d"][dt * 128:(dt + 1) * 128, :])
    for nm, t in [("opbc", opbc), ("wrs", wrs), ("identd", identt)]:
        load(t[:, :], D[nm][:, :])
    ld0 = C.ld
    C.ev += 1
    def init_ones(e, tgt=ld0):
        e.wait_ge(s_ld, tgt)
        e.memset(onesc[:, :], 1.0)
        e.memset(onesrB[:, :], 1.0)
        e.memset(cols[:, :], 0.0)
        e.memset(cols[:, 60:61], 1e-5).then_inc(s_ev, 1)
    qq("vector", init_ones)
    qq("tensor", lambda e, v=C.ev: e.wait_ge(s_ev, v))
    qq("tensor", lambda e, v=C.ld: e.wait_ge(s_ld, v))

    for et in range(8):
        C.w += 16
        def lw(e, et=et, wp=gw_last[et % 2]):
            e.wait_ge(s_pe, wp)
            e.dma_start(out=gwwin[:, (et % 2) * 1024:(et % 2) * 1024 + 1024],
                        in_=opw_r[:, et, :]).then_inc(s_w, 16)
        qq("gpsimd", lw)
        wq = C.w
        for ch in range(4):
            ps = psA if (4 * et + ch) % 2 == 0 else psB
            def mm_op(e, et=et, ch=ch, ps=ps, wq=wq, etgt=C.ev, first=(et == 0 and ch == 0)):
                e.wait_ge(s_w, wq)
                if not first:
                    e.wait_ge(s_ev, etgt - 1)
                for dt in range(8):
                    m = e.matmul(ps[:, 0:512], gwwin[:, (et % 2) * 1024 + dt * 128:(et % 2) * 1024 + dt * 128 + 128],
                                 vno[:, dt * 2048 + ch * 512:dt * 2048 + ch * 512 + 512],
                                 start=(dt == 0), stop=(dt == 7))
                m.then_inc(s_pe, 1)
            qq("tensor", mm_op)
            C.pe += 1
            C.ev += 1
            def ev_op(e, et=et, ch=ch, ps=ps, tgt=C.pe):
                e.wait_ge(s_pe, tgt)
                sl = slice(et * 2048 + ch * 512, et * 2048 + ch * 512 + 512)
                e.scalar_tensor_tensor(out=x3[:, sl], in0=ps[:, 0:512], scalar=opbc[:, et:et + 1], in1=x3[:, sl],
                                       op0=ALU.add, op1=ALU.add).then_inc(s_ev, 1)
            qq("vector", ev_op)
        gw_last[et % 2] = C.pe

    phase_barrier()
    def mm_sum(e, etgt=C.ev):
        e.wait_ge(s_ev, etgt)
        for ch in range(4):
            for t in range(8):
                m = e.matmul(psA[0:1, ch * 512:ch * 512 + 512], onesc[:, 0:1],
                             x3[:, t * 2048 + ch * 512:t * 2048 + ch * 512 + 512], start=(t == 0), stop=(t == 7))
        m.then_inc(s_pe, 1)
    qq("tensor", mm_sum)
    C.pe += 1
    C.ev += 1
    def ev_sum(e, tgt=C.pe):
        e.wait_ge(s_pe, tgt)
        e.tensor_scalar_mul(rowf[0:1, 0:2048], psA[0:1, 0:2048], -1.0 / 1024.0)
        e.tensor_scalar_add(cols[:, 62:63], cols[:, 62:63], 0.0).then_inc(s_ev, 1)
    qq("vector", ev_sum)
    sums_ev = C.ev
    for ch in range(4):
        for t in range(8):
            C.ac += 1
            def sq(e, ch=ch, t=t, ptgt=C.pe):
                if not (ch == 0 and t == 0):
                    e.wait_ge(s_pe, ptgt)
                e.activation(out=sqb[:, 0:512], in_=x3[:, t * 2048 + ch * 512:t * 2048 + ch * 512 + 512],
                             func=AF.Square, bias=0.0, scale=1.0).then_inc(s_ac, 1)
            qq("scalar", sq)
            def mm_sq(e, ch=ch, t=t, atgt=C.ac, etgt=sums_ev):
                e.wait_ge(s_ac, atgt)
                if ch == 0 and t == 0:
                    e.wait_ge(s_ev, etgt)
                m = e.matmul(psA[0:1, ch * 512:ch * 512 + 512], onesc[:, 0:1], sqb[:, 0:512],
                             start=(t == 0), stop=(t == 7))
                m.then_inc(s_pe, 1)
            qq("tensor", mm_sq)
            C.pe += 1
    C.ev += 1
    def rowm(e, tgt=C.pe):
        e.wait_ge(s_pe, tgt)
        e.tensor_tensor(rowf[0:1, 2048:4096], rowf[0:1, 0:2048], rowf[0:1, 0:2048], op=ALU.mult)
        e.scalar_tensor_tensor(out=rowf[0:1, 2048:4096], in0=psA[0:1, 0:2048], scalar=1.0 / 1024.0,
                               in1=rowf[0:1, 2048:4096], op0=ALU.mult, op1=ALU.subtract)
        e.tensor_scalar_add(rowf[0:1, 2048:4096], rowf[0:1, 2048:4096], 1e-5).then_inc(s_ev, 1)
    qq("vector", rowm)
    C.ac += 1
    def rows_sqrt(e, tgt=C.ev):
        e.wait_ge(s_ev, tgt)
        e.activation(out=rowf[0:1, 2048:4096], in_=rowf[0:1, 2048:4096], func=AF.Sqrt, bias=0.0, scale=1.0).then_inc(s_ac, 1)
    qq("scalar", rows_sqrt)
    C.ev += 1
    def rows_rec(e, tgt=C.ac):
        e.wait_ge(s_ac, tgt)
        e.reciprocal(rowf[0:1, 2048:4096], rowf[0:1, 2048:4096]).then_inc(s_ev, 1)
    qq("vector", rows_rec)
    rr_ev = C.ev
    for ch in range(4):
        pcol = (ch % 2) * 512
        def mm_bc(e, ch=ch, pcol=pcol, tgt=rr_ev, etgt=C.ev):
            e.wait_ge(s_ev, tgt)
            if ch >= 2:
                e.wait_ge(s_ev, etgt)
            e.matmul(psA[:, pcol:pcol + 512], onesrB[0:1, :], rowf[0:1, ch * 512:ch * 512 + 512], start=True, stop=True)
            e.matmul(psB[:, pcol:pcol + 512], onesrB[0:1, :], rowf[0:1, 2048 + ch * 512:2048 + ch * 512 + 512],
                     start=True, stop=True).then_inc(s_pe, 1)
        qq("tensor", mm_bc)
        C.pe += 1
        for t in range(8):
            C.ev += 1
            def apply(e, t=t, ch=ch, pcol=pcol, tgt=C.pe):
                e.wait_ge(s_pe, tgt)
                sl = slice(t * 2048 + ch * 512, t * 2048 + ch * 512 + 512)
                e.tensor_tensor(x3[:, sl], x3[:, sl], psA[:, pcol:pcol + 512], op=ALU.add)
                e.tensor_tensor(x3[:, sl], x3[:, sl], psB[:, pcol:pcol + 512], op=ALU.mult).then_inc(s_ev, 1)
            qq("vector", apply)
    phase_barrier()

    lg_evb = C.ev
    for tb in range(16):
        def mm_lg(e, tb=tb, etgt=C.ev, evb=lg_evb):
            if tb == 0:
                e.wait_ge(s_ev, etgt)
            if tb >= 2:
                e.wait_ge(s_ev, evb + tb - 1)
            for dt in range(8):
                m = e.matmul(psB[:, 1024 + (tb % 2) * 512:1024 + (tb % 2) * 512 + 8],
                             x3[:, dt * 2048 + tb * 128:dt * 2048 + tb * 128 + 128],
                             wrs[:, dt * 8:dt * 8 + 8], start=(dt == 0), stop=(dt == 7))
            m.then_inc(s_pe, 1)
        qq("tensor", mm_lg)
        C.pe += 1
        C.ev += 1
        def ev_lg(e, tb=tb, tgt=C.pe):
            e.wait_ge(s_pe, tgt)
            e.tensor_copy(lt[:, tb * 8:tb * 8 + 8],
                          psB[:, 1024 + (tb % 2) * 512:1024 + (tb % 2) * 512 + 8]).then_inc(s_ev, 1)
        qq("vector", ev_lg)
    lg_ev = C.ev
    C.ev += 1
    def mkmaskA(e, tgt=lg_ev):
        e.wait_ge(s_ev, tgt)
        ltv = lt.rearrange("p (tb e) -> p tb e", tb=16, e=8)
        e.tensor_reduce(cols[:, 0:16].rearrange("p (a b) -> p a b", a=16, b=1), ltv, axis=mybir.AxisListType.X, op=ALU.max)
        e.tensor_scalar_add(cols[:, 62:63], cols[:, 62:63], 0.0).then_inc(s_ev, 1)
    qq("vector", mkmaskA)
    C.ac += 1
    def echoA(e, tgt=C.ev):
        e.wait_ge(s_ev, tgt)
        e.activation(out=cols[:, 61:62], in_=cols[:, 61:62], func=AF.Copy, bias=0.0, scale=0.0).then_inc(s_ac, 1)
    qq("scalar", echoA)
    C.ev += 1
    def mkmaskB(e, tgt=C.ac):
        e.wait_ge(s_ac, tgt)
        ltv = lt.rearrange("p (tb e) -> p tb e", tb=16, e=8)
        for tb in range(16):
            e.tensor_tensor(mew[:, tb * 8:tb * 8 + 8], lt[:, tb * 8:tb * 8 + 8],
                            cols[:, tb:tb + 1].to_broadcast([128, 8]), op=ALU.is_equal)
        for tb in range(16):
            e.scalar_tensor_tensor(out=lt[:, tb * 8:tb * 8 + 8], in0=mew[:, tb * 8:tb * 8 + 8], scalar=-1e30,
                                   in1=lt[:, tb * 8:tb * 8 + 8], op0=ALU.mult, op1=ALU.add)
        e.tensor_reduce(cols[:, 16:32].rearrange("p (a b) -> p a b", a=16, b=1), ltv, axis=mybir.AxisListType.X,
                        op=ALU.max, negate=True)
        e.tensor_scalar_add(cols[:, 62:63], cols[:, 62:63], 0.0).then_inc(s_ev, 1)
    qq("vector", mkmaskB)
    C.ac += 1
    def echoB(e, tgt=C.ev):
        e.wait_ge(s_ev, tgt)
        e.activation(out=cols[:, 61:62], in_=cols[:, 61:62], func=AF.Copy, bias=0.0, scale=0.0).then_inc(s_ac, 1)
    qq("scalar", echoB)
    C.ev += 1
    def mkmaskC(e, tgt=C.ac):
        e.wait_ge(s_ac, tgt)
        e.tensor_tensor(cols[:, 16:32], cols[:, 0:16], cols[:, 16:32], op=ALU.add)
        e.tensor_scalar_add(cols[:, 62:63], cols[:, 62:63], 0.0).then_inc(s_ev, 1)
    qq("vector", mkmaskC)
    C.ac += 1
    def mkw3(e, tgt=C.ev):
        e.wait_ge(s_ev, tgt)
        e.activation(out=cols[:, 32:48], in_=cols[:, 16:32], func=AF.Sigmoid, bias=0.0, scale=1.0)
        e.activation(out=cols[:, 61:62], in_=cols[:, 61:62], func=AF.Copy, bias=0.0, scale=0.0).then_inc(s_ac, 1)
    qq("scalar", mkw3)
    C.ev += 1
    def mkmew(e, tgt=C.ac):
        e.wait_ge(s_ac, tgt)
        for tb in range(16):
            e.tensor_scalar(mew[:, tb * 8:tb * 8 + 8], mew[:, tb * 8:tb * 8 + 8], cols[:, 32 + tb:33 + tb], None,
                            op0=ALU.mult)
        e.tensor_scalar_add(cols[:, 62:63], cols[:, 62:63], 0.0).then_inc(s_ev, 1)
    qq("vector", mkmew)
    phase_barrier()

    w1_r2 = D["w1e"].rearrange("(ex h2 a p) n -> p ex h2 a n", ex=8, h2=16, a=2, p=128)
    w2_r2 = D["w2e"].rearrange("(ex h2 a p) e -> p ex h2 a e", ex=8, h2=16, a=2, p=128)
    gwp_last = [max(gw_last), 0]
    w2p_last = [0, 0]
    prev_evb = None
    for ex in range(8):
        for tc in range(4):
            gel_acb = C.ac
            for ht in range(32):
                if ht % 2 == 0:
                    j = ht // 2
                    C.ld += 16
                    def lw1(e, ex=ex, j=j, wp=gwp_last[j % 2]):
                        e.wait_ge(s_pe, wp)
                        e.dma_start(out=gwwin[:, (j % 2) * 2048:(j % 2) * 2048 + 2048].rearrange(
                                        "p (a n) -> p a n", a=2, n=1024),
                                    in_=w1_r2[:, ex, j, :, :]).then_inc(s_ld, 16)
                    qq("sync", lw1)
                w1w = C.ld
                ps = psA if ht % 2 == 0 else psB
                def mm_m1(e, ex=ex, tc=tc, ht=ht, ps=ps, wq=w1w, acb=gel_acb, pevb=prev_evb):
                    e.wait_ge(s_ld, wq)
                    if ht >= 2:
                        e.wait_ge(s_ac, acb + ht - 1)
                    if pevb is not None and ht == 0:
                        e.wait_ge(s_ev, pevb + 1)
                    if pevb is not None and ht == 1:
                        e.wait_ge(s_ev, pevb + 3)
                    for dt in range(8):
                        m = e.matmul(ps[:, 0:512], gwwin[:, (ht % 4) * 1024 + dt * 128:(ht % 4) * 1024 + dt * 128 + 128],
                                     x3[:, dt * 2048 + tc * 512:dt * 2048 + tc * 512 + 512],
                                     start=(dt == 0), stop=(dt == 7))
                    m.then_inc(s_pe, 1)
                qq("tensor", mm_m1)
                C.pe += 1
                if ht % 2 == 1:
                    gwp_last[(ht // 2) % 2] = C.pe
                C.ac += 1
                def ev_m1(e, ht=ht, ps=ps, tgt=C.pe):
                    e.wait_ge(s_pe, tgt)
                    e.activation(out=hst[:, ht * 512:ht * 512 + 512], in_=ps[:, 0:512],
                                 func=AF.Gelu, bias=0.0, scale=1.0).then_inc(s_ac, 1)
                qq("scalar", ev_m1)
            m1_ac = C.ac
            for ht in range(32):
                if ht % 2 == 0:
                    j = ht // 2
                    C.w += 16
                    def lw2(e, ex=ex, j=j, wp=w2p_last[j % 2]):
                        e.wait_ge(s_pe, wp)
                        e.dma_start(out=w2win[:, (j % 2) * 2048:(j % 2) * 2048 + 2048].rearrange(
                                        "p (a n) -> p a n", a=2, n=1024),
                                    in_=w2_r2[:, ex, j, :, :]).then_inc(s_w, 16)
                    qq("gpsimd", lw2)
                wq = C.w
                def mm_m2(e, ex=ex, tc=tc, ht=ht, wq=wq, atgt=m1_ac, pevb=prev_evb):
                    e.wait_ge(s_w, wq)
                    if ht == 0:
                        e.wait_ge(s_ac, atgt)
                        if pevb is not None:
                            e.wait_ge(s_ev, pevb + 4)
                    for tb in range(4):
                        psx = psA if tb < 2 else psB
                        col = (tb % 2) * 1024
                        m = e.matmul(psx[:, col:col + 512], hst[:, ht * 512 + tb * 128:ht * 512 + tb * 128 + 128],
                                     w2win[:, (ht % 4) * 1024:(ht % 4) * 1024 + 512], start=(ht == 0), stop=(ht == 31))
                        m = e.matmul(psx[:, col + 512:col + 1024], hst[:, ht * 512 + tb * 128:ht * 512 + tb * 128 + 128],
                                     w2win[:, (ht % 4) * 1024 + 512:(ht % 4) * 1024 + 1024],
                                     start=(ht == 0), stop=(ht == 31))
                    m.then_inc(s_pe, 1)
                qq("tensor", mm_m2)
                C.pe += 1
                if ht % 2 == 1:
                    w2p_last[(ht // 2) % 2] = C.pe
            prev_evb = C.ev
            for tb in range(4):
                tbg = tc * 4 + tb
                psx = psA if tb < 2 else psB
                col = (tb % 2) * 1024
                C.ev += 1
                def ev_m2(e, ex=ex, tbg=tbg, psx=psx, col=col, tgt=C.pe):
                    e.wait_ge(s_pe, tgt)
                    o = slice(tbg * 1024, (tbg + 1) * 1024)
                    if ex == 0:
                        e.tensor_scalar(acc[:, o], psx[:, col:col + 1024], mew[:, tbg * 8 + ex:tbg * 8 + ex + 1], None,
                                        op0=ALU.mult).then_inc(s_ev, 1)
                    else:
                        e.scalar_tensor_tensor(out=acc[:, o], in0=psx[:, col:col + 1024],
                                               scalar=mew[:, tbg * 8 + ex:tbg * 8 + ex + 1], in1=acc[:, o],
                                               op0=ALU.mult, op1=ALU.add).then_inc(s_ev, 1)
                qq("vector", ev_m2)
    qq("tensor", lambda e, v=C.ev: e.wait_ge(s_ev, v))
    phase_barrier()

    tr_evb = C.ev
    for tb in range(16):
        for dt in range(8):
            it = tb * 8 + dt
            slot = it % 4
            def mm_t(e, tb=tb, dt=dt, slot=slot, etgt=C.ev, it=it):
                if it >= 4:
                    e.wait_ge(s_ev, etgt - 3)
                e.matmul(psB[:, slot * 512:slot * 512 + 128], x3[:, dt * 2048 + tb * 128:dt * 2048 + tb * 128 + 128],
                         identt[:, :], start=True, stop=True).then_inc(s_pe, 1)
            qq("tensor", mm_t)
            C.pe += 1
            C.ev += 1
            def ev_t(e, tb=tb, dt=dt, slot=slot, tgt=C.pe):
                e.wait_ge(s_pe, tgt)
                e.tensor_copy(x3T[:, tb * 1024 + dt * 128:tb * 1024 + dt * 128 + 128],
                              psB[:, slot * 512:slot * 512 + 128]).then_inc(s_ev, 1)
            qq("vector", ev_t)
    phase_barrier()

    for tb in range(16):
        C.ev += 1
        def fz(e, tb=tb):
            o = slice(tb * 1024, (tb + 1) * 1024)
            e.tensor_tensor(x3T[:, o], x3T[:, o], acc[:, o], op=ALU.add)
            e.bn_stats(cols[:, 16:22], x3T[:, o.start:o.start + 512])
            e.bn_stats(cols[:, 22:28], x3T[:, o.start + 512:o.start + 1024])
            e.tensor_scalar_add(cols[:, 62:63], cols[:, 62:63], 0.0).then_inc(s_ev, 1)
        qq("vector", fz)
        C.ac += 1
        def fe(e, tgt=C.ev):
            e.wait_ge(s_ev, tgt)
            e.activation(out=cols[:, 61:62], in_=cols[:, 61:62], func=AF.Copy, bias=0.0, scale=0.0).then_inc(s_ac, 1)
        qq("scalar", fe)
        C.ev += 1
        def fg(e, tgt=C.ac):
            e.wait_ge(s_ac, tgt)
            e.bn_aggr(cols[:, 8:10], cols[:, 16:28])
            e.tensor_scalar_add(cols[:, 62:63], cols[:, 62:63], 0.0).then_inc(s_ev, 1)
        qq("vector", fg)
        C.ac += 1
        def fh(e, tgt=C.ev):
            e.wait_ge(s_ev, tgt)
            e.activation(out=cols[:, 4:5], in_=cols[:, 9:10], func=AF.Sqrt, bias=cols[:, 60:61], scale=1.0)
            e.activation(out=cols[:, 3:4], in_=cols[:, 8:9], func=AF.Copy, bias=0.0, scale=-1.0)
            e.activation(out=cols[:, 61:62], in_=cols[:, 61:62], func=AF.Copy, bias=0.0, scale=0.0).then_inc(s_ac, 1)
        qq("scalar", fh)
        C.ev += 1
        def fapply(e, tb=tb, tgt=C.ac, ptgt=C.po):
            e.wait_ge(s_ac, tgt)
            e.wait_ge(s_po, ptgt)
            e.reciprocal(cols[:, 5:6], cols[:, 4:5])
            o = slice(tb * 1024, (tb + 1) * 1024)
            e.tensor_scalar(outst[:, 0:1024], x3T[:, o], cols[:, 3:4], None, op0=ALU.add)
            e.tensor_scalar(outst[:, 0:1024], outst[:, 0:1024], cols[:, 5:6], None, op0=ALU.mult).then_inc(s_ev, 1)
        qq("vector", fapply)
        C.po += 1
        C.out += 16
        def dout(e, tb=tb, etgt=C.ev, otgt=C.out):
            e.wait_ge(s_ev, etgt)
            e.dma_start(out=D["outT"][tb * 128:(tb + 1) * 128, :], in_=outst[:, 0:1024]).then_inc(s_out, 16)
            e.wait_ge(s_out, otgt)
            e.sem_inc(s_po, 1)
        qq("gpsimd", dout)
    for t in reversed(ctx):
        t.__exit__(None, None, None)


def build_fused():
    nc = bass.Bass("TRN2", target_bir_lowering=False)
    D = {}
    shapes = dict(P1_IN)
    shapes["maskT"] = ([384, 384], BF16)
    for name, (shape, dt) in shapes.items():
        D[name] = nc.dram_tensor(name, shape, dt, kind="ExternalInput").ap()
    for name, shape, dt in [("cwc0", [128, 24], F32), ("cbc0", [128, 8], F32), ("gbc20", [128, 8], F32),
                            ("gw0", [1024, 1024], BF16),
                            ("cwc1", [128, 24], F32), ("cbc1", [128, 8], F32), ("gbc21", [128, 8], F32),
                            ("gw1", [1024, 1024], BF16),
                            ("hm", [128, 4], F32), ("SsT", [128, 2048], BF16),
                            ("opw", [1024, 1024], BF16), ("opbc", [128, 8], F32),
                            ("wrs", [128, 64], BF16), ("w1e", [32768, 1024], BF16), ("w2e", [32768, 1024], BF16)]:
        D[name] = nc.dram_tensor(name, shape, dt, kind="ExternalInput").ap()
    for k, shp in CONV_TAB_SHAPES.items():
        if k != "Ss":
            D["tab_" + k] = nc.dram_tensor("tab_" + k, shp, BF16, kind="ExternalInput").ap()
    D["outT"] = nc.dram_tensor("outT", [2048, 1024], F32, kind="ExternalOutput").ap()
    intern = {}
    def dint(name, shape, dt=BF16):
        t = nc.dram_tensor(name, shape, dt)
        intern[name] = t
        D[name] = t.ap()
        return D[name]
    dint("x2d", [1024, 2048])
    dint("v0d", [1024, 2048])
    dint("c0d", [1024, 2048])
    dint("c1d", [1024, 2048])
    dint("h0d", [512, 4096])
    dint("h1d", [512, 4096])
    dint("edges", [1024, 4])
    dint("v0g", [2048, 2048])
    dint("v1d", [1024, 2048])
    dint("v1g", [2048, 2048])
    dint("hg0", [1024, 4096])
    dint("hg1", [1024, 4096])
    dint("edgeg", [2048, 4])
    D["x2own"] = D["x2d"]
    D["v0own"] = D["v0d"]
    D["c0own"] = D["c0d"]
    D["c1own"] = D["c1d"]
    D["h0own"] = D["h0d"]
    D["h1own"] = D["h1d"]

    SEMS = []
    sem_ctx = []
    for nm in ["s_ld", "s_w", "s_pe", "s_ev", "s_ac", "s_po", "s_out", "s_cc"]:
        cm = nc.semaphore(nm)
        sem_ctx.append(cm)
        SEMS.append(cm.__enter__())
    SEM = tuple(SEMS[:7])
    s_cc = SEMS[7]
    (s_ld, s_w, s_pe, s_ev, s_ac, s_po, s_out) = SEM

    class C:
        ld = 0; w = 0; pe = 0; ev = 0; ac = 0; po = 0; out = 0; cc = 0
    Q = {"sync": [], "tensor": [], "vector": [], "scalar": [], "gpsimd": []}

    def full_barrier():
        for eng in ("sync", "tensor", "vector", "scalar", "gpsimd"):
            def bar(e, a=C.ld, b=C.w, c=C.pe, d=C.ev, f=C.ac, g=C.po, h=C.out, i=C.cc):
                e.wait_ge(s_ld, a); e.wait_ge(s_w, b); e.wait_ge(s_pe, c); e.wait_ge(s_ev, d)
                e.wait_ge(s_ac, f); e.wait_ge(s_po, g); e.wait_ge(s_out, h); e.wait_ge(s_cc, i)
            Q[eng].append(bar)

    def collective(in_ap, out_ap):
        C.cc += 1
        def cc(e, i=in_ap, o=out_ap, w_out=C.out, w_po=C.po, prev=C.cc - 1):
            e.wait_ge(s_out, w_out)
            e.wait_ge(s_po, w_po)
            e.wait_ge(s_cc, prev)
            e.collective_compute("AllGather", ALU.bypass,
                                 replica_groups=[[0, 1], [2, 3], [4, 5], [6, 7]],
                                 ins=[i.opt()], outs=[o.opt()]).then_inc(s_cc)
        Q["gpsimd"].append(cc)

    def mid_cb():
        collective(intern["v0d"].ap(), intern["v0g"].ap())
        collective(intern["edges"].ap(), intern["edgeg"].ap())
    def h_cb():
        collective(intern["h0d"].ap(), intern["hg0"].ap())
    emit_p1(nc, D, Q, C, SEM, mid_cb, h_cb)
    full_barrier()

    pctx = []
    def pent(t):
        pctx.append(t); return t.__enter__()
    yown = pent(nc.sbuf_tensor("yown", [128, 16384], BF16))
    tabs_ctx = []
    tabs = {}
    for k, shp in CONV_TAB_SHAPES.items():
        if k == "Ss":
            continue
        cm = nc.sbuf_tensor("tb_" + k, shp, BF16)
        tabs_ctx.append(cm)
        tabs[k] = cm.__enter__()
    cmS = nc.sbuf_tensor("tb_SsT", [128, 2048], BF16)
    tabs_ctx.append(cmS)
    tabs["SsT"] = cmS.__enter__()
    for k in tabs:
        src = D["SsT"] if k == "SsT" else D["tab_" + k]
        C.ld += 16
        Q["sync"].append(lambda e, d=tabs[k], s=src: e.dma_start(out=d[:, :], in_=s[:, :]).then_inc(s_ld, 16))
    collective(intern["h1d"].ap(), intern["hg1"].ap())

    emit_conv(nc, Q, C, SEM, tabs, D["v0g"], D["hg0"], yown, 0, ccsem=s_cc, vcc=1, hcc=3)
    full_barrier()
    emit_gate(nc, D, Q, C, SEM, yown, None, 0, 0)
    full_barrier()
    collective(intern["v1d"].ap(), intern["v1g"].ap())

    emit_conv(nc, Q, C, SEM, tabs, D["v1g"], D["hg1"], yown, 1, ccsem=s_cc, vcc=5, hcc=4, first_tag="h")
    full_barrier()
    for cm in reversed(tabs_ctx):
        cm.__exit__(None, None, None)

    vno = pent(nc.sbuf_tensor("vno_f", [128, 16384], BF16))
    emit_gate(nc, D, Q, C, SEM, yown, vno, 1, 1)
    full_barrier()
    emit_p4rest(nc, D, Q, C, SEM, vno, yown)
    Q["gpsimd"].append(lambda e, v=C.out: e.wait_ge(s_out, v))

    with nc.allow_low_precision(reason="bf16 row stats within validated tolerance"):
        with nc.Block() as block:
            @block.sync
            def _(e):
                for f in Q["sync"]:
                    f(e)
            @block.tensor
            def _(e):
                for f in Q["tensor"]:
                    f(e)
            @block.vector
            def _(e):
                for f in Q["vector"]:
                    f(e)
            @block.scalar
            def _(e):
                for f in Q["scalar"]:
                    f(e)
            @block.gpsimd
            def _(e):
                for f in Q["gpsimd"]:
                    f(e)
    for cm in reversed(pctx):
        cm.__exit__(None, None, None)
    for cm in reversed(sem_ctx):
        cm.__exit__(None, None, None)
    return nc, D



def _blk(w, nb):
    K, N = w.shape
    d = K // 128
    return np.ascontiguousarray(w.reshape(d, 128, nb, 128).transpose(2, 1, 0, 3).reshape(nb * 128, K))


def _p1_host_inputs(inp):
    f32 = np.float32
    x0 = np.ascontiguousarray(inp['L0_attn_x'], f32)
    gamma = inp['L0_attn_gamma'].astype(f32); beta = inp['L0_attn_beta'].astype(f32)
    gbc = np.zeros((128, 16), f32)
    gbc[:, 0::2] = gamma.reshape(8, 128).T
    gbc[:, 1::2] = beta.reshape(8, 128).T
    wqkvT = np.ascontiguousarray(inp['L0_attn_W_qkv'].astype(f32).T)
    woutT = np.ascontiguousarray(inp['L0_attn_W_out'].astype(f32).T)
    shared = dict(
        gbc=gbc,
        wqkvT=wqkvT.astype(nbf),
        wqkvb=_blk(wqkvT[:, 0:2048], 16).astype(nbf),
        woutb=_blk(woutT, 8).astype(nbf),
        w1b0=_blk(inp['L0_ffn_w1'].astype(f32), 32).astype(nbf),
        b1c=np.ascontiguousarray(inp['L0_ffn_b1'].astype(f32).reshape(32, 128).T),
        w2b0=_blk(inp['L0_ffn_w2'].astype(f32), 8).astype(nbf),
        b2c=np.ascontiguousarray(inp['L0_ffn_b2'].astype(f32).reshape(8, 128).T),
        inpb=_blk(inp['L1_attn_in_proj_w'].astype(f32), 24).astype(nbf),
        inpbc=np.ascontiguousarray(inp['L1_attn_in_proj_b'].astype(f32).reshape(24, 128).T),
        identd=np.eye(128, dtype=f32).astype(nbf),
        fbias=np.array([0, 0, 0, 0, np.pi / 2, np.pi / 2, np.pi / 2], f32).reshape(7, 1),
        titc=np.ascontiguousarray(np.broadcast_to(np.arange(4096, dtype=f32), (128, 4096))),
    )
    freqs = inp['L1_attn_freqs'].astype(f32)
    frcol = np.zeros((7, 2), f32)
    for i in range(2):
        frcol[1:4, i] = 2 * np.pi * freqs[i] / 4095.0
        frcol[4:7, i] = 2 * np.pi * freqs[i] / 4095.0
    shared['frcol'] = frcol
    perm = [6, 0, 1, 2, 3, 4, 5]
    mw1p = np.zeros((7, 128), f32); mw2s = np.zeros((64, 128), f32)
    mb1c = np.zeros((64, 2), f32); mb2c = np.zeros((64, 2), f32)
    for i in range(2):
        mw1p[:, i * 64:(i + 1) * 64] = inp['L1_attn_mlp_w1'].astype(f32)[i][perm, :]
        mw2s[:, i * 64:(i + 1) * 64] = inp['L1_attn_mlp_w2'].astype(f32)[i]
        mb1c[:, i] = inp['L1_attn_mlp_b1'].astype(f32)[i]
        mb2c[:, i] = inp['L1_attn_mlp_b2'].astype(f32)[i]
    shared.update(mw1p=mw1p.astype(nbf), mw2s=mw2s.astype(nbf), mb1c=mb1c, mb2c=mb2c)
    masks = []
    for half in range(2):
        m = np.full((384, 384), -1e30, f32)
        own_start = half * 2048
        for mi in range(3):
            for r in range(128):
                for j0 in range(r, min(384, r + 257)):
                    if mi == 2 or (own_start + mi * 128 + j0 - 256) >= 0:
                        m[mi * 128 + r, j0] = 0.0
        masks.append(m.astype(nbf))
    w3 = inp['L1_attn_mlp_w3'].astype(f32)
    b3 = inp['L1_attn_mlp_b3'].astype(f32)
    dec = inp['L1_attn_decay'].astype(f32)
    per_core = []
    for core in range(8):
        b = core // 2; half = core % 2; os_ = half * 2048
        xin = np.zeros((1024, 2304), f32)
        lo = max(0, os_ - 256)
        xin[:, 256 - (os_ - lo):2304] = x0[b].T[:, lo:os_ + 2048]
        ch0 = half * 512
        mw3s = np.zeros((64, 1024), f32)
        mb3c = np.zeros((128, 8), f32)
        decc = np.zeros((128, 8), f32)
        for i in range(2):
            mw3s[:, i * 512:(i + 1) * 512] = w3[i][:, ch0:ch0 + 512]
            for mt in range(4):
                mb3c[:, i * 4 + mt] = b3[i][ch0 + mt * 128:ch0 + (mt + 1) * 128]
                decc[:, i * 4 + mt] = dec[i][ch0 + mt * 128:ch0 + (mt + 1) * 128]
        m = dict(shared)
        m.update(xin=np.ascontiguousarray(xin).astype(nbf), maskT=masks[half],
                 mw3s=mw3s.astype(nbf), mb3c=mb3c, decc=decc)
        per_core.append(m)
    return per_core


def _fused_inputs(inp):
    f32 = np.float32
    per_core = _p1_host_inputs(inp)
    tabs = conv_tables()
    Ss_full = tabs.pop("Ss")
    base = {("tab_" + k): v for k, v in tabs.items()}
    for order in range(2):
        cw = inp['L1_attn_conv_w'].astype(f32)[order][:, 0, :]
        cb = inp['L1_attn_conv_b'].astype(f32)[order]
        gb = inp['L1_attn_gate_b'].astype(f32)[order]
        cwc = np.zeros((128, 24), f32)
        for k in range(3):
            cwc[:, k * 8:(k + 1) * 8] = cw[k].reshape(8, 128).T
        base["cwc%d" % order] = cwc
        base["cbc%d" % order] = np.ascontiguousarray(cb.reshape(8, 128).T)
        base["gbc2%d" % order] = np.ascontiguousarray(gb.reshape(8, 128).T)
        base["gw%d" % order] = _blk(inp['L1_attn_gate_w'].astype(f32)[order], 8).astype(nbf)
    base["opw"] = _blk(inp['L1_attn_out_proj_w'].astype(f32), 8).astype(nbf)
    base["opbc"] = np.ascontiguousarray(inp['L1_attn_out_proj_b'].astype(f32).reshape(8, 128).T)
    Wr = inp['L1_ffn_Wr'].astype(f32)
    base["wrs"] = np.ascontiguousarray(Wr.reshape(8, 128, 8).transpose(1, 0, 2).reshape(128, 64)).astype(nbf)
    base["w1e"] = np.ascontiguousarray(
        inp['L1_ffn_W1'].astype(f32).reshape(8, 8, 128, 32, 128).transpose(0, 3, 2, 1, 4).reshape(32768, 1024)
    ).astype(nbf)
    base["w2e"] = np.ascontiguousarray(inp['L1_ffn_W2'].astype(f32).reshape(32768, 1024)).astype(nbf)
    Ssv = np.asarray(Ss_full.astype(f32)).reshape(128, 128, 32)
    for core in range(8):
        half = core % 2
        SsT = np.ascontiguousarray(Ssv[:, :, half * 16:(half + 1) * 16].reshape(128, 2048)).astype(nbf)
        hm = np.zeros((128, 4), f32)
        if half == 1:
            hm[:, 0] = 1.0
        else:
            hm[:, 3] = 1.0
        per_core[core].update(base)
        per_core[core]["SsT"] = SsT
        per_core[core]["hm"] = hm
    return per_core


_CACHE = {}


def kernel(**inputs):
    if "nc" not in _CACHE:
        _CACHE["nc"] = build_fused()
    nc, D = _CACHE["nc"]
    per_core = _fused_inputs(inputs)
    res = run_bass_kernel_spmd(nc, per_core, core_ids=list(range(8))).results
    out = np.zeros((4, 4096, 1024), np.float32)
    for core in range(8):
        b = core // 2; half = core % 2; os_ = half * 2048
        out[b, os_:os_ + 2048, :] = res[core]["outT"]
    return out
